# version 06
# baseline (speedup 1.0000x reference)
import math
import os
from contextlib import ExitStack

import numpy as np
import concourse.bass as bass
import concourse.mybir as mybir
from concourse.bass_utils import run_bass_kernel_spmd

F32 = mybir.dt.float32
BF16 = mybir.dt.bfloat16
AF = mybir.ActivationFunctionType
ALU = mybir.AluOpType
AX = mybir.AxisListType

COMPUTE_ENGINES = ("tensor", "vector", "scalar", "gpsimd")
DMA_QUEUES = ("sync", "scalar_dma", "gpsimd_dma")


class Buf:

    __slots__ = ("name", "ap", "last_writer", "readers", "space")

    def __init__(self, name, ap, space):
        self.name = name
        self.ap = ap
        self.space = space
        self.last_writer = None
        self.readers = []

    def __getitem__(self, idx):
        return self.ap[idx]


class Task:
    __slots__ = ("queue", "fn", "reads", "writes", "n_dma", "waits", "tag")

    def __init__(self, queue, fn, reads, writes, n_dma, tag):
        self.queue = queue
        self.fn = fn
        self.reads = reads
        self.writes = writes
        self.n_dma = n_dma
        self.waits = {}
        self.tag = tag


class Prog:

    def __init__(self, nc):
        self.nc = nc
        self.tasks = []
        self.counts = {q: 0 for q in COMPUTE_ENGINES + DMA_QUEUES}
        self.stack = ExitStack()
        self._uid = 0
        self.out_store_bufs = []
        self._const_pool = None
        self._const_used = 0
        self._const_vals = {}
        self._sb_bytes = 0
        self._names = set()

    def uname(self, base):
        self._uid += 1
        return f"{base}_{self._uid}"

    def sb(self, name, shape, dtype=F32):
        name = self.uname(name)
        h = self.nc.alloc_sbuf_tensor(name, shape, dtype)
        self._sb_bytes += shape[1] * mybir.dt.to_np(dtype)().itemsize if False else 0
        return Buf(name, h.ap(), "SBUF")

    def ps(self, name, shape, dtype=F32):
        name = self.uname(name)
        h = self.nc.alloc_psum_tensor(name, shape, dtype)
        return Buf(name, h.ap(), "PSUM")

    def dram(self, name, shape, dtype=F32, kind="Internal"):
        h = self.nc.dram_tensor(name, shape, dtype, kind=kind)
        return Buf(name, h.ap(), "DRAM")

    def const(self, value):
        if self._const_pool is None:
            self._const_pool = self.nc.alloc_sbuf_tensor("constpool", [128, 16], F32)
        if value in self._const_vals:
            return
        i = self._const_used
        assert i < 16
        self._const_used += 1
        ap = self._const_pool.ap()[:, i:i + 1]
        self._const_vals[value] = ap
        self.nc.const_aps.register(F32, float(value), ap)
        cp = Buf(f"const_{i}", ap, "SBUF")

        def fn(v):
            return v.memset(ap, float(value))
        self.task("vector", fn, reads=[], writes=[cp], tag="const")

    def task(self, queue, fn, reads=(), writes=(), n_dma=0, tag=""):
        t = Task(queue, fn, list(reads), list(writes), n_dma, tag)
        if queue in DMA_QUEUES:
            assert n_dma > 0
            self.counts[queue] += 16 * n_dma
        else:
            assert n_dma == 0
            self.counts[queue] += 1
        cnt = self.counts[queue]
        waits = {}

        def need(q, c):
            if q == queue == "tensor":
                return
            waits[q] = max(waits.get(q, 0), c)

        if queue in DMA_QUEUES and cnt > 16 * n_dma:
            need(queue, cnt - 16 * n_dma)
        for b in t.reads:
            if b.last_writer is not None:
                q, c = b.last_writer
                need(q, c)
        for b in t.writes:
            if b.last_writer is not None:
                q, c = b.last_writer
                need(q, c)
            for (q, c) in b.readers:
                need(q, c)
        t.waits = waits
        for b in t.reads:
            b.readers.append((queue, cnt))
        for b in t.writes:
            b.last_writer = (queue, cnt)
            b.readers = []
        self.tasks.append(t)
        return t

    def dma(self, queue, dst, src, reads=(), writes=(), tag="dma", **kw):
        def fn(eng, _dst=dst, _src=src):
            return eng.dma_start(out=_dst, in_=_src, **kw)
        return self.task(queue, fn, reads=reads, writes=writes, n_dma=1, tag=tag)

    def finalize(self):
        nc = self.nc
        sems = {}
        for q in COMPUTE_ENGINES + DMA_QUEUES:
            if self.counts[q] > 0:
                sems[q] = self.stack.enter_context(nc.semaphore(f"sem_{q}"))
        block = self.stack.enter_context(nc.Block())
        per_queue = {q: [] for q in sems}
        for t in self.tasks:
            per_queue[t.queue].append(t)
        for q, tasks in per_queue.items():
            seen = {}
            for t in tasks:
                pruned = {}
                for oq, val in t.waits.items():
                    if oq not in sems:
                        continue
                    if seen.get(oq, 0) >= val:
                        continue
                    seen[oq] = val
                    pruned[oq] = val
                t.waits = pruned

        engine_of = {
            "tensor": "tensor", "vector": "vector", "scalar": "scalar",
            "gpsimd": "gpsimd", "sync": "sync", "scalar_dma": "scalar",
            "gpsimd_dma": "gpsimd",
        }
        phys = {}
        for t in self.tasks:
            phys.setdefault(engine_of[t.queue], []).append(t)

        prog_self = self

        def make_section(eng_name, tasks):
            def section(eng):
                done = {q: 0 for q in sems}
                for t in tasks:
                    for oq, val in t.waits.items():
                        eng.wait_ge(sems[oq], val)
                    inst = t.fn(eng)
                    if t.queue in DMA_QUEUES:
                        assert inst is not None, f"dma task {t.tag} returned None"
                        inst.then_inc(sems[t.queue], 16 * t.n_dma)
                    else:
                        assert inst is not None, (
                            f"compute task {t.tag} must return last instruction")
                        inst.then_inc(sems[t.queue], 1)
                if eng_name == "sync" and prog_self._final_waits:
                    for q, val in prog_self._final_waits.items():
                        if q in sems:
                            eng.wait_ge(sems[q], val)
            return section

        self._final_waits = {q: self.counts[q] for q in DMA_QUEUES if self.counts.get(q, 0) > 0}
        if "sync" not in phys:
            phys["sync"] = []

        for eng_name, tasks in phys.items():
            getattr(block, {"scalar_dma": "scalar", "gpsimd_dma": "gpsimd"}.get(eng_name, eng_name))(
                make_section(eng_name, tasks))
        self.stack.close()



def ceil_div(a, b):
    return (a + b - 1) // b


def t_copy(p, queue, dst, dst_idx, src, src_idx, reads_extra=(), tag="copy"):
    def fn(eng):
        return eng.tensor_copy(dst.ap[dst_idx] if dst_idx else dst.ap[:],
                               src.ap[src_idx] if src_idx else src.ap[:])
    return p.task(queue, fn, reads=[src, *reads_extra], writes=[dst], tag=tag)


def load_cast_bf16(p, dst_sb, dram_ap, reads=(), tag="wload"):
    def fn(eng):
        return eng.dma_start(out=dst_sb.ap[:], in_=dram_ap)
    return p.task("gpsimd_dma", fn, reads=list(reads), writes=[dst_sb], n_dma=1,
                  tag=tag)



D_MODEL = 1024
B = 2
S = 8192
N_CORES = 8
TOK_PER_CORE = 2048
BLK = 1024


def core_blocks(c):
    j = c % 4
    return (j, 7 - j)


def core_token_slices(c):
    j, k = core_blocks(c)
    return [(j * BLK, (j + 1) * BLK), (k * BLK, (k + 1) * BLK)]


def kernel(**inputs):
    raise NotImplementedError("under construction")



LN_EPS_DEFAULT = 1e-5


class PsumPool:
    def __init__(self, p, n=8, cols=512):
        self.bufs = [p.ps("pp", [128, cols], F32) for _ in range(n)]
        self.i = 0

    def nxt(self):
        b = self.bufs[self.i % len(self.bufs)]
        self.i += 1
        return b


def make_ones_col(p):
    b = p.sb("onescol", [1, 128], BF16)
    p.task("vector", lambda v: v.memset(b.ap[:], 1.0), writes=[b], tag="ones")
    return b


def make_ones_lhs(p):
    b = p.sb("oneslhs", [128, 1], BF16)
    p.task("vector", lambda v: v.memset(b.ap[:], 1.0), writes=[b], tag="ones")
    return b


def emit_bcast_row(p, pool, ones_col, row_ap, row_buf, out_buf, out_col0, width,
                   extra_reads=()):
    psb = pool.nxt()

    def mm(te):
        return te.matmul(psb.ap[:, 0:width], ones_col.ap[0:1, :], row_ap,
                         start=True, stop=True)
    p.task("tensor", mm, reads=[ones_col, row_buf, *extra_reads], writes=[psb],
           tag="bcast_mm")

    def ev(v):
        return v.tensor_copy(out_buf.ap[:, out_col0:out_col0 + width],
                             psb.ap[:, 0:width])
    p.task("vector", ev, reads=[psb], writes=[out_buf], tag="bcast_ev")


def emit_ln_chunk(p, pool, ones_lhs, ones_col, x_buf, Tc, xn_bf_buf,
                  scr, eps=LN_EPS_DEFAULT, n_f=8, gain=None, out_t0=0, out_T=None):
    D = n_f * 128
    p.const(-0.5)
    x2 = scr["x2"]; xbf = scr["xbf"]
    for fc in range(n_f):
        sl = slice(fc * Tc, (fc + 1) * Tc)
        p.task("scalar",
               lambda sc, sl=sl: sc.activation(x2.ap[:, sl], x_buf.ap[:, sl], AF.Square),
               reads=[x_buf], writes=[x2], tag="ln_sq")
        p.task("vector",
               lambda v, sl=sl: v.tensor_copy(xbf.ap[:, sl], x_buf.ap[:, sl]),
               reads=[x_buf], writes=[xbf], tag="ln_cast")
    n_sc = ceil_div(Tc, 512)
    stat = scr["stat"]
    mrep = scr["mrep"]; rrep = scr["rrep"]
    for sc_i in range(n_sc):
        w = min(512, Tc - sc_i * 512)
        c0 = sc_i * 512
        ps_sum = pool.nxt(); ps_sq = pool.nxt()

        def mm_sum(te, c0=c0, w=w, ps=ps_sum):
            last = None
            for fc in range(n_f):
                last = te.matmul(ps.ap[0:1, 0:w], ones_lhs.ap[:, 0:1],
                                 xbf.ap[:, fc * Tc + c0: fc * Tc + c0 + w],
                                 start=(fc == 0), stop=(fc == n_f - 1))
            return last
        p.task("tensor", mm_sum, reads=[xbf, ones_lhs], writes=[ps_sum], tag="ln_mm1")

        def mm_sq(te, c0=c0, w=w, ps=ps_sq):
            last = None
            for fc in range(n_f):
                last = te.matmul(ps.ap[0:1, 0:w], ones_lhs.ap[:, 0:1],
                                 x2.ap[:, fc * Tc + c0: fc * Tc + c0 + w],
                                 start=(fc == 0), stop=(fc == n_f - 1))
            return last
        p.task("tensor", mm_sq, reads=[x2, ones_lhs], writes=[ps_sq], tag="ln_mm2")

        ME, EX, VA, RS = 0 * Tc, 1 * Tc, 2 * Tc, 3 * Tc

        def st1(v, ps=ps_sum, c0=c0, w=w):
            return v.tensor_scalar_mul(stat.ap[0:1, ME + c0:ME + c0 + w],
                                       ps.ap[0:1, 0:w], 1.0 / D)
        p.task("vector", st1, reads=[ps_sum], writes=[stat], tag="ln_mean")

        def st2(v, ps=ps_sq, c0=c0, w=w):
            return v.tensor_scalar_mul(stat.ap[0:1, EX + c0:EX + c0 + w],
                                       ps.ap[0:1, 0:w], 1.0 / D)
        p.task("vector", st2, reads=[ps_sq], writes=[stat], tag="ln_ex2")

        def st3b(v, c0=c0, w=w):
            return v.tensor_tensor(stat.ap[0:1, VA + c0:VA + c0 + w],
                                   stat.ap[0:1, ME + c0:ME + c0 + w],
                                   stat.ap[0:1, ME + c0:ME + c0 + w], ALU.mult)
        p.task("vector", st3b, reads=[stat], writes=[stat], tag="ln_msq")

        def st4(v, c0=c0, w=w):
            return v.tensor_sub(stat.ap[0:1, VA + c0:VA + c0 + w],
                                stat.ap[0:1, EX + c0:EX + c0 + w],
                                stat.ap[0:1, VA + c0:VA + c0 + w])
        p.task("vector", st4, reads=[stat], writes=[stat], tag="ln_var")

        def st5(v, c0=c0, w=w, eps=eps):
            return v.tensor_scalar_add(stat.ap[0:1, VA + c0:VA + c0 + w],
                                       stat.ap[0:1, VA + c0:VA + c0 + w], eps)
        p.task("vector", st5, reads=[stat], writes=[stat], tag="ln_vareps")

        def st6(sc, c0=c0, w=w):
            return sc.activation(stat.ap[0:1, RS + c0:RS + c0 + w],
                                 stat.ap[0:1, VA + c0:VA + c0 + w], AF.Ln)
        p.task("scalar", st6, reads=[stat], writes=[stat], tag="ln_ln")

        def st7(sc, c0=c0, w=w):
            return sc.activation(stat.ap[0:1, RS + c0:RS + c0 + w],
                                 stat.ap[0:1, RS + c0:RS + c0 + w], AF.Exp, scale=-0.5)
        p.task("scalar", st7, reads=[stat], writes=[stat], tag="ln_rstd")

        def st8(v, c0=c0, w=w):
            return v.tensor_copy(scr["mean_bf"].ap[0:1, c0:c0 + w],
                                 stat.ap[0:1, ME + c0:ME + c0 + w])
        p.task("vector", st8, reads=[stat], writes=[scr["mean_bf"]], tag="ln_mbf")

        def st9(v, c0=c0, w=w):
            return v.tensor_copy(scr["rstd_bf"].ap[0:1, c0:c0 + w],
                                 stat.ap[0:1, RS + c0:RS + c0 + w])
        p.task("vector", st9, reads=[stat], writes=[scr["rstd_bf"]], tag="ln_rbf")

        emit_bcast_row(p, pool, ones_col, scr["mean_bf"].ap[0:1, c0:c0 + w],
                       scr["mean_bf"], mrep, c0, w)
        emit_bcast_row(p, pool, ones_col, scr["rstd_bf"].ap[0:1, c0:c0 + w],
                       scr["rstd_bf"], rrep, c0, w)

    tmp = scr["tmp"]
    oT = out_T if out_T is not None else Tc
    for fc in range(n_f):
        sl = slice(fc * Tc, (fc + 1) * Tc)
        osl = slice(fc * oT + out_t0, fc * oT + out_t0 + Tc)

        def ap1(v, sl=sl):
            return v.tensor_sub(tmp.ap[:, 0:Tc], x_buf.ap[:, sl], mrep.ap[:, 0:Tc])
        p.task("vector", ap1, reads=[x_buf, mrep], writes=[tmp], tag="ln_sub")

        if gain is None:
            def ap2(v, osl=osl):
                return v.tensor_mul(xn_bf_buf.ap[:, osl], tmp.ap[:, 0:Tc],
                                    rrep.ap[:, 0:Tc])
            p.task("vector", ap2, reads=[tmp, rrep], writes=[xn_bf_buf], tag="ln_mul")
        else:
            gbuf, gcol = gain
            def ap2(v):
                return v.tensor_mul(tmp.ap[:, 0:Tc], tmp.ap[:, 0:Tc], rrep.ap[:, 0:Tc])
            p.task("vector", ap2, reads=[tmp, rrep], writes=[tmp], tag="ln_mul")
            def ap3(v, osl=osl, fc=fc):
                return v.tensor_scalar_mul(xn_bf_buf.ap[:, osl], tmp.ap[:, 0:Tc],
                                           gbuf.ap[:, gcol + fc:gcol + fc + 1])
            p.task("vector", ap3, reads=[tmp, gbuf], writes=[xn_bf_buf], tag="ln_gain")


def ln_scratch(p, Tc, n_f=8):
    return {
        "x2": p.sb("ln_x2", [128, n_f * Tc], BF16),
        "xbf": p.sb("ln_xbf", [128, n_f * Tc], BF16),
        "stat": p.sb("ln_stat", [1, 4 * Tc], F32),
        "mean_bf": p.sb("ln_mbf", [1, Tc], BF16),
        "rstd_bf": p.sb("ln_rbf", [1, Tc], BF16),
        "mrep": p.sb("ln_mrep", [128, Tc], F32),
        "rrep": p.sb("ln_rrep", [128, Tc], F32),
        "tmp": p.sb("ln_tmp", [128, Tc], F32),
    }


def emit_ln_to(p, pool, ones_lhs, ones_col, x_buf, Tc, out_bf, t0, Tbig, scr,
               eps=LN_EPS_DEFAULT, n_f=8):
    emit_ln_chunk(p, pool, ones_lhs, ones_col, x_buf, Tc, out_bf, scr, eps=eps,
                  n_f=n_f, out_t0=t0, out_T=Tbig)


def act_task(p, out_buf, out_idx, in_buf, in_idx, func, bias=0.0, scale=1.0,
             reads_extra=(), accum=None, tag="act"):
    if isinstance(bias, float) and bias != 0.0:
        p.const(bias)
    if isinstance(scale, float) and scale != 1.0:
        p.const(scale)
    reads = [in_buf] + list(reads_extra)
    kw = {}
    if isinstance(bias, tuple):
        reads.append(bias[0]); kw["bias"] = bias[0].ap[bias[1]]
    else:
        kw["bias"] = bias
    if isinstance(scale, tuple):
        reads.append(scale[0]); kw["scale"] = scale[1] if False else scale[0].ap[scale[1]]
    else:
        kw["scale"] = scale
    writes = [out_buf]
    if accum is not None:
        writes.append(accum[0]); kw["accum_out"] = accum[0].ap[accum[1]]

    def fn(sc):
        return sc.activation(out_buf.ap[out_idx], in_buf.ap[in_idx], func, **kw)
    return p.task("scalar", fn, reads=reads, writes=writes, tag=tag)


def tt_task(p, q, op, out, oi, a, ai, b, bi, tag="tt"):
    def fn(e):
        return e.tensor_tensor(out.ap[oi], a.ap[ai], b.ap[bi], op)
    return p.task(q, fn, reads=[a, b], writes=[out], tag=tag)


def ts_task(p, q, op, out, oi, a, ai, scalar, tag="ts"):
    reads = [a]
    if isinstance(scalar, tuple):
        reads.append(scalar[0]); s = scalar[0].ap[scalar[1]]
    else:
        s = scalar

    def fn(e):
        return e.tensor_single_scalar(out.ap[oi], a.ap[ai], s, op)
    return p.task(q, fn, reads=reads, writes=[out], tag=tag)


def stt_task(p, q, out, oi, a, ai, scalar, b, bi, op0, op1, tag="stt"):
    reads = [a, b]
    if isinstance(scalar, tuple):
        reads.append(scalar[0]); s = scalar[0].ap[scalar[1]]
    else:
        s = scalar

    def fn(e):
        return e.scalar_tensor_tensor(out.ap[oi], a.ap[ai], s, b.ap[bi], op0, op1)
    return p.task(q, fn, reads=reads, writes=[out], tag=tag)


def copy_task(p, q, out, oi, a, ai, tag="copy"):
    if q == "scalar":
        def fn(e):
            return e.copy(out.ap[oi], a.ap[ai])
    else:
        def fn(e):
            return e.tensor_copy(out.ap[oi], a.ap[ai])
    return p.task(q, fn, reads=[a], writes=[out], tag=tag)


def emit_mm_acc(p, psb, w_buf, w_slices, x_buf, x_slices, width, start=True,
                stop=True, extra_reads=(), tag="mm"):
    n = len(w_slices)

    def fn(te):
        last = None
        for k in range(n):
            last = te.matmul(psb.ap[:psb_rows(w_buf, w_slices[k]), 0:width] if False
                             else psb.ap[:, 0:width],
                             w_buf.ap[w_slices[k]], x_buf.ap[x_slices[k]],
                             start=(start and k == 0), stop=(stop and k == n - 1))
        return last
    return p.task("tensor", fn, reads=[w_buf, x_buf, *extra_reads], writes=[psb],
                  tag=tag)



def build_l0_a1():
    nc = bass.Bass("TRN2", target_bir_lowering=False, debug=False)
    xb = Buf("xb", nc.dram_tensor("xb", [1024, 8192], F32, kind="ExternalInput").ap(), "DRAM")
    w_inproj = Buf("w_inproj", nc.dram_tensor("w_inproj", [1024, 1024], F32, kind="ExternalInput").ap(), "DRAM")
    conv_w = Buf("conv_w", nc.dram_tensor("conv_w", [512, 4], F32, kind="ExternalInput").ap(), "DRAM")
    conv_b = Buf("conv_b", nc.dram_tensor("conv_b", [512], F32, kind="ExternalInput").ap(), "DRAM")
    xproj = Buf("xproj", nc.dram_tensor("xproj", [512, 96], F32, kind="ExternalInput").ap(), "DRAM")
    o_xc = Buf("o_xc", nc.dram_tensor("o_xc", [512, 8192], F32, kind="ExternalOutput").ap(), "DRAM")
    o_z = Buf("o_z", nc.dram_tensor("o_z", [512, 8192], F32, kind="ExternalOutput").ap(), "DRAM")
    o_dbc = Buf("o_dbc", nc.dram_tensor("o_dbc", [96, 8192], F32, kind="ExternalOutput").ap(), "DRAM")

    p = Prog(nc)
    pool = PsumPool(p, n=8)
    ones_col = make_ones_col(p)
    ones_lhs = make_ones_lhs(p)
    Tc = 512
    n_tc = 8192 // Tc

    w_sb = p.sb("w_inproj_sb", [128, 8 * 1024], BF16)
    p.dma("gpsimd_dma", w_sb.ap[:], w_inproj.ap.rearrange("(c p) o -> p c o", p=128),
          reads=[w_inproj], writes=[w_sb], tag="w_inproj")
    xproj_sb = p.sb("xproj_sb", [128, 4 * 96], BF16)
    p.dma("gpsimd_dma", xproj_sb.ap[:], xproj.ap.rearrange("(c p) o -> p c o", p=128),
          reads=[xproj], writes=[xproj_sb], tag="w_xproj")
    convw_sb = p.sb("convw_sb", [128, 4 * 4], F32)
    p.dma("sync", convw_sb.ap[:], conv_w.ap.rearrange("(c p) i -> p c i", p=128),
          reads=[conv_w], writes=[convw_sb], tag="w_convw")
    convb_sb = p.sb("convb_sb", [128, 4], F32)
    p.dma("sync", convb_sb.ap[:], conv_b.ap.rearrange("(c p) -> p c", p=128),
          reads=[conv_b], writes=[convb_sb], tag="w_convb",
          allow_slow_non_contiguous=True)

    scr = ln_scratch(p, Tc)
    x_ch = p.sb("x_ch", [128, 8 * Tc], F32)
    xn_bf = p.sb("xn_bf", [128, 8 * Tc], BF16)
    conv_in = [p.sb("conv_in", [128, Tc + 3], F32) for _ in range(4)]
    tail = [p.sb("tail", [128, 3], F32) for _ in range(4)]
    conv_acc = p.sb("conv_acc", [128, Tc], F32)
    xcc_f32 = p.sb("xcc_f32", [128, 4 * Tc], F32)
    xcc_bf = p.sb("xcc_bf", [128, 4 * Tc], BF16)
    z_ev = p.sb("z_ev", [128, 4 * Tc], F32)
    dbc_ev = p.sb("dbc_ev", [96, Tc], F32)

    for fc in range(4):
        p.task("vector", lambda v, fc=fc: v.memset(tail[fc].ap[:], 0.0),
               writes=[tail[fc]], tag="tail0")

    for tc in range(n_tc):
        t0 = tc * Tc
        p.dma("sync", x_ch.ap[:],
              xb.ap.rearrange("(c p) t -> p c t", p=128)[:, :, t0:t0 + Tc],
              reads=[xb], writes=[x_ch], tag="x_ld")
        emit_ln_chunk(p, pool, ones_lhs, ones_col, x_ch, Tc, xn_bf, scr)

        for oc in range(8):
            psb = pool.nxt()
            wsl = [(slice(None), slice(dc * 1024 + oc * 128, dc * 1024 + oc * 128 + 128))
                   for dc in range(8)]
            xsl = [(slice(None), slice(dc * Tc, dc * Tc + Tc)) for dc in range(8)]

            def mmfn(te, wsl=wsl, xsl=xsl, psb=psb):
                last = None
                for k in range(8):
                    last = te.matmul(psb.ap[:, 0:Tc], w_sb.ap[wsl[k]],
                                     xn_bf.ap[xsl[k]], start=(k == 0), stop=(k == 7))
                return last
            p.task("tensor", mmfn, reads=[w_sb, xn_bf], writes=[psb], tag="inproj")

            if oc < 4:
                fc = oc
                copy_task(p, "vector", conv_in[fc], (slice(None), slice(0, 3)),
                          tail[fc], (slice(None), slice(0, 3)), tag="tailcp")
                copy_task(p, "vector", conv_in[fc], (slice(None), slice(3, Tc + 3)),
                          psb, (slice(None), slice(0, Tc)), tag="convld")
                copy_task(p, "vector", tail[fc], (slice(None), slice(0, 3)),
                          conv_in[fc], (slice(None), slice(Tc, Tc + 3)), tag="tailsv")
                ts_task(p, "vector", ALU.mult, conv_acc, (slice(None), slice(0, Tc)),
                        conv_in[fc], (slice(None), slice(0, Tc)),
                        (convw_sb, (slice(None), slice(fc * 4, fc * 4 + 1))), tag="cv0")
                for i in range(1, 4):
                    stt_task(p, "vector", conv_acc, (slice(None), slice(0, Tc)),
                             conv_in[fc], (slice(None), slice(i, i + Tc)),
                             (convw_sb, (slice(None), slice(fc * 4 + i, fc * 4 + i + 1))),
                             conv_acc, (slice(None), slice(0, Tc)),
                             ALU.mult, ALU.add, tag="cv")
                act_task(p, xcc_f32, (slice(None), slice(fc * Tc, fc * Tc + Tc)),
                         conv_acc, (slice(None), slice(0, Tc)), AF.Silu,
                         bias=(convb_sb, (slice(None), slice(fc, fc + 1))), tag="convsilu")
                copy_task(p, "vector", xcc_bf, (slice(None), slice(fc * Tc, fc * Tc + Tc)),
                          xcc_f32, (slice(None), slice(fc * Tc, fc * Tc + Tc)), tag="xccbf")
            else:
                fc = oc - 4
                copy_task(p, "vector", z_ev, (slice(None), slice(fc * Tc, fc * Tc + Tc)),
                          psb, (slice(None), slice(0, Tc)), tag="zev")

        psd = pool.nxt()

        def dbcfn(te, psd=psd):
            last = None
            for k in range(4):
                last = te.matmul(psd.ap[0:96, 0:Tc],
                                 xproj_sb.ap[:, k * 96:(k + 1) * 96],
                                 xcc_bf.ap[:, k * Tc:(k + 1) * Tc],
                                 start=(k == 0), stop=(k == 3))
            return last
        p.task("tensor", dbcfn, reads=[xproj_sb, xcc_bf], writes=[psd], tag="dbc")
        copy_task(p, "vector", dbc_ev, (slice(None), slice(0, Tc)),
                  psd, (slice(0, 96), slice(0, Tc)), tag="dbcev")

        p.dma("gpsimd_dma", o_xc.ap.rearrange("(c p) t -> p c t", p=128)[:, :, t0:t0 + Tc],
              xcc_f32.ap.rearrange("p (c t) -> p c t", t=Tc), reads=[xcc_f32],
              writes=[o_xc], tag="st_xc")
        p.dma("gpsimd_dma", o_z.ap.rearrange("(c p) t -> p c t", p=128)[:, :, t0:t0 + Tc],
              z_ev.ap.rearrange("p (c t) -> p c t", t=Tc), reads=[z_ev],
              writes=[o_z], tag="st_z")
        p.dma("scalar_dma", o_dbc.ap[:, t0:t0 + Tc], dbc_ev.ap[0:96, 0:Tc],
              reads=[dbc_ev], writes=[o_dbc], tag="st_dbc")

    p.finalize()
    return nc



def build_l0_a2():
    nc = bass.Bass("TRN2", target_bir_lowering=False, debug=False)
    I = lambda n, s: Buf(n, nc.dram_tensor(n, s, F32, kind="ExternalInput").ap(), "DRAM")
    dbc4 = I("dbc4", [4, 96, 8192]); xcc = I("xcc", [512, 8192])
    z_pre = I("z_pre", [512, 8192]); dtp = I("dtp", [64, 512])
    dtb = I("dtb", [512]); alog = I("alog", [512, 16]); dvec = I("dvec", [512])
    o_sg = Buf("o_sg", nc.dram_tensor("o_sg", [512, 8192], F32, kind="ExternalOutput").ap(), "DRAM")
    bc_dram = Buf("bc_dram", nc.dram_tensor("bc_dram", [32, 8192], F32, kind="Internal").ap(), "DRAM")

    p = Prog(nc)
    pool = PsumPool(p, n=8)
    Tt = 1024
    n_pass = 8192 // Tt

    dtp_sb = p.sb("dtp_sb", [64, 512], BF16)
    p.dma("gpsimd_dma", dtp_sb.ap[:], dtp.ap[:, :], reads=[dtp], writes=[dtp_sb])
    dtb_sb = p.sb("dtb_sb", [128, 4], F32)
    p.dma("sync", dtb_sb.ap[:], dtb.ap.rearrange("(c p) -> p c", p=128),
          reads=[dtb], writes=[dtb_sb], allow_slow_non_contiguous=True)
    alog_sb = p.sb("alog_sb", [128, 4 * 16], F32)
    p.dma("sync", alog_sb.ap[:], alog.ap.rearrange("(c p) n -> p c n", p=128),
          reads=[alog], writes=[alog_sb])
    d_sb = p.sb("d_sb", [128, 4], F32)
    p.dma("sync", d_sb.ap[:], dvec.ap.rearrange("(c p) -> p c", p=128),
          reads=[dvec], writes=[d_sb], allow_slow_non_contiguous=True)
    a_sb = p.sb("a_sb", [128, 4 * 16], F32)
    act_task(p, a_sb, (slice(None), slice(None)), alog_sb, (slice(None), slice(None)),
             AF.Exp, tag="a_exp")
    ts_task(p, "vector", ALU.mult, a_sb, (slice(None), slice(None)),
            a_sb, (slice(None), slice(None)), -1.0, tag="a_neg")

    dbc_sb = p.sb("dbc_sb", [96, 8192], F32)
    pb = p.sb("dbc_pb", [96, 8192], F32)
    p.dma("sync", dbc_sb.ap[:, :], dbc4.ap[0, :, :], reads=[dbc4],
          writes=[dbc_sb], tag="dbcld")
    for k in range(1, 4):
        p.dma("sync", pb.ap[:, :], dbc4.ap[k, :, :], reads=[dbc4],
              writes=[pb], tag="dbcld")
        tt_task(p, "vector", ALU.add, dbc_sb, (slice(None), slice(None)),
                dbc_sb, (slice(None), slice(None)),
                pb, (slice(0, 96), slice(None)), tag="dbcsum")
    db_bf = p.sb("db_bf", [64, 8192], BF16)
    copy_task(p, "vector", db_bf, (slice(None), slice(None)),
              dbc_sb, (slice(0, 64), slice(None)), tag="dbbf")
    p.dma("scalar_dma", bc_dram.ap[:, :], dbc_sb.ap[64:96, :], reads=[dbc_sb],
          writes=[bc_dram], tag="bcst")

    delta_p = p.sb("delta_p", [128, 4 * Tt], F32)
    dx_p = p.sb("dx_p", [128, 4 * Tt], F32)
    y_p = p.sb("y_p", [128, 4 * Tt], F32)
    brep = p.sb("brep", [128, Tt], F32)
    crep = p.sb("crep", [128, Tt], F32)
    da_buf = p.sb("da_buf", [128, Tt], F32)
    bx_buf = p.sb("bx_buf", [128, Tt], F32)
    hs_buf = p.sb("hs_buf", [128, Tt], F32)
    tmp_buf = p.sb("tmp_buf", [128, Tt], F32)
    xcc_ch = p.sb("xcc_ch", [128, Tt], F32)
    z_ch = p.sb("z_ch", [128, Tt], F32)
    sg_ch = p.sb("sg_ch", [128, Tt], F32)
    carry = [p.sb("carry", [128, 16], F32) for _ in range(4)]
    spA = p.sb("spA", [128, Tt], F32)
    spB = p.sb("spB", [128, Tt], F32)
    for ec in range(4):
        p.task("vector", lambda v, ec=ec: v.memset(carry[ec].ap[:], 0.0),
               writes=[carry[ec]], tag="carry0")

    for ps_i in range(n_pass):
        t0 = ps_i * Tt
        for ec in range(4):
            dsl = slice(ec * Tt, (ec + 1) * Tt)
            for tch in range(Tt // 512):
                c0 = t0 + tch * 512
                psb = pool.nxt()

                def mmfn(te, ec=ec, c0=c0, psb=psb):
                    return te.matmul(psb.ap[:, 0:512],
                                     dtp_sb.ap[:, ec * 128:(ec + 1) * 128],
                                     db_bf.ap[:, c0:c0 + 512],
                                     start=True, stop=True)
                p.task("tensor", mmfn, reads=[dtp_sb, db_bf], writes=[psb], tag="dmm")
                act_task(p, delta_p, (slice(None), slice(ec * Tt + tch * 512, ec * Tt + tch * 512 + 512)),
                         psb, (slice(None), slice(0, 512)), AF.Identity,
                         bias=(dtb_sb, (slice(None), slice(ec, ec + 1))), tag="dev")
            act_task(p, spA, (slice(None), slice(0, Tt)), delta_p,
                     (slice(None), dsl), AF.Abs, tag="sp_abs")
            act_task(p, spB, (slice(None), slice(0, Tt)), spA,
                     (slice(None), slice(0, Tt)), AF.Exp, scale=-1.0, tag="sp_exp")
            act_task(p, spB, (slice(None), slice(0, Tt)), spB,
                     (slice(None), slice(0, Tt)), AF.Ln, bias=1.0, tag="sp_ln")
            ts_task(p, "vector", ALU.max, spA, (slice(None), slice(0, Tt)),
                    delta_p, (slice(None), dsl), 0.0, tag="sp_relu")
            tt_task(p, "vector", ALU.add, delta_p, (slice(None), dsl),
                    spA, (slice(None), slice(0, Tt)),
                    spB, (slice(None), slice(0, Tt)), tag="sp_add")
            p.dma("sync", xcc_ch.ap[:],
                  xcc.ap.rearrange("(c p) t -> p c t", p=128)[:, ec, t0:t0 + Tt],
                  reads=[xcc], writes=[xcc_ch], tag="xcld")
            tt_task(p, "vector", ALU.mult, dx_p, (slice(None), dsl),
                    delta_p, (slice(None), dsl), xcc_ch, (slice(None), slice(0, Tt)),
                    tag="dx")
            p.task("vector", lambda v, dsl=dsl: v.memset(y_p.ap[:, dsl], 0.0),
                   writes=[y_p], tag="y0")

        for n in range(16):
            p.dma("sync", brep.ap[:],
                  bc_dram.ap[n:n + 1, t0:t0 + Tt].partition_broadcast(128),
                  reads=[bc_dram], writes=[brep], tag="brep")
            p.dma("scalar_dma", crep.ap[:],
                  bc_dram.ap[16 + n:17 + n, t0:t0 + Tt].partition_broadcast(128),
                  reads=[bc_dram], writes=[crep], tag="crep")
            for ec in range(4):
                dsl = slice(ec * Tt, (ec + 1) * Tt)
                act_task(p, da_buf, (slice(None), slice(0, Tt)), delta_p,
                         (slice(None), dsl), AF.Exp,
                         scale=(a_sb, (slice(None), slice(ec * 16 + n, ec * 16 + n + 1))),
                         tag="deltaA")
                tt_task(p, "gpsimd", ALU.mult, bx_buf, (slice(None), slice(0, Tt)),
                        dx_p, (slice(None), dsl), brep, (slice(None), slice(0, Tt)),
                        tag="bx")

                def scfn(v, n=n, ec=ec):
                    return v.tensor_tensor_scan(
                        hs_buf.ap[:, 0:Tt], da_buf.ap[:, 0:Tt], bx_buf.ap[:, 0:Tt],
                        carry[ec].ap[:, n:n + 1], ALU.mult, ALU.add)
                p.task("vector", scfn, reads=[da_buf, bx_buf, carry[ec]],
                       writes=[hs_buf], tag="scan")
                copy_task(p, "vector", carry[ec], (slice(None), slice(n, n + 1)),
                          hs_buf, (slice(None), slice(Tt - 1, Tt)), tag="carrysv")
                tt_task(p, "gpsimd", ALU.mult, tmp_buf, (slice(None), slice(0, Tt)),
                        hs_buf, (slice(None), slice(0, Tt)), crep,
                        (slice(None), slice(0, Tt)), tag="yc")
                tt_task(p, "vector", ALU.add, y_p, (slice(None), dsl),
                        y_p, (slice(None), dsl), tmp_buf, (slice(None), slice(0, Tt)),
                        tag="yacc")

        for ec in range(4):
            dsl = slice(ec * Tt, (ec + 1) * Tt)
            p.dma("sync", xcc_ch.ap[:],
                  xcc.ap.rearrange("(c p) t -> p c t", p=128)[:, ec, t0:t0 + Tt],
                  reads=[xcc], writes=[xcc_ch], tag="xcld2")
            stt_task(p, "vector", y_p, (slice(None), dsl), xcc_ch,
                     (slice(None), slice(0, Tt)),
                     (d_sb, (slice(None), slice(ec, ec + 1))),
                     y_p, (slice(None), dsl), ALU.mult, ALU.add, tag="yD")
            p.dma("sync", z_ch.ap[:],
                  z_pre.ap.rearrange("(c p) t -> p c t", p=128)[:, ec, t0:t0 + Tt],
                  reads=[z_pre], writes=[z_ch], tag="zld")
            act_task(p, z_ch, (slice(None), slice(0, Tt)), z_ch,
                     (slice(None), slice(0, Tt)), AF.Silu, tag="zsilu")
            tt_task(p, "vector", ALU.mult, sg_ch, (slice(None), slice(0, Tt)),
                    y_p, (slice(None), dsl), z_ch, (slice(None), slice(0, Tt)),
                    tag="sg")
            p.dma("gpsimd_dma",
                  o_sg.ap.rearrange("(c p) t -> p c t", p=128)[:, ec, t0:t0 + Tt],
                  sg_ch.ap[:], reads=[sg_ch], writes=[o_sg], tag="sgst")

    p.finalize()
    return nc


def make_identity_bf(p):
    ones = p.sb("idones", [128, 128], BF16)
    ident = p.sb("ident", [128, 128], BF16)
    p.task("vector", lambda v: v.memset(ones.ap[:], 1.0), writes=[ones], tag="id1")

    def fn(g):
        return g.affine_select(ident.ap[:], ones.ap[:], pattern=[[-1, 128]],
                               compare_op=ALU.is_equal, fill=0.0, base=0,
                               channel_multiplier=1)
    p.task("gpsimd", fn, reads=[ones], writes=[ident], tag="idsel")
    return ident


def load_w_nat(p, dram_buf, R, C, tag="wnat"):
    n_rc = ceil_div(R, 128)
    b = p.sb(tag, [128, n_rc * C], BF16)
    if R % 128 == 0 and n_rc > 1:
        p.dma("gpsimd_dma", b.ap[:],
              dram_buf.ap.rearrange("(c p) o -> p c o", p=128),
              reads=[dram_buf], writes=[b], tag=tag)
    else:
        for rc in range(n_rc):
            r = min(128, R - rc * 128)
            p.dma("gpsimd_dma", b.ap[0:r, rc * C:(rc + 1) * C],
                  dram_buf.ap[rc * 128:rc * 128 + r, :],
                  reads=[dram_buf], writes=[b], tag=tag)
    return b, n_rc


def load_w_T(p, pool, ident, dram_buf, R, C, tag="wT", out_dtype=BF16):
    nat, n_rc = load_w_nat(p, dram_buf, R, C, tag=tag + "_nat")
    n_cc = ceil_div(C, 128)
    wT = p.sb(tag, [128, n_cc * R], out_dtype)
    for rc in range(n_rc):
        r = min(128, R - rc * 128)
        for cc in range(n_cc):
            c = min(128, C - cc * 128)
            psb = pool.nxt()

            def trf(te, rc=rc, cc=cc, r=r, c=c, psb=psb):
                return te.transpose(psb.ap.bitcast(BF16)[0:c, 0:r],
                                    nat.ap[0:r, rc * C + cc * 128: rc * C + cc * 128 + c],
                                    ident.ap[0:r, 0:r])
            p.task("tensor", trf, reads=[nat, ident], writes=[psb], tag=tag + "_tr")

            def ev(v, rc=rc, cc=cc, r=r, c=c, psb=psb):
                return v.tensor_copy(wT.ap[0:c, cc * R + rc * 128: cc * R + rc * 128 + r],
                                     psb.ap.bitcast(BF16)[0:c, 0:r])
            p.task("vector", ev, reads=[psb], writes=[wT], tag=tag + "_ev")
    return wT



def build_l0_a3():
    nc = bass.Bass("TRN2", target_bir_lowering=False, debug=False)
    I = lambda n, s: Buf(n, nc.dram_tensor(n, s, F32, kind="ExternalInput").ap(), "DRAM")
    x_own = I("x_own", [1024, 2048]); sg_own = I("sg_own", [2048, 2048])
    outproj = I("outproj", [2048, 1024]); w_ffn = I("w_ffn", [1024, 1024])
    b_ffn = I("b_ffn", [1024]); lora_down = I("lora_down", [4, 100])
    lora_up = I("lora_up", [50, 4]); down_aux = I("down_aux", [100, 1024])
    up_aux = I("up_aux", [1024, 50])
    o_x1 = Buf("o_x1", nc.dram_tensor("o_x1", [1024, 2048], F32, kind="ExternalOutput").ap(), "DRAM")

    p = Prog(nc)
    pool = PsumPool(p, n=8)
    ones_col = make_ones_col(p)
    ones_lhs = make_ones_lhs(p)
    ident = make_identity_bf(p)

    wop_sb, _ = load_w_nat(p, outproj, 2048, 1024, tag="wop")
    wT = load_w_T(p, pool, ident, w_ffn, 1024, 1024, tag="wT")
    ldT = load_w_T(p, pool, ident, lora_down, 4, 100, tag="ldT")
    luT = load_w_T(p, pool, ident, lora_up, 50, 4, tag="luT")
    uaT = load_w_T(p, pool, ident, up_aux, 1024, 50, tag="uaT")
    da_sb, _ = load_w_nat(p, down_aux, 100, 1024, tag="da")
    bffn_sb = p.sb("bffn", [128, 8], F32)
    p.dma("sync", bffn_sb.ap[:], b_ffn.ap.rearrange("(c p) -> p c", p=128),
          reads=[b_ffn], writes=[bffn_sb], allow_slow_non_contiguous=True)

    m1 = p.sb("m1", [4, 1024], BF16)
    for ncol in range(2):
        psb = pool.nxt()

        def f(te, ncol=ncol, psb=psb):
            return te.matmul(psb.ap[0:4, 0:512], ldT.ap[0:100, 0:4],
                             da_sb.ap[0:100, ncol * 512:(ncol + 1) * 512],
                             start=True, stop=True)
        p.task("tensor", f, reads=[ldT, da_sb], writes=[psb], tag="m1")
        copy_task(p, "vector", m1, (slice(0, 4), slice(ncol * 512, (ncol + 1) * 512)),
                  psb, (slice(0, 4), slice(0, 512)), tag="m1ev")
    m2 = p.sb("m2", [50, 1024], BF16)
    for ncol in range(2):
        psb = pool.nxt()

        def f(te, ncol=ncol, psb=psb):
            return te.matmul(psb.ap[0:50, 0:512], luT.ap[0:4, 0:50],
                             m1.ap[0:4, ncol * 512:(ncol + 1) * 512],
                             start=True, stop=True)
        p.task("tensor", f, reads=[luT, m1], writes=[psb], tag="m2")
        copy_task(p, "vector", m2, (slice(0, 50), slice(ncol * 512, (ncol + 1) * 512)),
                  psb, (slice(0, 50), slice(0, 512)), tag="m2ev")
    weff = p.sb("weff", [128, 8 * 1024], BF16)
    for ic in range(8):
        for oc2 in range(2):
            psb = pool.nxt()

            def f(te, ic=ic, oc2=oc2, psb=psb):
                return te.matmul(psb.ap[:, 0:512],
                                 m2.ap[0:50, ic * 128:(ic + 1) * 128],
                                 uaT.ap[0:50, oc2 * 512:(oc2 + 1) * 512],
                                 start=True, stop=True)
            p.task("tensor", f, reads=[m2, uaT], writes=[psb], tag="dw")

            def ev(v, ic=ic, oc2=oc2, psb=psb):
                sl = slice(ic * 1024 + oc2 * 512, ic * 1024 + oc2 * 512 + 512)
                return v.tensor_add(weff.ap[:, sl], psb.ap[:, 0:512], wT.ap[:, sl])
            p.task("vector", ev, reads=[psb, wT], writes=[weff], tag="dwev")

    Tc = 512
    scr = ln_scratch(p, Tc)
    x_ch = p.sb("x_ch", [128, 8 * Tc], F32)
    xn_bf = p.sb("xn_bf", [128, 8 * Tc], BF16)
    g_sb = p.sb("g_sb", [128, 8 * Tc], F32)
    sg_bf = p.sb("sg_bf", [128, 16 * Tc], BF16)
    s_ch = p.sb("s_ch", [128, Tc], F32)
    x1_ch = p.sb("x1_ch", [128, 8 * Tc], F32)

    for tq in range(2048 // Tc):
        t0 = tq * Tc
        p.dma("sync", x_ch.ap[:],
              x_own.ap.rearrange("(c p) t -> p c t", p=128)[:, :, t0:t0 + Tc],
              reads=[x_own], writes=[x_ch], tag="xld")
        emit_ln_chunk(p, pool, ones_lhs, ones_col, x_ch, Tc, xn_bf, scr)
        p.dma("gpsimd_dma", sg_bf.ap[:],
              sg_own.ap.rearrange("(c p) t -> p c t", p=128)[:, :, t0:t0 + Tc],
              reads=[sg_own], writes=[sg_bf], tag="sgld")
        for oc in range(8):
            psb = pool.nxt()

            def f(te, oc=oc, psb=psb):
                last = None
                for k in range(8):
                    last = te.matmul(psb.ap[:, 0:Tc],
                                     weff.ap[:, k * 1024 + oc * 128: k * 1024 + oc * 128 + 128],
                                     xn_bf.ap[:, k * Tc:(k + 1) * Tc],
                                     start=(k == 0), stop=(k == 7))
                return last
            p.task("tensor", f, reads=[weff, xn_bf], writes=[psb], tag="gmm")
            act_task(p, g_sb, (slice(None), slice(oc * Tc, oc * Tc + Tc)),
                     psb, (slice(None), slice(0, Tc)), AF.Silu,
                     bias=(bffn_sb, (slice(None), slice(oc, oc + 1))), tag="gsilu")
            psb2 = pool.nxt()

            def f2(te, oc=oc, psb2=psb2):
                last = None
                for k in range(16):
                    last = te.matmul(psb2.ap[:, 0:Tc],
                                     wop_sb.ap[:, k * 1024 + oc * 128: k * 1024 + oc * 128 + 128],
                                     sg_bf.ap[:, k * Tc:(k + 1) * Tc],
                                     start=(k == 0), stop=(k == 15))
                return last
            p.task("tensor", f2, reads=[wop_sb, sg_bf], writes=[psb2], tag="smm")
            copy_task(p, "vector", s_ch, (slice(None), slice(0, Tc)),
                      psb2, (slice(None), slice(0, Tc)), tag="sev")
            tt_task(p, "vector", ALU.mult, s_ch, (slice(None), slice(0, Tc)),
                    s_ch, (slice(None), slice(0, Tc)),
                    g_sb, (slice(None), slice(oc * Tc, oc * Tc + Tc)), tag="sgmul")
            tt_task(p, "vector", ALU.add, x1_ch, (slice(None), slice(oc * Tc, oc * Tc + Tc)),
                    x_ch, (slice(None), slice(oc * Tc, oc * Tc + Tc)),
                    s_ch, (slice(None), slice(0, Tc)), tag="x1")
        p.dma("gpsimd_dma",
              o_x1.ap.rearrange("(c p) t -> p c t", p=128)[:, :, t0:t0 + Tc],
              x1_ch.ap.rearrange("p (c t) -> p c t", t=Tc),
              reads=[x1_ch], writes=[o_x1], tag="x1st")

    p.finalize()
    return nc



TWO_PI = 6.283185307179586
LOG1E4 = 9.210340371976184


def emit_range_reduce_sin(p, out_buf, osl, ang_buf, asl, scr_q, scr_i, tag="rrs"):
    ts_task(p, "vector", ALU.mult, scr_q, (slice(None), slice(None)),
            ang_buf, asl, 1.0 / TWO_PI, tag=tag + "_q")
    copy_task(p, "vector", scr_i, (slice(None), slice(None)),
              scr_q, (slice(None), slice(None)), tag=tag + "_i")
    copy_task(p, "vector", scr_q, (slice(None), slice(None)),
              scr_i, (slice(None), slice(None)), tag=tag + "_f")
    stt_task(p, "vector", scr_q, (slice(None), slice(None)),
             scr_q, (slice(None), slice(None)), -TWO_PI,
             ang_buf, asl, ALU.mult, ALU.add, tag=tag + "_u")
    act_task(p, out_buf, osl, scr_q, (slice(None), slice(None)), AF.Sin,
             tag=tag + "_sin")


def build_l1_b1():
    nc = bass.Bass("TRN2", target_bir_lowering=False, debug=False)
    I = lambda n, s: Buf(n, nc.dram_tensor(n, s, F32, kind="ExternalInput").ap(), "DRAM")
    x1 = I("x1", [1024, 2048])
    wq_e = I("wq_e", [1024, 512]); wq_o = I("wq_o", [1024, 512])
    wk_e = I("wk_e", [1024, 512]); wk_o = I("wk_o", [1024, 512])
    wvT = I("wvT", [1024, 1024])
    posv = I("posv", [1, 2048])
    O = lambda n, s: Buf(n, nc.dram_tensor(n, s, BF16, kind="ExternalOutput").ap(), "DRAM")
    o_q = O("o_q", [1024, 2048]); o_k = O("o_k", [1024, 2048]); o_v = O("o_v", [1024, 2048])

    p = Prog(nc)
    pool = PsumPool(p, n=8)
    ones_col = make_ones_col(p)
    ones_lhs = make_ones_lhs(p)

    wq_e_sb, _ = load_w_nat(p, wq_e, 1024, 512, tag="wqe")
    wq_o_sb, _ = load_w_nat(p, wq_o, 1024, 512, tag="wqo")
    wk_e_sb, _ = load_w_nat(p, wk_e, 1024, 512, tag="wke")
    wk_o_sb, _ = load_w_nat(p, wk_o, 1024, 512, tag="wko")
    wv_sb, _ = load_w_nat(p, wvT, 1024, 1024, tag="wv")

    SC = -2.0 * LOG1E4 / 1024.0
    th = p.sb("theta", [128, 4], F32)

    def iofn(g):
        return g.iota(th.ap[:], pattern=[[128, 4]], base=0, channel_multiplier=1,
                      allow_small_or_imprecise_dtypes=True)
    p.task("gpsimd", iofn, writes=[th], tag="th_iota")
    act_task(p, th, (slice(None), slice(None)), th, (slice(None), slice(None)),
             AF.Exp, scale=SC, bias=-SC, tag="th_exp")

    posrep = p.sb("posrep", [128, 2048], F32)
    p.dma("sync", posrep.ap[:], posv.ap[0:1, :].partition_broadcast(128),
          reads=[posv], writes=[posrep], tag="posrep")

    cosb = p.sb("cosb", [128, 4 * 2048], BF16)
    sinb = p.sb("sinb", [128, 4 * 2048], BF16)
    ang = p.sb("ang", [128, 2048], F32)
    srq = p.sb("srq", [128, 2048], F32)
    sri = p.sb("sri", [128, 2048], mybir.dt.int32)
    for fc in range(4):
        sl = (slice(None), slice(fc * 2048, (fc + 1) * 2048))
        ts_task(p, "vector", ALU.mult, ang, (slice(None), slice(None)),
                posrep, (slice(None), slice(None)),
                (th, (slice(None), slice(fc, fc + 1))), tag="ang")
        emit_range_reduce_sin(p, sinb, sl, ang, (slice(None), slice(None)),
                              srq, sri, tag="sin")
        ts_task(p, "vector", ALU.add, ang, (slice(None), slice(None)),
                ang, (slice(None), slice(None)), TWO_PI / 4.0, tag="angc")
        emit_range_reduce_sin(p, cosb, sl, ang, (slice(None), slice(None)),
                              srq, sri, tag="cos")

    Tc = 512
    scr = ln_scratch(p, Tc)
    x_ch = p.sb("x_ch", [128, 8 * Tc], F32)
    xn_bf = p.sb("xn_bf", [128, 8 * Tc], BF16)
    qe = p.sb("qe", [128, 4 * Tc], F32)
    qo = p.sb("qo", [128, 4 * Tc], F32)
    rot = p.sb("rot", [128, Tc], F32)
    out_bf = p.sb("out_bf", [128, Tc], BF16)

    def mm8(tag, w_sb, wcols, oc, psb, rhs):
        def f(te, oc=oc, psb=psb):
            last = None
            for k in range(8):
                last = te.matmul(psb.ap[:, 0:Tc],
                                 w_sb.ap[:, k * wcols + oc * 128:
                                         k * wcols + oc * 128 + 128],
                                 rhs.ap[:, k * Tc:(k + 1) * Tc],
                                 start=(k == 0), stop=(k == 7))
            return last
        p.task("tensor", f, reads=[w_sb, rhs], writes=[psb], tag=tag)

    for tq in range(2048 // Tc):
        t0 = tq * Tc
        p.dma("sync", x_ch.ap[:],
              x1.ap.rearrange("(c p) t -> p c t", p=128)[:, :, t0:t0 + Tc],
              reads=[x1], writes=[x_ch], tag="xld")
        emit_ln_chunk(p, pool, ones_lhs, ones_col, x_ch, Tc, xn_bf, scr)
        for which, we_sb, wo_sb, o_buf in (("q", wq_e_sb, wq_o_sb, o_q),
                                           ("k", wk_e_sb, wk_o_sb, o_k)):
            for oc in range(4):
                psb = pool.nxt(); psb2 = pool.nxt()
                mm8(which + "e", we_sb, 512, oc, psb, xn_bf)
                mm8(which + "o", wo_sb, 512, oc, psb2, xn_bf)
                copy_task(p, "vector", qe, (slice(None), slice(oc * Tc, oc * Tc + Tc)),
                          psb, (slice(None), slice(0, Tc)), tag="qeev")
                copy_task(p, "vector", qo, (slice(None), slice(oc * Tc, oc * Tc + Tc)),
                          psb2, (slice(None), slice(0, Tc)), tag="qoev")
            for oc in range(4):
                qsl = (slice(None), slice(oc * Tc, oc * Tc + Tc))
                csl = (slice(None), slice(oc * 2048 + t0, oc * 2048 + t0 + Tc))
                tt_task(p, "vector", ALU.mult, rot, (slice(None), slice(0, Tc)),
                        qe, qsl, cosb, csl, tag="re1")
                tt_task(p, "vector", ALU.mult, srq, (slice(None), slice(0, Tc)),
                        qo, qsl, sinb, csl, tag="re2")
                tt_task(p, "vector", ALU.add, rot, (slice(None), slice(0, Tc)),
                        rot, (slice(None), slice(0, Tc)),
                        srq, (slice(None), slice(0, Tc)), tag="re3")
                copy_task(p, "vector", out_bf, (slice(None), slice(0, Tc)),
                          rot, (slice(None), slice(0, Tc)), tag="rebf")
                p.dma("gpsimd_dma",
                      o_buf.ap.rearrange("(c p) t -> p c t", p=128)[:, oc, t0:t0 + Tc],
                      out_bf.ap[:, 0:Tc], reads=[out_bf], writes=[o_buf], tag="rest")
                tt_task(p, "vector", ALU.mult, rot, (slice(None), slice(0, Tc)),
                        qo, qsl, cosb, csl, tag="ro1")
                tt_task(p, "vector", ALU.mult, srq, (slice(None), slice(0, Tc)),
                        qe, qsl, sinb, csl, tag="ro2")
                tt_task(p, "vector", ALU.subtract, rot, (slice(None), slice(0, Tc)),
                        rot, (slice(None), slice(0, Tc)),
                        srq, (slice(None), slice(0, Tc)), tag="ro3")
                copy_task(p, "vector", out_bf, (slice(None), slice(0, Tc)),
                          rot, (slice(None), slice(0, Tc)), tag="robf")
                p.dma("gpsimd_dma",
                      o_buf.ap.rearrange("(c p) t -> p c t", p=128)[:, 4 + oc, t0:t0 + Tc],
                      out_bf.ap[:, 0:Tc], reads=[out_bf], writes=[o_buf], tag="rost")
        for oc in range(8):
            psb = pool.nxt()
            mm8("v", wv_sb, 1024, oc, psb, xn_bf)
            copy_task(p, "vector", out_bf, (slice(None), slice(0, Tc)),
                      psb, (slice(None), slice(0, Tc)), tag="vev")
            p.dma("gpsimd_dma",
                  o_v.ap.rearrange("(c p) t -> p c t", p=128)[:, oc, t0:t0 + Tc],
                  out_bf.ap[:, 0:Tc], reads=[out_bf], writes=[o_v], tag="vst")

    p.finalize()
    return nc


def build_l1_b2():
    nc = bass.Bass("TRN2", target_bir_lowering=False, debug=False)
    q_rot = Buf("q_rot", nc.dram_tensor("q_rot", [1024, 2048], BF16, kind="ExternalInput").ap(), "DRAM")
    k_rot = Buf("k_rot", nc.dram_tensor("k_rot", [1024, 8192], BF16, kind="ExternalInput").ap(), "DRAM")
    v_tm = Buf("v_tm", nc.dram_tensor("v_tm", [8192, 1024], BF16, kind="ExternalInput").ap(), "DRAM")
    qpos = Buf("qpos", nc.dram_tensor("qpos", [128, 16], F32, kind="ExternalInput").ap(), "DRAM")
    kpos = Buf("kpos", nc.dram_tensor("kpos", [1, 8192], F32, kind="ExternalInput").ap(), "DRAM")
    o_att = Buf("o_att", nc.dram_tensor("o_att", [2048, 1024], F32, kind="ExternalOutput").ap(), "DRAM")

    p = Prog(nc)
    pool = PsumPool(p, n=8)
    ident = make_identity_bf(p)

    q_sb = p.sb("q_sb", [128, 8 * 2048], BF16)
    p.dma("sync", q_sb.ap[:], q_rot.ap.rearrange("(c p) t -> p c t", p=128),
          reads=[q_rot], writes=[q_sb], tag="qld")
    qp_sb = p.sb("qp_sb", [128, 16], F32)
    p.dma("sync", qp_sb.ap[:], qpos.ap[:, :], reads=[qpos], writes=[qp_sb], tag="qpld")
    krep = p.sb("krep", [128, 8192], F32)
    p.dma("sync", krep.ap[:], kpos.ap[0:1, :].partition_broadcast(128),
          reads=[kpos], writes=[krep], tag="kpld")

    k_ch = p.sb("k_ch", [128, 8 * 512], BF16)
    v_ch = p.sb("v_ch", [128, 4 * 1024], BF16)
    s_sb = p.sb("s_sb", [128, 512], F32)
    cmp_sb = p.sb("cmp_sb", [128, 512], F32)
    p_bf = p.sb("p_bf", [128, 512], BF16)
    pT = p.sb("pT", [128, 4 * 128], BF16)
    o_acc = p.sb("o_acc", [128, 1024], F32)
    m_run = p.sb("m_run", [128, 4], F32)
    lp = p.sb("lp", [128, 2], F32)
    o_out = p.sb("o_out", [128, 1024], F32)
    neg_m = p.sb("neg_m", [128, 1], F32)

    for qc in range(16):
        q0 = qc * 128
        p.task("vector", lambda v: v.memset(o_acc.ap[:], 0.0), writes=[o_acc], tag="o0")
        p.task("vector", lambda v: v.memset(m_run.ap[:, 0:1], -2000.0), writes=[m_run], tag="m0")
        p.task("vector", lambda v: v.memset(m_run.ap[:, 3:4], 0.0), writes=[m_run], tag="l0")
        for kb in range(16):
            kb0 = kb * 512
            p.dma("sync", k_ch.ap[:],
                  k_rot.ap.rearrange("(c p) t -> p c t", p=128)[:, :, kb0:kb0 + 512],
                  reads=[k_rot], writes=[k_ch], tag="kld")
            p.dma("scalar_dma", v_ch.ap[:],
                  v_tm.ap[kb0:kb0 + 512, :].rearrange("(c p) d -> p c d", p=128),
                  reads=[v_tm], writes=[v_ch], tag="vld")
            psb = pool.nxt()

            def smm(te, psb=psb, q0=q0):
                last = None
                for k in range(8):
                    last = te.matmul(psb.ap[:, 0:512],
                                     q_sb.ap[:, k * 2048 + q0:k * 2048 + q0 + 128],
                                     k_ch.ap[:, k * 512:(k + 1) * 512],
                                     start=(k == 0), stop=(k == 7))
                return last
            p.task("tensor", smm, reads=[q_sb, k_ch], writes=[psb], tag="smm")
            ts_task(p, "vector", ALU.is_le, cmp_sb, (slice(None), slice(0, 512)),
                    krep, (slice(None), slice(kb0, kb0 + 512)),
                    (qp_sb, (slice(None), slice(qc, qc + 1))), tag="mcmp")
            def mka(v):
                return v.tensor_scalar(cmp_sb.ap[:, 0:512], cmp_sb.ap[:, 0:512],
                                       937.5, -937.5, ALU.mult, ALU.add)
            p.task("vector", mka, reads=[cmp_sb], writes=[cmp_sb], tag="mka")
            stt_task(p, "vector", s_sb, (slice(None), slice(0, 512)),
                     psb, (slice(None), slice(0, 512)), 1.0 / 32.0,
                     cmp_sb, (slice(None), slice(0, 512)), ALU.mult, ALU.add,
                     tag="madd")
            def rmax(v):
                return v.reduce_max(m_run.ap[:, 1:2], s_sb.ap[:, 0:512], axis=AX.X)
            p.task("vector", rmax, reads=[s_sb], writes=[m_run], tag="rmax")
            def mmax(v):
                return v.tensor_max(m_run.ap[:, 1:2], m_run.ap[:, 1:2], m_run.ap[:, 0:1])
            p.task("vector", mmax, reads=[m_run], writes=[m_run], tag="mmax")
            def fdiff(v):
                return v.tensor_sub(m_run.ap[:, 2:3], m_run.ap[:, 0:1], m_run.ap[:, 1:2])
            p.task("vector", fdiff, reads=[m_run], writes=[m_run], tag="fdiff")
            act_task(p, m_run, (slice(None), slice(2, 3)), m_run,
                     (slice(None), slice(2, 3)), AF.Exp, tag="fexp")
            copy_task(p, "vector", m_run, (slice(None), slice(0, 1)),
                      m_run, (slice(None), slice(1, 2)), tag="mupd")
            ts_task(p, "vector", ALU.mult, neg_m, (slice(None), slice(0, 1)),
                    m_run, (slice(None), slice(1, 2)), -1.0, tag="negm")

            def pexp(sc):
                return sc.activation(p_bf.ap[:, 0:512], s_sb.ap[:, 0:512], AF.Exp,
                                     bias=neg_m.ap[:, 0:1], scale=1.0)
            p.task("scalar", pexp, reads=[s_sb, neg_m], writes=[p_bf], tag="pexp")

            def lps(v):
                return v.reduce_sum(lp.ap[:, 0:1], p_bf.ap[:, 0:512], axis=AX.X)
            p.task("vector", lps, reads=[p_bf], writes=[lp], tag="lps")
            stt_task(p, "vector", m_run, (slice(None), slice(3, 4)),
                     m_run, (slice(None), slice(3, 4)),
                     (m_run, (slice(None), slice(2, 3))),
                     lp, (slice(None), slice(0, 1)), ALU.mult, ALU.add, tag="lupd")
            for sub in range(4):
                pst = pool.nxt()

                def trf(te, sub=sub, pst=pst):
                    return te.transpose(pst.ap.bitcast(BF16)[0:128, 0:128],
                                        p_bf.ap[:, sub * 128:(sub + 1) * 128],
                                        ident.ap[:, :])
                p.task("tensor", trf, reads=[p_bf, ident], writes=[pst], tag="ptr")

                def ptev(v, sub=sub, pst=pst):
                    return v.tensor_copy(pT.ap[:, sub * 128:(sub + 1) * 128],
                                         pst.ap.bitcast(BF16)[0:128, 0:128])
                p.task("vector", ptev, reads=[pst], writes=[pT], tag="ptev")
            for dv in range(2):
                pav = pool.nxt()

                def avm(te, dv=dv, pav=pav):
                    last = None
                    for sub in range(4):
                        last = te.matmul(pav.ap[:, 0:512],
                                         pT.ap[:, sub * 128:(sub + 1) * 128],
                                         v_ch.ap[:, sub * 1024 + dv * 512:
                                                 sub * 1024 + dv * 512 + 512],
                                         start=(sub == 0), stop=(sub == 3))
                    return last
                p.task("tensor", avm, reads=[pT, v_ch], writes=[pav], tag="avm")
                stt_task(p, "vector", o_acc, (slice(None), slice(dv * 512, dv * 512 + 512)),
                         o_acc, (slice(None), slice(dv * 512, dv * 512 + 512)),
                         (m_run, (slice(None), slice(2, 3))),
                         pav, (slice(None), slice(0, 512)), ALU.mult, ALU.add,
                         tag="oupd")
        def rec(v):
            return v.reciprocal(lp.ap[:, 1:2], m_run.ap[:, 3:4])
        p.task("vector", rec, reads=[m_run], writes=[lp], tag="linv")
        for dv in range(2):
            ts_task(p, "vector", ALU.mult, o_out, (slice(None), slice(dv * 512, dv * 512 + 512)),
                    o_acc, (slice(None), slice(dv * 512, dv * 512 + 512)),
                    (lp, (slice(None), slice(1, 2))), tag="onorm")
        p.dma("gpsimd_dma", o_att.ap[q0:q0 + 128, :], o_out.ap[:],
              reads=[o_out], writes=[o_att], tag="ost")

    p.finalize()
    return nc


def make_identity_f32(p):
    ones = p.sb("idonesf", [128, 128], F32)
    ident = p.sb("identf", [128, 128], F32)
    p.task("vector", lambda v: v.memset(ones.ap[:], 1.0), writes=[ones], tag="idf1")

    def fn(g):
        return g.affine_select(ident.ap[:], ones.ap[:], pattern=[[-1, 128]],
                               compare_op=ALU.is_equal, fill=0.0, base=0,
                               channel_multiplier=1)
    p.task("gpsimd", fn, reads=[ones], writes=[ident], tag="idfsel")
    return ident


def build_l1_b3():
    nc = bass.Bass("TRN2", target_bir_lowering=False, debug=False)
    I = lambda n, s: Buf(n, nc.dram_tensor(n, s, F32, kind="ExternalInput").ap(), "DRAM")
    x1 = I("x1", [1024, 2048]); s_att = I("s_att", [1024, 2048])
    gwT = I("gwT", [1024, 8]); ebias = I("ebias", [1, 8])
    gT = I("gT", [8, 1024, 512]); uT = I("uT", [8, 1024, 512])
    dT = I("dT", [8, 512, 1024])
    o_x2 = Buf("o_x2", nc.dram_tensor("o_x2", [1024, 2048], F32, kind="ExternalOutput").ap(), "DRAM")
    cw_dram = Buf("cw_dram", nc.dram_tensor("cw_dram", [8, 2048], F32, kind="Internal").ap(), "DRAM")

    p = Prog(nc)
    pool = PsumPool(p, n=8)
    ones_col = make_ones_col(p)
    ones_lhs = make_ones_lhs(p)
    identf = make_identity_f32(p)

    gw_sb, _ = load_w_nat(p, gwT, 1024, 8, tag="gw")
    eb_rep = p.sb("eb_rep", [128, 8], F32)
    p.dma("sync", eb_rep.ap[:], ebias.ap[0:1, :].partition_broadcast(128),
          reads=[ebias], writes=[eb_rep], tag="ebld")

    Tc = 512
    scr = ln_scratch(p, Tc)
    x_ch = p.sb("x_ch", [128, 8 * Tc], F32)
    xn_bf = p.sb("xn_bf", [128, 8 * Tc], BF16)
    lg = p.sb("lg", [8, Tc], F32)
    sc_fm = p.sb("sc_fm", [8, Tc], F32)
    sc_tm = p.sb("sc_tm", [128, 4 * 8], F32)
    s4c = p.sb("s4c", [128, 8], F32)
    gs = p.sb("gs", [128, 4], F32)
    em = p.sb("em", [128, 8], F32)
    mx = p.sb("mx", [128, 4], F32)
    cw_tm = p.sb("cw_tm", [128, 8], F32)
    cw_fm = p.sb("cw_fm", [8, Tc], F32)
    wg_sb = p.sb("wg_sb", [128, 8 * 512], BF16)
    wu_sb = p.sb("wu_sb", [128, 8 * 512], BF16)
    wd_sb = p.sb("wd_sb", [128, 4 * 1024], BF16)
    hg = p.sb("hg", [128, Tc], F32)
    h_bf = p.sb("h_bf", [128, 4 * Tc], BF16)
    y_sb = p.sb("y_sb", [128, 8 * Tc], F32)
    yt = p.sb("yt", [128, Tc], F32)
    cwrep = p.sb("cwrep", [128, Tc], F32)
    sa_ch = p.sb("sa_ch", [128, 8 * Tc], F32)

    for tq in range(2048 // Tc):
        t0 = tq * Tc
        p.dma("sync", x_ch.ap[:],
              x1.ap.rearrange("(c p) t -> p c t", p=128)[:, :, t0:t0 + Tc],
              reads=[x1], writes=[x_ch], tag="xld")
        emit_ln_chunk(p, pool, ones_lhs, ones_col, x_ch, Tc, xn_bf, scr)
        psb = pool.nxt()

        def lmm(te, psb=psb):
            last = None
            for k in range(8):
                last = te.matmul(psb.ap[0:8, 0:Tc], gw_sb.ap[:, k * 8:(k + 1) * 8],
                                 xn_bf.ap[:, k * Tc:(k + 1) * Tc],
                                 start=(k == 0), stop=(k == 7))
            return last
        p.task("tensor", lmm, reads=[gw_sb, xn_bf], writes=[psb], tag="lmm")
        copy_task(p, "vector", lg, (slice(0, 8), slice(0, Tc)),
                  psb, (slice(0, 8), slice(0, Tc)), tag="lgev")
        act_task(p, sc_fm, (slice(0, 8), slice(0, Tc)), lg, (slice(0, 8), slice(0, Tc)),
                 AF.Exp, scale=-1.0, tag="sige")
        ts_task(p, "vector", ALU.add, sc_fm, (slice(0, 8), slice(0, Tc)),
                sc_fm, (slice(0, 8), slice(0, Tc)), 1.0, tag="sig1")
        def sigr(v):
            return v.reciprocal(sc_fm.ap[0:8, 0:Tc], sc_fm.ap[0:8, 0:Tc])
        p.task("vector", sigr, reads=[sc_fm], writes=[sc_fm], tag="sigr")
        for sub in range(4):
            pst = pool.nxt()

            def trf(te, sub=sub, pst=pst):
                return te.transpose(pst.ap[0:128, 0:8],
                                    sc_fm.ap[0:8, sub * 128:(sub + 1) * 128],
                                    identf.ap[0:8, 0:8])
            p.task("tensor", trf, reads=[sc_fm, identf], writes=[pst], tag="sctr")
            copy_task(p, "vector", sc_tm, (slice(None), slice(sub * 8, sub * 8 + 8)),
                      pst, (slice(None), slice(0, 8)), tag="sctev")
        for sub in range(4):
            ssl = (slice(None), slice(sub * 8, sub * 8 + 8))
            tt_task(p, "vector", ALU.add, s4c, (slice(None), slice(0, 8)),
                    sc_tm, ssl, eb_rep, (slice(None), slice(0, 8)), tag="s4c")
            def gsf(v):
                return v.tensor_add(gs.ap[:, 0:4],
                                    s4c.ap.rearrange("p (g two) -> p g two", two=2)[:, :, 0],
                                    s4c.ap.rearrange("p (g two) -> p g two", two=2)[:, :, 1])
            p.task("vector", gsf, reads=[s4c], writes=[gs], tag="gs")
            def m1f(v):
                return v.reduce_max(mx.ap[:, 0:1], gs.ap[:, 0:4], axis=AX.X)
            p.task("vector", m1f, reads=[gs], writes=[mx], tag="m1")
            ts_task(p, "vector", ALU.is_equal, em, (slice(None), slice(0, 4)),
                    gs, (slice(None), slice(0, 4)), (mx, (slice(None), slice(0, 1))),
                    tag="eq1")
            stt_task(p, "vector", em, (slice(None), slice(0, 4)),
                     em, (slice(None), slice(0, 4)), -1.0e30,
                     gs, (slice(None), slice(0, 4)), ALU.mult, ALU.add, tag="g2")
            def m2f(v):
                return v.reduce_max(mx.ap[:, 1:2], em.ap[:, 0:4], axis=AX.X)
            p.task("vector", m2f, reads=[em], writes=[mx], tag="m2")
            ts_task(p, "vector", ALU.is_ge, gs, (slice(None), slice(0, 4)),
                    gs, (slice(None), slice(0, 4)), (mx, (slice(None), slice(1, 2))),
                    tag="gm")
            def emf(v):
                return v.tensor_copy(em.ap.rearrange("p (g two) -> p g two", two=2)[:, :, 0],
                                     gs.ap[:, 0:4])
            p.task("vector", emf, reads=[gs], writes=[em], tag="em0")
            def emf2(v):
                return v.tensor_copy(em.ap.rearrange("p (g two) -> p g two", two=2)[:, :, 1],
                                     gs.ap[:, 0:4])
            p.task("vector", emf2, reads=[gs], writes=[em], tag="em1")
            tt_task(p, "vector", ALU.mult, s4c, (slice(None), slice(0, 8)),
                    s4c, (slice(None), slice(0, 8)), em, (slice(None), slice(0, 8)),
                    tag="tmpsc")
            def m1e(v):
                return v.reduce_max(mx.ap[:, 0:1], s4c.ap[:, 0:8], axis=AX.X)
            p.task("vector", m1e, reads=[s4c], writes=[mx], tag="m1e")
            ts_task(p, "vector", ALU.is_equal, em, (slice(None), slice(0, 8)),
                    s4c, (slice(None), slice(0, 8)), (mx, (slice(None), slice(0, 1))),
                    tag="eq1e")
            stt_task(p, "vector", em, (slice(None), slice(0, 8)),
                     em, (slice(None), slice(0, 8)), -1.0e30,
                     s4c, (slice(None), slice(0, 8)), ALU.mult, ALU.add, tag="g2e")
            def m2e(v):
                return v.reduce_max(mx.ap[:, 1:2], em.ap[:, 0:8], axis=AX.X)
            p.task("vector", m2e, reads=[em], writes=[mx], tag="m2e")
            ts_task(p, "vector", ALU.is_ge, em, (slice(None), slice(0, 8)),
                    s4c, (slice(None), slice(0, 8)), (mx, (slice(None), slice(1, 2))),
                    tag="sel")
            tt_task(p, "vector", ALU.mult, cw_tm, (slice(None), slice(0, 8)),
                    em, (slice(None), slice(0, 8)), sc_tm, ssl, tag="wsel")
            def denf(v):
                return v.reduce_sum(mx.ap[:, 2:3], cw_tm.ap[:, 0:8], axis=AX.X)
            p.task("vector", denf, reads=[cw_tm], writes=[mx], tag="den")
            ts_task(p, "vector", ALU.add, mx, (slice(None), slice(2, 3)),
                    mx, (slice(None), slice(2, 3)), 1.0e-20, tag="dene")
            def recf(v):
                return v.reciprocal(mx.ap[:, 3:4], mx.ap[:, 2:3])
            p.task("vector", recf, reads=[mx], writes=[mx], tag="rden")
            ts_task(p, "vector", ALU.mult, cw_tm, (slice(None), slice(0, 8)),
                    cw_tm, (slice(None), slice(0, 8)), (mx, (slice(None), slice(3, 4))),
                    tag="cw")
            pst = pool.nxt()

            def trb(te, pst=pst):
                return te.transpose(pst.ap[0:8, 0:128], cw_tm.ap[:, 0:8],
                                    identf.ap[:, :])
            p.task("tensor", trb, reads=[cw_tm, identf], writes=[pst], tag="cwtr")
            copy_task(p, "vector", cw_fm, (slice(0, 8), slice(sub * 128, sub * 128 + 128)),
                      pst, (slice(0, 8), slice(0, 128)), tag="cwev")
        p.dma("scalar_dma", cw_dram.ap[:, t0:t0 + Tc], cw_fm.ap[0:8, 0:Tc],
              reads=[cw_fm], writes=[cw_dram], tag="cwst")
        p.task("vector", lambda v: v.memset(y_sb.ap[:], 0.0), writes=[y_sb], tag="y0")
        for e in range(8):
            p.dma("gpsimd_dma", wg_sb.ap[:],
                  gT.ap[e].rearrange("(c p) o -> p c o", p=128),
                  reads=[gT], writes=[wg_sb], tag="wgld")
            p.dma("gpsimd_dma", wu_sb.ap[:],
                  uT.ap[e].rearrange("(c p) o -> p c o", p=128),
                  reads=[uT], writes=[wu_sb], tag="wuld")
            p.dma("gpsimd_dma", wd_sb.ap[:],
                  dT.ap[e].rearrange("(c p) o -> p c o", p=128),
                  reads=[dT], writes=[wd_sb], tag="wdld")
            p.dma("sync", cwrep.ap[:],
                  cw_dram.ap[e:e + 1, t0:t0 + Tc].partition_broadcast(128),
                  reads=[cw_dram], writes=[cwrep], tag="cwrep")
            for oc in range(4):
                psg = pool.nxt(); psu = pool.nxt()

                def gmm(te, oc=oc, psg=psg):
                    last = None
                    for k in range(8):
                        last = te.matmul(psg.ap[:, 0:Tc],
                                         wg_sb.ap[:, k * 512 + oc * 128:k * 512 + oc * 128 + 128],
                                         xn_bf.ap[:, k * Tc:(k + 1) * Tc],
                                         start=(k == 0), stop=(k == 7))
                    return last
                p.task("tensor", gmm, reads=[wg_sb, xn_bf], writes=[psg], tag="gmm")

                def umm(te, oc=oc, psu=psu):
                    last = None
                    for k in range(8):
                        last = te.matmul(psu.ap[:, 0:Tc],
                                         wu_sb.ap[:, k * 512 + oc * 128:k * 512 + oc * 128 + 128],
                                         xn_bf.ap[:, k * Tc:(k + 1) * Tc],
                                         start=(k == 0), stop=(k == 7))
                    return last
                p.task("tensor", umm, reads=[wu_sb, xn_bf], writes=[psu], tag="umm")
                act_task(p, hg, (slice(None), slice(0, Tc)), psg,
                         (slice(None), slice(0, Tc)), AF.Silu, tag="hsilu")
                def hmul(v, oc=oc, psu=psu):
                    return v.tensor_mul(h_bf.ap[:, oc * Tc:(oc + 1) * Tc],
                                        hg.ap[:, 0:Tc], psu.ap[:, 0:Tc])
                p.task("vector", hmul, reads=[hg, psu], writes=[h_bf], tag="hmul")
            for oc2 in range(8):
                psd = pool.nxt()

                def dmm(te, oc2=oc2, psd=psd):
                    last = None
                    for k in range(4):
                        last = te.matmul(psd.ap[:, 0:Tc],
                                         wd_sb.ap[:, k * 1024 + oc2 * 128:k * 1024 + oc2 * 128 + 128],
                                         h_bf.ap[:, k * Tc:(k + 1) * Tc],
                                         start=(k == 0), stop=(k == 3))
                    return last
                p.task("tensor", dmm, reads=[wd_sb, h_bf], writes=[psd], tag="dmm")
                def ymul(v, psd=psd):
                    return v.tensor_mul(yt.ap[:, 0:Tc], psd.ap[:, 0:Tc],
                                        cwrep.ap[:, 0:Tc])
                p.task("vector", ymul, reads=[psd, cwrep], writes=[yt], tag="ymul")
                tt_task(p, "vector", ALU.add, y_sb, (slice(None), slice(oc2 * Tc, oc2 * Tc + Tc)),
                        y_sb, (slice(None), slice(oc2 * Tc, oc2 * Tc + Tc)),
                        yt, (slice(None), slice(0, Tc)), tag="yacc")
        p.dma("sync", sa_ch.ap[:],
              s_att.ap.rearrange("(c p) t -> p c t", p=128)[:, :, t0:t0 + Tc],
              reads=[s_att], writes=[sa_ch], tag="sald")
        for oc2 in range(8):
            ysl = (slice(None), slice(oc2 * Tc, oc2 * Tc + Tc))
            act_task(p, y_sb, ysl, y_sb, ysl, AF.Silu, tag="gsilu")
            tt_task(p, "vector", ALU.mult, y_sb, ysl, y_sb, ysl, sa_ch, ysl, tag="sg")
            tt_task(p, "vector", ALU.add, y_sb, ysl, y_sb, ysl, x_ch, ysl, tag="x2")
        p.dma("gpsimd_dma",
              o_x2.ap.rearrange("(c p) t -> p c t", p=128)[:, :, t0:t0 + Tc],
              y_sb.ap.rearrange("p (c t) -> p c t", t=Tc),
              reads=[y_sb], writes=[o_x2], tag="x2st")

    p.finalize()
    return nc



def emit_rms_stats(p, pool, ones_lhs, ones_col, src_bf16, Tc, rstd_out, c0,
                   scr, eps, n_f=8, use_mean=True, tag="rms"):
    D = n_f * 128
    p.const(-0.5)
    n_sc = ceil_div(Tc, 512)
    for sc_i in range(n_sc):
        w = min(512, Tc - sc_i * 512)
        cc = sc_i * 512
        ps_sq = pool.nxt()

        def mm_sq(te, cc=cc, w=w, ps=ps_sq):
            last = None
            for fc in range(n_f):
                last = te.matmul(ps.ap[0:1, 0:w], ones_lhs.ap[:, 0:1],
                                 src_bf16.ap[:, fc * Tc + cc: fc * Tc + cc + w],
                                 start=(fc == 0), stop=(fc == n_f - 1))
            return last
        p.task("tensor", mm_sq, reads=[src_bf16, ones_lhs], writes=[ps_sq],
               tag=tag + "_mm")
        mul = (1.0 / D) if use_mean else 1.0

        def st(v, cc=cc, w=w, ps=ps_sq):
            return v.tensor_scalar(rstd_out.ap[0:1, c0 + cc:c0 + cc + w],
                                   ps.ap[0:1, 0:w], mul, eps, ALU.mult, ALU.add)
        p.task("vector", st, reads=[ps_sq], writes=[rstd_out], tag=tag + "_m")

        def st2(sc_, cc=cc, w=w):
            return sc_.activation(rstd_out.ap[0:1, c0 + cc:c0 + cc + w],
                                  rstd_out.ap[0:1, c0 + cc:c0 + cc + w], AF.Ln)
        p.task("scalar", st2, reads=[rstd_out], writes=[rstd_out], tag=tag + "_ln")

        def st3(sc_, cc=cc, w=w):
            return sc_.activation(rstd_out.ap[0:1, c0 + cc:c0 + cc + w],
                                  rstd_out.ap[0:1, c0 + cc:c0 + cc + w], AF.Exp,
                                  scale=-0.5)
        p.task("scalar", st3, reads=[rstd_out], writes=[rstd_out], tag=tag + "_ex")


def build_l3_d1():
    nc = bass.Bass("TRN2", target_bir_lowering=False, debug=False)
    I = lambda n, s: Buf(n, nc.dram_tensor(n, s, F32, kind="ExternalInput").ap(), "DRAM")
    x3 = I("x3", [1024, 2048])
    rin_w = I("rin_w", [128, 8]); rk_w = I("rk_w", [128, 8]); rv_w = I("rv_w", [128, 8])
    wqT = I("wqT", [1024, 1024]); wkT = I("wkT", [1024, 1024]); wvT = I("wvT", [1024, 1024])
    betaT = I("betaT", [1024, 16]); beta_b = I("beta_b", [1, 16])
    base_beta = I("base_beta", [1, 16])
    O = lambda n, s, dt=F32: Buf(n, nc.dram_tensor(n, s, dt, kind="ExternalOutput").ap(), "DRAM")
    olocT = O("olocT", [1024, 2048]); qg = O("qg", [1024, 2048])
    m1s = O("m1s", [16, 64, 1024])
    gl_out = O("gl_out", [1, 256])
    gam_o = O("gam_o", [16, 2048])
    denc_o = O("denc", [16, 2048]); cumb_o = O("cumb", [16, 2048])
    dsum = O("dsum", [16, 8])
    gam_d = Buf("gam_d", nc.dram_tensor("gam_d", [16, 2048], F32, kind="Internal").ap(), "DRAM")
    glr_d = Buf("glr_d", nc.dram_tensor("glr_d", [1, 256], F32, kind="Internal").ap(), "DRAM")
    q_d = Buf("q_d", nc.dram_tensor("q_d", [1024, 2048], F32, kind="Internal").ap(), "DRAM")
    k_d = Buf("k_d", nc.dram_tensor("k_d", [1024, 2048], F32, kind="Internal").ap(), "DRAM")
    v_d = Buf("v_d", nc.dram_tensor("v_d", [1024, 2048], F32, kind="Internal").ap(), "DRAM")

    p = Prog(nc)
    pool = PsumPool(p, n=8)
    ones_col = make_ones_col(p)
    ones_lhs = make_ones_lhs(p)
    identf = make_identity_f32(p)

    wq_sb, _ = load_w_nat(p, wqT, 1024, 1024, tag="wq")
    wk_sb, _ = load_w_nat(p, wkT, 1024, 1024, tag="wk")
    wv_sb, _ = load_w_nat(p, wvT, 1024, 1024, tag="wv")
    wb_sb, _ = load_w_nat(p, betaT, 1024, 16, tag="wb")
    rin_sb = p.sb("rin_sb", [128, 8], F32)
    p.dma("sync", rin_sb.ap[:], rin_w.ap[:, :], reads=[rin_w], writes=[rin_sb])
    rk_sb = p.sb("rk_sb", [128, 8], F32)
    p.dma("sync", rk_sb.ap[:], rk_w.ap[:, :], reads=[rk_w], writes=[rk_sb])
    rv_sb = p.sb("rv_sb", [128, 8], F32)
    p.dma("sync", rv_sb.ap[:], rv_w.ap[:, :], reads=[rv_w], writes=[rv_sb])

    Tc = 512
    scrl = ln_scratch(p, Tc)
    x_ch = p.sb("x_ch", [128, 8 * Tc], F32)
    xnb = p.sb("xnb", [128, 8 * Tc], BF16)
    xsq = scrl["x2"]
    xn3 = x_ch
    xn_bf = p.sb("xn_bf", [128, 8 * Tc], BF16)
    kv_bf = xn_bf
    rst = scrl["stat"]
    rrep = p.sb("rrep", [128, Tc], F32)
    tmp = p.sb("tmp", [128, Tc], F32)
    ev_bf = p.sb("ev_bf", [128, Tc], F32)
    beta = p.sb("beta", [16, 2048], F32)
    T1 = p.sb("T1", [16, 2048], F32)
    T2 = p.sb("T2", [16, 2048], F32)
    T3 = p.sb("T3", [16, 2048], F32)

    for tq in range(4):
        t0 = tq * Tc
        p.dma("sync", x_ch.ap[:],
              x3.ap.rearrange("(c p) t -> p c t", p=128)[:, :, t0:t0 + Tc],
              reads=[x3], writes=[x_ch], tag="xld")
        emit_ln_full(p, pool, ones_lhs, ones_col, x_ch, Tc, xnb, None, scrl)
        for fc in range(8):
            sl = (slice(None), slice(fc * Tc, (fc + 1) * Tc))
            act_task(p, xsq, sl, xnb, sl, AF.Square, tag="xsq")
        emit_rms_stats(p, pool, ones_lhs, ones_col, xsq, Tc, rst, 0,
                       None, 1e-6, use_mean=True, tag="rms1")
        rbf = p.sb("rbf", [1, Tc], BF16) if tq == 0 else rbf
        copy_task(p, "vector", rbf, (slice(0, 1), slice(0, Tc)),
                  rst, (slice(0, 1), slice(0, Tc)), tag="rbf")
        for sc_i in range(Tc // 512):
            emit_bcast_row(p, pool, ones_col, rbf.ap[0:1, sc_i * 512:sc_i * 512 + 512],
                           rbf, rrep, sc_i * 512, 512)
        for fc in range(8):
            sl = (slice(None), slice(fc * Tc, (fc + 1) * Tc))
            tt_task(p, "vector", ALU.mult, xn3, sl, xnb, sl, rrep,
                    (slice(None), slice(0, Tc)), tag="xn3a")
            ts_task(p, "vector", ALU.mult, xn3, sl, xn3, sl,
                    (rin_sb, (slice(None), slice(fc, fc + 1))), tag="xn3b")
            copy_task(p, "vector", xn_bf, sl, xn3, sl, tag="xnbf")
            act_task(p, xsq, sl, xn3, sl, AF.Square, tag="xn3sq")
        emit_rms_stats(p, pool, ones_lhs, ones_col, xsq, Tc, rst, Tc,
                       None, 1e-6, use_mean=True, tag="rms2")
        copy_task(p, "vector", rbf, (slice(0, 1), slice(0, Tc)),
                  rst, (slice(0, 1), slice(Tc, 2 * Tc)), tag="rbf2")
        for sc_i in range(Tc // 512):
            emit_bcast_row(p, pool, ones_col, rbf.ap[0:1, sc_i * 512:sc_i * 512 + 512],
                           rbf, rrep, sc_i * 512, 512)
        for oc in range(8):
            psb = pool.nxt()

            def qmm(te, oc=oc, psb=psb):
                last = None
                for k in range(8):
                    last = te.matmul(psb.ap[:, 0:Tc],
                                     wq_sb.ap[:, k * 1024 + oc * 128:k * 1024 + oc * 128 + 128],
                                     xn_bf.ap[:, k * Tc:(k + 1) * Tc],
                                     start=(k == 0), stop=(k == 7))
                return last
            p.task("tensor", qmm, reads=[wq_sb, xn_bf], writes=[psb], tag="qmm")
            copy_task(p, "vector", ev_bf, (slice(None), slice(0, Tc)),
                      psb, (slice(None), slice(0, Tc)), tag="qev")
            p.dma("gpsimd_dma", q_d.ap[oc * 128:(oc + 1) * 128, t0:t0 + Tc],
                  ev_bf.ap[:, 0:Tc], reads=[ev_bf], writes=[q_d], tag="qst")
        psb = pool.nxt()

        def bmm(te, psb=psb):
            last = None
            for k in range(8):
                last = te.matmul(psb.ap[0:16, 0:Tc],
                                 wb_sb.ap[:, k * 16:(k + 1) * 16],
                                 xn_bf.ap[:, k * Tc:(k + 1) * Tc],
                                 start=(k == 0), stop=(k == 7))
            return last
        p.task("tensor", bmm, reads=[wb_sb, xn_bf], writes=[psb], tag="bmm")
        copy_task(p, "vector", beta, (slice(0, 16), slice(t0, t0 + Tc)),
                  psb, (slice(0, 16), slice(0, Tc)), tag="bev")
        for which, wsb, rw, dstd in (("k", wk_sb, rk_sb, k_d), ("v", wv_sb, rv_sb, v_d)):
            for fc in range(8):
                sl = (slice(None), slice(fc * Tc, (fc + 1) * Tc))
                tt_task(p, "vector", ALU.mult, tmp, (slice(None), slice(0, Tc)),
                        xn3, sl, rrep, (slice(None), slice(0, Tc)), tag="kv1")
                ts_task(p, "vector", ALU.mult, kv_bf, sl, tmp, (slice(None), slice(0, Tc)),
                        (rw, (slice(None), slice(fc, fc + 1))), tag="kv2")
            for oc in range(8):
                psb = pool.nxt()

                def kmm(te, oc=oc, psb=psb, wsb=wsb):
                    last = None
                    for k in range(8):
                        last = te.matmul(psb.ap[:, 0:Tc],
                                         wsb.ap[:, k * 1024 + oc * 128:k * 1024 + oc * 128 + 128],
                                         kv_bf.ap[:, k * Tc:(k + 1) * Tc],
                                         start=(k == 0), stop=(k == 7))
                    return last
                p.task("tensor", kmm, reads=[wsb, kv_bf], writes=[psb], tag="kmm")
                copy_task(p, "vector", ev_bf, (slice(None), slice(0, Tc)),
                          psb, (slice(None), slice(0, Tc)), tag="kev")
                p.dma("gpsimd_dma", dstd.ap[oc * 128:(oc + 1) * 128, t0:t0 + Tc],
                      ev_bf.ap[:, 0:Tc], reads=[ev_bf], writes=[dstd], tag="kst")

    bbcol = p.sb("bbcol", [16, 1], F32)
    p.dma("sync", bbcol.ap[:], beta_b.ap.rearrange("o (e x) -> e (o x)", x=1),
          reads=[beta_b], writes=[bbcol], tag="bbcol", allow_slow_non_contiguous=True)
    bscol = p.sb("bscol", [16, 1], F32)
    p.dma("sync", bscol.ap[:], base_beta.ap.rearrange("o (e x) -> e (o x)", x=1),
          reads=[base_beta], writes=[bscol], tag="bscol", allow_slow_non_contiguous=True)
    act_task(p, bscol, (slice(0, 16), slice(0, 1)), bscol, (slice(0, 16), slice(0, 1)),
             AF.Exp, scale=-1.0, tag="bsc1")
    ts_task(p, "vector", ALU.add, bscol, (slice(0, 16), slice(0, 1)),
            bscol, (slice(0, 16), slice(0, 1)), 1.0, tag="bsc2")

    def bscr(v):
        return v.reciprocal(bscol.ap[0:16, 0:1], bscol.ap[0:16, 0:1])
    p.task("vector", bscr, reads=[bscol], writes=[bscol], tag="bsc3")
    bsl = (slice(0, 16), slice(None))
    ts_task(p, "vector", ALU.add, beta, bsl, beta, bsl,
            (bbcol, (slice(0, 16), slice(0, 1))), tag="badd")
    act_task(p, beta, bsl, beta, bsl, AF.Exp, scale=-1.0, tag="bsig1")
    ts_task(p, "vector", ALU.add, beta, bsl, beta, bsl, 1.0, tag="bsig2")

    def brec(v):
        return v.reciprocal(beta.ap[bsl], beta.ap[bsl])
    p.task("vector", brec, reads=[beta], writes=[beta], tag="bsig3")
    ts_task(p, "vector", ALU.mult, beta, bsl, beta, bsl,
            (bscol, (slice(0, 16), slice(0, 1))), tag="bbase")
    ts_task(p, "vector", ALU.max, beta, bsl, beta, bsl, 0.9, tag="bclip1")
    ts_task(p, "vector", ALU.min, beta, bsl, beta, bsl, 0.9995, tag="bclip2")

    act_task(p, T1, bsl, beta, bsl, AF.Ln, tag="lgb")
    p.task("vector", lambda v: v.memset(T2.ap[bsl], 1.0), writes=[T2])
    for c0m in range(0, 2048, 128):
        p.task("vector", lambda v, c0m=c0m: v.memset(T2.ap[0:16, c0m:c0m + 1], 0.0),
               writes=[T2])

    def gsc(v):
        return v.tensor_tensor_scan(T3.ap[bsl], T2.ap[bsl], T1.ap[bsl], 0.0,
                                    ALU.mult, ALU.add)
    p.task("vector", gsc, reads=[T2, T1], writes=[T3], tag="gsc")
    p.dma("scalar_dma", gam_d.ap[:, :], T3.ap[bsl], reads=[T3], writes=[gam_d])
    p.dma("gpsimd_dma", gam_o.ap[:, :], T3.ap[bsl], reads=[T3], writes=[gam_o])
    src_gl = gam_d.ap.rearrange("h (c l) -> c h l", l=128)[:, :, 127:128]
    p.dma("sync", glr_d.ap.rearrange("o (c h) -> c h o", h=16), src_gl,
          reads=[gam_d], writes=[glr_d], tag="glgather",
          allow_slow_non_contiguous=True)
    p.dma("sync", gl_out.ap.rearrange("o (c h) -> c h o", h=16), src_gl,
          reads=[gam_d], writes=[gl_out], tag="glout",
          allow_slow_non_contiguous=True)
    p.task("vector", lambda v: v.memset(T2.ap[bsl], 1.0), writes=[T2])
    for c0m in (0, 1024):
        p.task("vector", lambda v, c0m=c0m: v.memset(T2.ap[0:16, c0m:c0m + 1], 0.0),
               writes=[T2])
    tt_task(p, "vector", ALU.mult, T1, bsl, beta, bsl, T2, bsl, tag="bmask")
    p.task("vector", lambda v: v.memset(T2.ap[bsl], 1.0), writes=[T2])

    def dsc(v):
        return v.tensor_tensor_scan(T3.ap[bsl], T1.ap[bsl], T2.ap[bsl], 0.0,
                                    ALU.mult, ALU.add)
    p.task("vector", dsc, reads=[T1, T2], writes=[T3], tag="dsc")
    p.dma("gpsimd_dma", denc_o.ap[:, :], T3.ap[bsl], reads=[T3], writes=[denc_o])
    dsb = p.sb("dsb", [16, 8], F32)
    copy_task(p, "vector", dsb, (slice(0, 16), slice(0, 1)),
              T3, (slice(0, 16), slice(1023, 1024)), tag="dsb0")
    copy_task(p, "vector", dsb, (slice(0, 16), slice(1, 2)),
              T3, (slice(0, 16), slice(2047, 2048)), tag="dsb1")
    def bs2(v):
        return v.tensor_sub(T2.ap[bsl], beta.ap[bsl], T1.ap[bsl])
    p.task("vector", bs2, reads=[beta, T1], writes=[T2], tag="bstart")

    def csc(v):
        return v.tensor_tensor_scan(T3.ap[bsl], T1.ap[bsl], T2.ap[bsl], 0.0,
                                    ALU.mult, ALU.add)
    p.task("vector", csc, reads=[T1, T2], writes=[T3], tag="csc")
    p.dma("gpsimd_dma", cumb_o.ap[:, :], T3.ap[bsl], reads=[T3], writes=[cumb_o])
    copy_task(p, "vector", dsb, (slice(0, 16), slice(2, 3)),
              T3, (slice(0, 16), slice(1023, 1024)), tag="dsb2")
    copy_task(p, "vector", dsb, (slice(0, 16), slice(3, 4)),
              T3, (slice(0, 16), slice(2047, 2048)), tag="dsb3")
    p.task("vector", lambda v: v.memset(dsb.ap[0:16, 4:8], 0.0), writes=[dsb])
    p.dma("gpsimd_dma", dsum.ap[:, :], dsb.ap[0:16, 0:8], reads=[dsb], writes=[dsum])

    qblk = p.sb("qblk", [64, 128], F32)
    kblk = p.sb("kblk", [64, 128], F32)
    vblk = p.sb("vblk", [64, 128], F32)
    ktm = p.sb("ktm", [128, 64], F32)
    vtm = p.sb("vtm", [128, 64], F32)
    gcol = p.sb("gcol", [128, 1], F32)
    glrep_c = p.sb("glrep_c", [128, 16], F32)
    gtrep = p.sb("gtrep", [128, 128], F32)
    lam = p.sb("lam", [128, 128], F32)
    pp_bf = p.sb("pp_bf", [128, 128], F32)
    wcoef = p.sb("wcoef", [128, 1], F32)
    ktw = p.sb("ktw", [128, 64], F32)
    m1strip = p.sb("m1strip", [64, 16 * 64], F32)
    egt = p.sb("egt", [64, 128], F32)
    qgb = p.sb("qgb", [64, 128], F32)
    olb = p.sb("olb", [64, 128], F32)
    triu = p.sb("triu", [128, 128], F32)
    trones = p.sb("trones", [128, 128], F32)
    p.task("vector", lambda v: v.memset(trones.ap[:], 1.0), writes=[trones])

    def trsel(g):
        return g.affine_select(triu.ap[:], trones.ap[:], pattern=[[1, 128]],
                               compare_op=ALU.is_ge, fill=0.0, base=0,
                               channel_multiplier=-1)
    p.task("gpsimd", trsel, reads=[trones], writes=[triu], tag="triu")

    for c in range(16):
        c0 = c * 128
        p.dma("sync", glrep_c.ap[:],
              glr_d.ap[0:1, c * 16:(c + 1) * 16].partition_broadcast(128),
              reads=[glr_d], writes=[glrep_c], tag="glrld")
        for h in range(16):
            fc, off = h // 2, (h % 2) * 64
            r0 = fc * 128 + off
            p.dma("sync", gcol.ap[:],
                  gam_d.ap[h:h + 1, c0:c0 + 128].rearrange("o (t x) -> t (o x)", x=1),
                  reads=[gam_d], writes=[gcol], tag="gcol",
                  allow_slow_non_contiguous=True)
            p.dma("scalar_dma", gtrep.ap[:],
                  gam_d.ap[h:h + 1, c0:c0 + 128].partition_broadcast(128),
                  reads=[gam_d], writes=[gtrep], tag="gtrep")
            p.dma("sync", qblk.ap[0:64, :], q_d.ap[r0:r0 + 64, c0:c0 + 128],
                  reads=[q_d], writes=[qblk], tag="qbld")
            p.dma("sync", kblk.ap[0:64, :], k_d.ap[r0:r0 + 64, c0:c0 + 128],
                  reads=[k_d], writes=[kblk], tag="kbld")
            p.dma("sync", vblk.ap[0:64, :], v_d.ap[r0:r0 + 64, c0:c0 + 128],
                  reads=[v_d], writes=[vblk], tag="vbld")
            for src_, dst in ((kblk, ktm), (vblk, vtm)):
                pst = pool.nxt()

                def trf(te, src_=src_, pst=pst):
                    return te.transpose(pst.ap[0:128, 0:64],
                                        src_.ap[0:64, 0:128],
                                        identf.ap[0:64, 0:64])
                p.task("tensor", trf, reads=[src_, identf], writes=[pst], tag="ktr")

                def tev(v, dst=dst, pst=pst):
                    return v.tensor_copy(dst.ap[:, 0:64], pst.ap[0:128, 0:64])
                p.task("vector", tev, reads=[pst], writes=[dst], tag="ktev")
            psb = pool.nxt()

            def kqm(te, psb=psb):
                return te.matmul(psb.ap[:, 0:128], kblk.ap[0:64, :], qblk.ap[0:64, :],
                                 start=True, stop=True)
            p.task("tensor", kqm, reads=[kblk, qblk], writes=[psb], tag="kqm")
            ts_task(p, "vector", ALU.subtract, lam, (slice(None), slice(0, 128)),
                    gtrep, (slice(None), slice(0, 128)),
                    (gcol, (slice(None), slice(0, 1))), tag="lam1")
            act_task(p, lam, (slice(None), slice(0, 128)), lam,
                     (slice(None), slice(0, 128)), AF.Exp, tag="lam2")
            tt_task(p, "vector", ALU.mult, lam, (slice(None), slice(0, 128)),
                    lam, (slice(None), slice(0, 128)), triu, (slice(None), slice(0, 128)),
                    tag="lam3")

            def ppm(v, psb=psb):
                return v.tensor_mul(pp_bf.ap[:, 0:128], lam.ap[:, 0:128],
                                    psb.ap[:, 0:128])
            p.task("vector", ppm, reads=[lam, psb], writes=[pp_bf], tag="ppm")
            pso = pool.nxt()

            def olm(te, pso=pso):
                return te.matmul(pso.ap[0:64, 0:128], vtm.ap[:, 0:64],
                                 pp_bf.ap[:, 0:128], start=True, stop=True)
            p.task("tensor", olm, reads=[vtm, pp_bf], writes=[pso], tag="olm")
            copy_task(p, "vector", olb, (slice(0, 64), slice(0, 128)),
                      pso, (slice(0, 64), slice(0, 128)), tag="olev")
            p.dma("gpsimd_dma", olocT.ap[r0:r0 + 64, c0:c0 + 128],
                  olb.ap[0:64, 0:128], reads=[olb], writes=[olocT], tag="olst")
            ts_task(p, "vector", ALU.subtract, wcoef, (slice(None), slice(0, 1)),
                    gcol, (slice(None), slice(0, 1)),
                    (glrep_c, (slice(None), slice(h, h + 1))), tag="wc1")
            act_task(p, wcoef, (slice(None), slice(0, 1)), wcoef,
                     (slice(None), slice(0, 1)), AF.Exp, scale=-1.0, tag="wc2")
            ts_task(p, "vector", ALU.mult, ktw, (slice(None), slice(0, 64)),
                    ktm, (slice(None), slice(0, 64)), (wcoef, (slice(None), slice(0, 1))),
                    tag="ktw")
            psm = pool.nxt()

            def m1m(te, psm=psm):
                return te.matmul(psm.ap[0:64, 0:64], ktw.ap[:, 0:64], vtm.ap[:, 0:64],
                                 start=True, stop=True)
            p.task("tensor", m1m, reads=[ktw, vtm], writes=[psm], tag="m1m")
            copy_task(p, "vector", m1strip, (slice(0, 64), slice(h * 64, h * 64 + 64)),
                      psm, (slice(0, 64), slice(0, 64)), tag="m1ev")
            act_task(p, egt, (slice(0, 64), slice(0, 128)), gtrep,
                     (slice(0, 64), slice(0, 128)), AF.Exp, tag="egt")
            tt_task(p, "vector", ALU.mult, qgb, (slice(0, 64), slice(0, 128)),
                    qblk, (slice(0, 64), slice(0, 128)), egt, (slice(0, 64), slice(0, 128)),
                    tag="qgb")
            p.dma("scalar_dma", qg.ap[r0:r0 + 64, c0:c0 + 128],
                  qgb.ap[0:64, 0:128], reads=[qgb], writes=[qg], tag="qgst")
        p.dma("gpsimd_dma", m1s.ap[c, :, :], m1strip.ap[0:64, :],
              reads=[m1strip], writes=[m1s], tag="m1st")

    p.finalize()
    return nc


def build_l3_d2a():
    nc = bass.Bass("TRN2", target_bir_lowering=False, debug=False)
    I = lambda n, s: Buf(n, nc.dram_tensor(n, s, F32, kind="ExternalInput").ap(), "DRAM")
    m1s_all = I("m1s_all", [64, 64, 1024]); gl_all = I("gl_all", [1, 1024])
    den_ends = I("den_ends", [16, 8]); cumb_ends = I("cumb_ends", [16, 8])
    O = lambda n, s: Buf(n, nc.dram_tensor(n, s, F32, kind="ExternalOutput").ap(), "DRAM")
    s_ins = O("s_ins", [64, 64, 1024]); den_ins = O("den_ins", [16, 8])

    p = Prog(nc)
    S = p.sb("S", [64, 16 * 64], F32)
    m1c = p.sb("m1c", [64, 16 * 64], F32)
    egl = p.sb("egl", [128, 16], F32)
    p.task("vector", lambda v: v.memset(S.ap[0:64, :], 0.0), writes=[S])
    for g in range(64):
        p.dma("gpsimd_dma", s_ins.ap[g, :, :], S.ap[0:64, :], reads=[S],
              writes=[s_ins], tag="sst")
        p.dma("sync", egl.ap[:],
              gl_all.ap[0:1, g * 16:(g + 1) * 16].partition_broadcast(128),
              reads=[gl_all], writes=[egl], tag="eglld")
        act_task(p, egl, (slice(None), slice(None)), egl, (slice(None), slice(None)),
                 AF.Exp, tag="eglx")
        p.dma("sync", m1c.ap[0:64, :], m1s_all.ap[g, :, :], reads=[m1s_all],
              writes=[m1c], tag="m1ld")
        for h in range(16):
            stt_task(p, "vector", S, (slice(0, 64), slice(h * 64, h * 64 + 64)),
                     S, (slice(0, 64), slice(h * 64, h * 64 + 64)),
                     (egl, (slice(0, 64), slice(h, h + 1))),
                     m1c, (slice(0, 64), slice(h * 64, h * 64 + 64)),
                     ALU.mult, ALU.add, tag="supd")
    de = p.sb("de", [16, 8], F32); ce = p.sb("ce", [16, 8], F32)
    di = p.sb("di", [16, 9], F32)
    p.dma("sync", de.ap[:], den_ends.ap[:, :], reads=[den_ends], writes=[de])
    p.dma("sync", ce.ap[:], cumb_ends.ap[:, :], reads=[cumb_ends], writes=[ce])
    p.task("vector", lambda v: v.memset(di.ap[0:16, 0:1], 0.0), writes=[di])
    for b in range(8):
        def f(v, b=b):
            return v.scalar_tensor_tensor(di.ap[0:16, b + 1:b + 2],
                                          ce.ap[0:16, b:b + 1] if False else di.ap[0:16, b:b + 1],
                                          1.0, de.ap[0:16, b:b + 1], ALU.mult, ALU.add)
        def f2(v, b=b):
            return v.scalar_tensor_tensor(di.ap[0:16, b + 1:b + 2],
                                          di.ap[0:16, b:b + 1],
                                          ce.ap[0:16, b:b + 1],
                                          de.ap[0:16, b:b + 1], ALU.mult, ALU.add)
        p.task("vector", f2, reads=[di, ce, de], writes=[di], tag="dpre")
    copy_task(p, "vector", de, (slice(0, 16), slice(0, 8)),
              di, (slice(0, 16), slice(0, 8)), tag="dicp")
    p.dma("gpsimd_dma", den_ins.ap[:, :], de.ap[0:16, 0:8], reads=[de], writes=[den_ins])
    p.finalize()
    return nc


def emit_ln_full(p, pool, ones_lhs, ones_col, x_buf, Tc, out_bf, out_f32, scr,
                 eps=LN_EPS_DEFAULT, n_f=8, gain=None, bias_add=None):
    emit_ln_chunk(p, pool, ones_lhs, ones_col, x_buf, Tc,
                  out_bf if out_f32 is None else out_f32, scr, eps=eps, n_f=n_f,
                  gain=gain)
    if bias_add is not None:
        gbuf, gcol = bias_add
        tgt = out_bf if out_f32 is None else out_f32
        for fc in range(n_f):
            sl = (slice(None), slice(fc * Tc, (fc + 1) * Tc))
            ts_task(p, "vector", ALU.add, tgt, sl, tgt, sl,
                    (gbuf, (slice(None), slice(gcol + fc, gcol + fc + 1))), tag="lnb")
    if out_f32 is not None and out_bf is not None:
        for fc in range(n_f):
            sl = (slice(None), slice(fc * Tc, (fc + 1) * Tc))
            copy_task(p, "vector", out_bf, sl, out_f32, sl, tag="lnbf")


def build_l3_d2b():
    nc = bass.Bass("TRN2", target_bir_lowering=False, debug=False)
    I = lambda n, s, dt=F32: Buf(n, nc.dram_tensor(n, s, dt, kind="ExternalInput").ap(), "DRAM")
    olocT = I("olocT", [1024, 2048])
    qg = I("qg", [1024, 2048]); s_ins = I("s_ins", [16, 64, 1024])
    denc = I("denc", [16, 2048]); cumb = I("cumb", [16, 2048])
    den_in_own = I("den_in_own", [16, 2])
    o_o = Buf("o_o", nc.dram_tensor("o_o", [1024, 2048], F32, kind="ExternalOutput").ap(), "DRAM")
    rden_d = Buf("rden_d", nc.dram_tensor("rden_d", [16, 2048], F32, kind="Internal").ap(), "DRAM")

    p = Prog(nc)
    pool = PsumPool(p, n=8)
    sin_sb = p.sb("sin_sb", [64, 1024], F32)
    qgblk = p.sb("qgblk", [64, 128], F32)
    ol_ch = p.sb("ol_ch", [64, 128], F32)
    rd_rep = p.sb("rd_rep", [64, 128], F32)
    ot = p.sb("ot", [64, 128], F32)
    ob = p.sb("ob", [64, 128], F32)
    dfull = p.sb("dfull", [16, 2048], F32)
    dio = p.sb("dio", [16, 2], F32)
    cb_sb = p.sb("cb_sb", [16, 2048], F32)
    p.dma("sync", dfull.ap[0:16, :], denc.ap[:, :], reads=[denc], writes=[dfull])
    p.dma("sync", cb_sb.ap[0:16, :], cumb.ap[:, :], reads=[cumb], writes=[cb_sb])
    p.dma("sync", dio.ap[0:16, :], den_in_own.ap[:, :], reads=[den_in_own], writes=[dio])
    for blk in range(2):
        sl = (slice(0, 16), slice(blk * 1024, (blk + 1) * 1024))

        def f(v, sl=sl, blk=blk):
            return v.scalar_tensor_tensor(dfull.ap[sl], cb_sb.ap[sl],
                                          dio.ap[0:16, blk:blk + 1],
                                          dfull.ap[sl], ALU.mult, ALU.add)
        p.task("vector", f, reads=[cb_sb, dio, dfull], writes=[dfull], tag="dfull")
    ts_task(p, "vector", ALU.add, dfull, (slice(0, 16), slice(None)),
            dfull, (slice(0, 16), slice(None)), 1.0e-6, tag="deps")

    def drec(v):
        return v.reciprocal(dfull.ap[0:16, :], dfull.ap[0:16, :])
    p.task("vector", drec, reads=[dfull], writes=[dfull], tag="drec")
    p.dma("scalar_dma", rden_d.ap[:, :], dfull.ap[0:16, :], reads=[dfull], writes=[rden_d])

    for c in range(16):
        c0 = c * 128
        p.dma("gpsimd_dma", sin_sb.ap[0:64, :], s_ins.ap[c, :, :],
              reads=[s_ins], writes=[sin_sb], tag="sinld")
        for h in range(16):
            fc, off = h // 2, (h % 2) * 64
            r0 = fc * 128 + off
            psb = pool.nxt()
            p.dma("sync", qgblk.ap[0:64, 0:128], qg.ap[r0:r0 + 64, c0:c0 + 128],
                  reads=[qg], writes=[qgblk], tag="qgld")

            def im(te, h=h, psb=psb):
                return te.matmul(psb.ap[0:64, 0:128], sin_sb.ap[0:64, h * 64:(h + 1) * 64],
                                 qgblk.ap[0:64, 0:128], start=True, stop=True)
            p.task("tensor", im, reads=[sin_sb, qgblk], writes=[psb], tag="ointer")
            p.dma("sync", ol_ch.ap[0:64, 0:128],
                  olocT.ap[r0:r0 + 64, c0:c0 + 128],
                  reads=[olocT], writes=[ol_ch], tag="olld")
            p.dma("scalar_dma", rd_rep.ap[0:64, :],
                  rden_d.ap[h:h + 1, c0:c0 + 128].partition_broadcast(64),
                  reads=[rden_d], writes=[rd_rep], tag="rdrep")

            def cmb(v, psb=psb):
                return v.tensor_add(ot.ap[0:64, 0:128], psb.ap[0:64, 0:128],
                                    ol_ch.ap[0:64, 0:128])
            p.task("vector", cmb, reads=[psb, ol_ch], writes=[ot], tag="ocmb")

            def nrm(v):
                return v.tensor_mul(ob.ap[0:64, 0:128], ot.ap[0:64, 0:128],
                                    rd_rep.ap[0:64, 0:128])
            p.task("vector", nrm, reads=[ot, rd_rep], writes=[ob], tag="onrm")
            p.dma("gpsimd_dma", o_o.ap[r0:r0 + 64, c0:c0 + 128], ob.ap[0:64, :],
                  reads=[ob], writes=[o_o], tag="ost")

    p.finalize()
    return nc


def build_l3_d2c():
    nc = bass.Bass("TRN2", target_bir_lowering=False, debug=False)
    I = lambda n, s, dt=F32: Buf(n, nc.dram_tensor(n, s, dt, kind="ExternalInput").ap(), "DRAM")
    x3 = I("x3", [1024, 2048]); o_i = I("o_i", [1024, 2048])
    WoT = I("WoT", [1024, 1024]); bo128 = I("bo128", [128, 8])
    lnw = I("lnw", [128, 8]); lnb = I("lnb", [128, 8])
    basewT = I("basewT", [1024, 1024]); baseb = I("baseb", [128, 8])
    A_all = I("A_all", [1024, 128]); B_all = I("B_all", [128, 1024])
    routerT = I("routerT", [1024, 8]); router_b = I("router_b", [1, 8])
    o_x4 = Buf("o_x4", nc.dram_tensor("o_x4", [1024, 2048], F32, kind="ExternalOutput").ap(), "DRAM")
    wfm_d = Buf("wfm_d", nc.dram_tensor("wfm_d", [8, 2048], F32, kind="Internal").ap(), "DRAM")

    p = Prog(nc)
    pool = PsumPool(p, n=8)
    ones_col = make_ones_col(p)
    ones_lhs = make_ones_lhs(p)
    identf = make_identity_f32(p)

    woh, wol = load_w_hilo(p, WoT, 1024, 1024, tag="wo")
    bw_sb, _ = load_w_nat(p, basewT, 1024, 1024, tag="bw")
    aa_sb, _ = load_w_nat(p, A_all, 1024, 128, tag="aa")
    ba_sb, _ = load_w_nat(p, B_all, 128, 1024, tag="ba")
    rt_sb, _ = load_w_nat(p, routerT, 1024, 8, tag="rt")
    sml = {}
    for nm, buf in (("bo", bo128), ("lnw", lnw), ("lnb", lnb), ("bb", baseb)):
        sml[nm] = p.sb(nm, [128, 8], F32)
        p.dma("sync", sml[nm].ap[:], buf.ap[:, :], reads=[buf], writes=[sml[nm]])
    rb_rep = p.sb("rb_rep", [128, 8], F32)
    p.dma("sync", rb_rep.ap[:], router_b.ap[0:1, :].partition_broadcast(128),
          reads=[router_b], writes=[rb_rep])

    Tc = 512
    scr = ln_scratch(p, Tc)
    x_ch = p.sb("x_ch", [128, 8 * Tc], F32)
    o_ch = p.sb("o_ch", [128, 8 * Tc], F32)
    ohi = p.sb("ohi", [128, 8 * Tc], BF16)
    olo = p.sb("olo", [128, 8 * Tc], BF16)
    xn_f = p.sb("xn_f", [128, 8 * Tc], F32)
    xn_bf = p.sb("xn_bf", [128, 8 * Tc], BF16)
    y_ch = p.sb("y_ch", [128, 8 * Tc], F32)
    s3_ch = p.sb("s3_ch", [128, 8 * Tc], F32)
    lg = p.sb("lg", [8, Tc], F32)
    lg_tm = p.sb("lg_tm", [128, 4 * 8], F32)
    mx = p.sb("mx", [128, 4], F32)
    sel = p.sb("sel", [128, 8], F32)
    w_tm = p.sb("w_tm", [128, 8], F32)
    w_fm = p.sb("w_fm", [8, Tc], F32)
    wrep = p.sb("wrep", [128, Tc], F32)
    axw = p.sb("axw", [128, Tc], BF16)
    t1 = p.sb("t1", [128, Tc], F32)
    g3 = y_ch

    for tq in range(4):
        t0 = tq * Tc
        p.dma("sync", x_ch.ap[:],
              x3.ap.rearrange("(c p) t -> p c t", p=128)[:, :, t0:t0 + Tc],
              reads=[x3], writes=[x_ch], tag="xld")
        p.dma("scalar_dma", o_ch.ap[:],
              o_i.ap.rearrange("(c p) t -> p c t", p=128)[:, :, t0:t0 + Tc],
              reads=[o_i], writes=[o_ch], tag="old")
        for fc in range(8):
            sl = (slice(None), slice(fc * Tc, (fc + 1) * Tc))
            emit_hilo_split(p, o_ch, sl, ohi, olo, sl, tag="ohl")
        emit_ln_full(p, pool, ones_lhs, ones_col, x_ch, Tc, xn_bf, xn_f, scr)
        for oc in range(8):
            psb = pool.nxt()
            emit_mm_hilo(p, pool, woh, wol, ohi, olo, 0, 0, oc, Tc, 8, 1024, Tc, psb)
            act_task(p, t1, (slice(None), slice(0, Tc)), psb,
                     (slice(None), slice(0, Tc)), AF.Identity,
                     bias=(sml["bo"], (slice(None), slice(oc, oc + 1))), tag="woev")
            tt_task(p, "vector", ALU.add, y_ch, (slice(None), slice(oc * Tc, oc * Tc + Tc)),
                    t1, (slice(None), slice(0, Tc)),
                    xn_f, (slice(None), slice(oc * Tc, oc * Tc + Tc)), tag="yres")
        emit_ln_full(p, pool, ones_lhs, ones_col, y_ch, Tc, None, s3_ch, scr,
                     gain=(sml["lnw"], 0), bias_add=(sml["lnb"], 0))
        psb = pool.nxt()

        def rmm(te, psb=psb):
            last = None
            for k in range(8):
                last = te.matmul(psb.ap[0:8, 0:Tc], rt_sb.ap[:, k * 8:(k + 1) * 8],
                                 xn_bf.ap[:, k * Tc:(k + 1) * Tc],
                                 start=(k == 0), stop=(k == 7))
            return last
        p.task("tensor", rmm, reads=[rt_sb, xn_bf], writes=[psb], tag="rmm")
        copy_task(p, "vector", lg, (slice(0, 8), slice(0, Tc)),
                  psb, (slice(0, 8), slice(0, Tc)), tag="rgev")
        for sub in range(4):
            pst = pool.nxt()

            def trf(te, sub=sub, pst=pst):
                return te.transpose(pst.ap[0:128, 0:8],
                                    lg.ap[0:8, sub * 128:(sub + 1) * 128],
                                    identf.ap[0:8, 0:8])
            p.task("tensor", trf, reads=[lg, identf], writes=[pst], tag="rtr")
            copy_task(p, "vector", lg_tm, (slice(None), slice(sub * 8, sub * 8 + 8)),
                      pst, (slice(None), slice(0, 8)), tag="rtev")
        for sub in range(4):
            ssl = (slice(None), slice(sub * 8, sub * 8 + 8))
            tt_task(p, "vector", ALU.add, lg_tm, ssl, lg_tm, ssl,
                    rb_rep, (slice(None), slice(0, 8)), tag="rbadd")

            def m1f(v, ssl=ssl):
                return v.reduce_max(mx.ap[:, 0:1], lg_tm.ap[ssl], axis=AX.X)
            p.task("vector", m1f, reads=[lg_tm], writes=[mx], tag="rm1")
            ts_task(p, "vector", ALU.is_equal, sel, (slice(None), slice(0, 8)),
                    lg_tm, ssl, (mx, (slice(None), slice(0, 1))), tag="req")
            stt_task(p, "vector", sel, (slice(None), slice(0, 8)),
                     sel, (slice(None), slice(0, 8)), -1.0e30,
                     lg_tm, ssl, ALU.mult, ALU.add, tag="rg2")

            def m2f(v):
                return v.reduce_max(mx.ap[:, 1:2], sel.ap[:, 0:8], axis=AX.X)
            p.task("vector", m2f, reads=[sel], writes=[mx], tag="rm2")
            ts_task(p, "vector", ALU.is_ge, sel, (slice(None), slice(0, 8)),
                    lg_tm, ssl, (mx, (slice(None), slice(1, 2))), tag="rsel")
            ts_task(p, "vector", ALU.subtract, w_tm, (slice(None), slice(0, 8)),
                    lg_tm, ssl, (mx, (slice(None), slice(0, 1))), tag="rsub")
            act_task(p, w_tm, (slice(None), slice(0, 8)), w_tm,
                     (slice(None), slice(0, 8)), AF.Exp, tag="rexp")
            tt_task(p, "vector", ALU.mult, w_tm, (slice(None), slice(0, 8)),
                    w_tm, (slice(None), slice(0, 8)), sel, (slice(None), slice(0, 8)),
                    tag="rmask")

            def dsum(v):
                return v.reduce_sum(mx.ap[:, 2:3], w_tm.ap[:, 0:8], axis=AX.X)
            p.task("vector", dsum, reads=[w_tm], writes=[mx], tag="rds")

            def rrec(v):
                return v.reciprocal(mx.ap[:, 3:4], mx.ap[:, 2:3])
            p.task("vector", rrec, reads=[mx], writes=[mx], tag="rrec")
            ts_task(p, "vector", ALU.mult, w_tm, (slice(None), slice(0, 8)),
                    w_tm, (slice(None), slice(0, 8)), (mx, (slice(None), slice(3, 4))),
                    tag="rnrm")
            pst = pool.nxt()

            def trb(te, pst=pst):
                return te.transpose(pst.ap[0:8, 0:128], w_tm.ap[:, 0:8],
                                    identf.ap[:, :])
            p.task("tensor", trb, reads=[w_tm, identf], writes=[pst], tag="rtrb")
            copy_task(p, "vector", w_fm, (slice(0, 8), slice(sub * 128, sub * 128 + 128)),
                      pst, (slice(0, 8), slice(0, 128)), tag="rwev")
        p.dma("scalar_dma", wfm_d.ap[:, t0:t0 + Tc], w_fm.ap[0:8, 0:Tc],
              reads=[w_fm], writes=[wfm_d], tag="rwst")
        psa = pool.nxt()

        def am(te, psa=psa):
            last = None
            for k in range(8):
                last = te.matmul(psa.ap[:, 0:Tc], aa_sb.ap[:, k * 128:(k + 1) * 128],
                                 xn_bf.ap[:, k * Tc:(k + 1) * Tc],
                                 start=(k == 0), stop=(k == 7))
            return last
        p.task("tensor", am, reads=[aa_sb, xn_bf], writes=[psa], tag="axm")
        for e in range(8):
            p.dma("sync", wrep.ap[e * 16:(e + 1) * 16, 0:Tc],
                  wfm_d.ap[e:e + 1, t0:t0 + Tc].partition_broadcast(16),
                  reads=[wfm_d], writes=[wrep], tag="wrp")

        def axwf(v, psa=psa):
            return v.tensor_mul(axw.ap[:, 0:Tc], psa.ap[:, 0:Tc], wrep.ap[:, 0:Tc])
        p.task("vector", axwf, reads=[psa, wrep], writes=[axw], tag="axw")
        for oc in range(8):
            psb1 = pool.nxt(); psb2 = pool.nxt()

            def bm(te, oc=oc, psb1=psb1):
                last = None
                for k in range(8):
                    last = te.matmul(psb1.ap[:, 0:Tc],
                                     bw_sb.ap[:, k * 1024 + oc * 128:k * 1024 + oc * 128 + 128],
                                     xn_bf.ap[:, k * Tc:(k + 1) * Tc],
                                     start=(k == 0), stop=(k == 7))
                return last
            p.task("tensor", bm, reads=[bw_sb, xn_bf], writes=[psb1], tag="bsm")

            def lm(te, oc=oc, psb2=psb2):
                return te.matmul(psb2.ap[:, 0:Tc],
                                 ba_sb.ap[:, oc * 128:(oc + 1) * 128],
                                 axw.ap[:, 0:Tc], start=True, stop=True)
            p.task("tensor", lm, reads=[ba_sb, axw], writes=[psb2], tag="lrm")
            act_task(p, t1, (slice(None), slice(0, Tc)), psb1,
                     (slice(None), slice(0, Tc)), AF.Identity,
                     bias=(sml["bb"], (slice(None), slice(oc, oc + 1))), tag="bev")
            stt_task(p, "vector", t1, (slice(None), slice(0, Tc)),
                     psb2, (slice(None), slice(0, Tc)), 2.0,
                     t1, (slice(None), slice(0, Tc)), ALU.mult, ALU.add, tag="lcomb")
            act_task(p, g3, (slice(None), slice(oc * Tc, oc * Tc + Tc)), t1,
                     (slice(None), slice(0, Tc)), AF.Silu, tag="g3")
        for oc in range(8):
            ysl = (slice(None), slice(oc * Tc, oc * Tc + Tc))
            tt_task(p, "vector", ALU.mult, g3, ysl, g3, ysl, s3_ch, ysl, tag="sg3")
            tt_task(p, "vector", ALU.add, g3, ysl, g3, ysl, x_ch, ysl, tag="x4")
        p.dma("gpsimd_dma",
              o_x4.ap.rearrange("(c p) t -> p c t", p=128)[:, :, t0:t0 + Tc],
              g3.ap.rearrange("p (c t) -> p c t", t=Tc),
              reads=[g3], writes=[o_x4], tag="x4st")

    p.finalize()
    return nc




def build_l2_c1():
    nc = bass.Bass("TRN2", target_bir_lowering=False, debug=False)
    I = lambda n, s: Buf(n, nc.dram_tensor(n, s, F32, kind="ExternalInput").ap(), "DRAM")
    x2f = I("x2f", [1024, 8192]); wUV = I("wUV", [1024, 256])
    theta2 = I("theta2", [128, 1]); cnw = I("cnw", [128, 1])
    cond_o = Buf("cond", nc.dram_tensor("cond", [128, 8192], F32, kind="ExternalOutput").ap(), "DRAM")

    p = Prog(nc)
    pool = PsumPool(p, n=8)
    ones_col = make_ones_col(p)
    ones_lhs = make_ones_lhs(p)

    wuv_sb, _ = load_w_nat(p, wUV, 1024, 256, tag="wuv")
    th_sb = p.sb("th_sb", [128, 1], F32)
    p.dma("sync", th_sb.ap[:], theta2.ap[:, :], reads=[theta2], writes=[th_sb])
    cnw_sb = p.sb("cnw_sb", [128, 1], F32)
    p.dma("sync", cnw_sb.ap[:], cnw.ap[:, :], reads=[cnw], writes=[cnw_sb])

    Tc = 512
    H = 4096
    scr = ln_scratch(p, Tc)
    x_ch = p.sb("x_ch", [128, 8 * Tc], F32)
    xn_bf = p.sb("xn_bf", [128, 8 * Tc], BF16)
    T1 = p.sb("T1", [128, H], F32)
    T2 = p.sb("T2", [128, H], F32)
    C1s = p.sb("C1s", [128, H], F32)
    C2s = p.sb("C2s", [128, H], F32)
    onesb = p.sb("onesb", [128, H], F32)
    p.task("vector", lambda v: v.memset(onesb.ap[:], 1.0), writes=[onesb])
    carry1 = p.sb("carry1", [128, 2], F32)
    p.task("vector", lambda v: v.memset(carry1.ap[:], 0.0), writes=[carry1])
    posr = p.sb("posr", [128, Tc], F32)
    ang = p.sb("ang", [128, Tc], F32)
    cosb = p.sb("cosb", [128, Tc], F32)
    sinb = p.sb("sinb", [128, Tc], F32)
    srq = p.sb("srq", [128, Tc], F32)
    sri = p.sb("sri", [128, Tc], mybir.dt.int32)
    uch = p.sb("uch", [128, Tc], F32)
    vch = p.sb("vch", [128, Tc], F32)
    zsq = p.sb("zsq", [128, Tc], BF16)
    rr = p.sb("rr", [1, Tc], F32)
    rrb = p.sb("rrb", [1, Tc], BF16)
    rrep = p.sb("rrep", [128, Tc], F32)
    p.const(-0.5)
    csl = (slice(None), slice(0, Tc))

    def emit_tables(t0):
        def iofn(g, t0=t0):
            return g.iota(posr.ap[:], pattern=[[1, Tc]], base=t0,
                          channel_multiplier=0, allow_small_or_imprecise_dtypes=True)
        p.task("gpsimd", iofn, writes=[posr], tag="iota")
        ts_task(p, "vector", ALU.mult, ang, csl, posr, csl,
                (th_sb, (slice(None), slice(0, 1))), tag="ang")
        emit_range_reduce_sin(p, sinb, csl, ang, csl, srq, sri, tag="sin")
        ts_task(p, "vector", ALU.add, ang, csl, ang, csl, TWO_PI / 4.0, tag="angc")
        emit_range_reduce_sin(p, cosb, csl, ang, csl, srq, sri, tag="cos")

    for half in range(2):
        hb = half * H
        for tq in range(H // Tc):
            t0 = hb + tq * Tc
            lsl = (slice(None), slice(tq * Tc, tq * Tc + Tc))
            p.dma("sync", x_ch.ap[:],
                  x2f.ap.rearrange("(c p) t -> p c t", p=128)[:, :, t0:t0 + Tc],
                  reads=[x2f], writes=[x_ch], tag="xld")
            emit_ln_chunk(p, pool, ones_lhs, ones_col, x_ch, Tc, xn_bf, scr)
            for col0, dst in ((0, uch), (128, vch)):
                psb = pool.nxt()

                def mm(te, col0=col0, psb=psb):
                    last = None
                    for k in range(8):
                        last = te.matmul(psb.ap[:, 0:Tc],
                                         wuv_sb.ap[:, k * 256 + col0:k * 256 + col0 + 128],
                                         xn_bf.ap[:, k * Tc:(k + 1) * Tc],
                                         start=(k == 0), stop=(k == 7))
                    return last
                p.task("tensor", mm, reads=[wuv_sb, xn_bf], writes=[psb], tag="uvmm")
                copy_task(p, "vector", dst, csl, psb, csl, tag="uvev")
            emit_tables(t0)
            tt_task(p, "vector", ALU.mult, T1, lsl, cosb, csl, uch, csl, tag="t1a")
            tt_task(p, "vector", ALU.mult, srq, csl, sinb, csl, vch, csl, tag="t1b")
            def t1c(v, lsl=lsl):
                return v.tensor_add(T1.ap[lsl], T1.ap[lsl], srq.ap[csl])
            p.task("vector", t1c, reads=[T1, srq], writes=[T1], tag="t1c")
            tt_task(p, "vector", ALU.mult, T2, lsl, cosb, csl, vch, csl, tag="t2a")
            tt_task(p, "vector", ALU.mult, srq, csl, sinb, csl, uch, csl, tag="t2b")
            def t2c(v, lsl=lsl):
                return v.tensor_sub(T2.ap[lsl], T2.ap[lsl], srq.ap[csl])
            p.task("vector", t2c, reads=[T2, srq], writes=[T2], tag="t2c")
        def sc1(v):
            return v.tensor_tensor_scan(C1s.ap[:], onesb.ap[:], T1.ap[:],
                                        carry1.ap[:, 0:1], ALU.mult, ALU.add)
        p.task("vector", sc1, reads=[onesb, T1, carry1], writes=[C1s], tag="sc1")

        def sc2(v):
            return v.tensor_tensor_scan(C2s.ap[:], onesb.ap[:], T2.ap[:],
                                        carry1.ap[:, 1:2], ALU.mult, ALU.add)
        p.task("vector", sc2, reads=[onesb, T2, carry1], writes=[C2s], tag="sc2")
        copy_task(p, "vector", carry1, (slice(None), slice(0, 1)),
                  C1s, (slice(None), slice(H - 1, H)), tag="cv1")
        copy_task(p, "vector", carry1, (slice(None), slice(1, 2)),
                  C2s, (slice(None), slice(H - 1, H)), tag="cv2")
        for tq in range(H // Tc):
            t0 = hb + tq * Tc
            lsl = (slice(None), slice(tq * Tc, tq * Tc + Tc))
            l0 = (slice(0, 64), slice(tq * Tc, tq * Tc + Tc))
            l1 = (slice(64, 128), slice(tq * Tc, tq * Tc + Tc))
            c0h = (slice(0, 64), slice(0, Tc))
            c1h = (slice(64, 128), slice(0, Tc))
            emit_tables(t0)
            tt_task(p, "vector", ALU.mult, uch, c0h, cosb, c0h, C1s, l0, tag="zre1")
            tt_task(p, "vector", ALU.mult, vch, c0h, sinb, c0h, C2s, l0, tag="zre2")
            def zre(v):
                return v.tensor_sub(uch.ap[c0h], uch.ap[c0h], vch.ap[c0h])
            p.task("vector", zre, reads=[uch, vch], writes=[uch], tag="zre3")
            tt_task(p, "vector", ALU.mult, uch, c1h, sinb, c1h, C1s, l1, tag="zim1")
            tt_task(p, "vector", ALU.mult, vch, c1h, cosb, c1h, C2s, l1, tag="zim2")
            def zim(v):
                return v.tensor_add(uch.ap[c1h], uch.ap[c1h], vch.ap[c1h])
            p.task("vector", zim, reads=[uch, vch], writes=[uch], tag="zim3")
            act_task(p, zsq, csl, uch, csl, AF.Square, tag="zsq")
            ps_sq = pool.nxt()

            def mm_sq(te, ps=ps_sq):
                return te.matmul(ps.ap[0:1, 0:Tc], ones_lhs.ap[:, 0:1],
                                 zsq.ap[:, 0:Tc], start=True, stop=True)
            p.task("tensor", mm_sq, reads=[zsq, ones_lhs], writes=[ps_sq], tag="zmm")

            def st(v, ps=ps_sq):
                return v.tensor_scalar(rr.ap[0:1, 0:Tc], ps.ap[0:1, 0:Tc],
                                       1.0 / 128.0, 1e-6, ALU.mult, ALU.add)
            p.task("vector", st, reads=[ps_sq], writes=[rr], tag="zm")

            def st2(sc_):
                return sc_.activation(rr.ap[0:1, 0:Tc], rr.ap[0:1, 0:Tc], AF.Ln)
            p.task("scalar", st2, reads=[rr], writes=[rr], tag="zln")

            def st3(sc_):
                return sc_.activation(rr.ap[0:1, 0:Tc], rr.ap[0:1, 0:Tc], AF.Exp,
                                      scale=-0.5)
            p.task("scalar", st3, reads=[rr], writes=[rr], tag="zex")
            copy_task(p, "vector", rrb, (slice(0, 1), slice(0, Tc)),
                      rr, (slice(0, 1), slice(0, Tc)), tag="zrb")
            emit_bcast_row(p, pool, ones_col, rrb.ap[0:1, 0:Tc], rrb, rrep, 0, Tc)
            def ap1(v):
                return v.tensor_mul(vch.ap[csl], uch.ap[csl], rrep.ap[csl])
            p.task("vector", ap1, reads=[uch, rrep], writes=[vch], tag="zap")
            ts_task(p, "vector", ALU.mult, vch, csl, vch, csl,
                    (cnw_sb, (slice(None), slice(0, 1))), tag="zw")
            p.dma("gpsimd_dma", cond_o.ap[:, t0:t0 + Tc], vch.ap[csl],
                  reads=[vch], writes=[cond_o], tag="cst")

    p.finalize()
    return nc


def build_l2_c2():
    nc = bass.Bass("TRN2", target_bir_lowering=False, debug=False)
    I = lambda n, s: Buf(n, nc.dram_tensor(n, s, F32, kind="ExternalInput").ap(), "DRAM")
    x2 = I("x2", [1024, 2048]); cond = I("cond", [128, 2048])
    dWp = I("dWp", [128, 3088]); ctrlW = I("ctrlW", [1024, 4128])
    cbq_i = I("cbq", [128, 24]); cbx_i = I("cbx", [32, 1]); cbg_i = I("cbg", [128, 8])
    ramp = I("ramp", [128, 24]); bramp = I("bramp", [16, 1])
    O = lambda n, s: Buf(n, nc.dram_tensor(n, s, F32, kind="ExternalOutput").ap(), "DRAM")
    cq = O("cq", [1024, 2048]); ck = O("ck", [1024, 2048]); cv = O("cv", [1024, 2048])
    bg = O("bg", [32, 2048]); gate_l = O("gate_l", [1024, 2048])

    p = Prog(nc)
    pool = PsumPool(p, n=8)
    ones_col = make_ones_col(p)
    ones_lhs = make_ones_lhs(p)

    dw_sb, _ = load_w_nat(p, dWp, 128, 3088, tag="dw")
    cw_sb, _ = load_w_nat(p, ctrlW, 1024, 4128, tag="cw")
    ramp_sb = p.sb("ramp_sb", [128, 24], F32)
    p.dma("sync", ramp_sb.ap[:], ramp.ap[:, :], reads=[ramp], writes=[ramp_sb])
    act_task(p, ramp_sb, (slice(None), slice(None)), ramp_sb,
             (slice(None), slice(None)), AF.Exp, scale=-1.0, tag="rs1")
    ts_task(p, "vector", ALU.add, ramp_sb, (slice(None), slice(None)),
            ramp_sb, (slice(None), slice(None)), 1.0, tag="rs2")

    def rrec(v):
        return v.reciprocal(ramp_sb.ap[:], ramp_sb.ap[:])
    p.task("vector", rrec, reads=[ramp_sb], writes=[ramp_sb], tag="rs3")
    br_sb = p.sb("br_sb", [16, 1], F32)
    p.dma("sync", br_sb.ap[:], bramp.ap[:, :], reads=[bramp], writes=[br_sb])
    act_task(p, br_sb, (slice(0, 16), slice(0, 1)), br_sb, (slice(0, 16), slice(0, 1)),
             AF.Exp, scale=-1.0, tag="br1")
    ts_task(p, "vector", ALU.add, br_sb, (slice(0, 16), slice(0, 1)),
            br_sb, (slice(0, 16), slice(0, 1)), 1.0, tag="br2")

    def brec(v):
        return v.reciprocal(br_sb.ap[0:16, 0:1], br_sb.ap[0:16, 0:1])
    p.task("vector", brec, reads=[br_sb], writes=[br_sb], tag="br3")

    cbq = p.sb("cbq_sb", [128, 24], F32)
    p.dma("sync", cbq.ap[:], cbq_i.ap[:, :], reads=[cbq_i], writes=[cbq])
    cbx = p.sb("cbx_sb", [16, 1], F32)
    p.dma("sync", cbx.ap[:], cbx_i.ap[0:16, :], reads=[cbx_i], writes=[cbx])
    cbb = p.sb("cbb_sb", [16, 1], F32)
    p.dma("sync", cbb.ap[:], cbx_i.ap[16:32, :], reads=[cbx_i], writes=[cbb])
    cbg = p.sb("cbg_sb", [128, 8], F32)
    p.dma("sync", cbg.ap[:], cbg_i.ap[:, :], reads=[cbg_i], writes=[cbg])

    Tc = 512
    scr = ln_scratch(p, Tc)
    x_ch = p.sb("x_ch", [128, 8 * Tc], F32)
    xn_bf = p.sb("xn_bf", [128, 8 * Tc], BF16)
    cond_bf = p.sb("cond_bf", [128, Tc], BF16)
    dd = p.sb("dd", [128, Tc], F32)
    ev = p.sb("ev", [128, Tc], F32)

    for tq in range(4):
        t0 = tq * Tc
        p.dma("sync", x_ch.ap[:],
              x2.ap.rearrange("(c p) t -> p c t", p=128)[:, :, t0:t0 + Tc],
              reads=[x2], writes=[x_ch], tag="xld")
        emit_ln_chunk(p, pool, ones_lhs, ones_col, x_ch, Tc, xn_bf, scr)
        p.dma("gpsimd_dma", cond_bf.ap[:], cond.ap[:, t0:t0 + Tc],
              reads=[cond], writes=[cond_bf], tag="cndld")
        for oc in range(24):
            which = oc // 8
            psd = pool.nxt()

            def dmm(te, oc=oc, psd=psd):
                return te.matmul(psd.ap[:, 0:Tc], dw_sb.ap[:, oc * 128:(oc + 1) * 128],
                                 cond_bf.ap[:, 0:Tc], start=True, stop=True)
            p.task("tensor", dmm, reads=[dw_sb, cond_bf], writes=[psd], tag="dmm")
            psc = pool.nxt()

            def cmm(te, oc=oc, psc=psc):
                last = None
                for k in range(8):
                    last = te.matmul(psc.ap[:, 0:Tc],
                                     cw_sb.ap[:, k * 4128 + oc * 128:k * 4128 + oc * 128 + 128],
                                     xn_bf.ap[:, k * Tc:(k + 1) * Tc],
                                     start=(k == 0), stop=(k == 7))
                return last
            p.task("tensor", cmm, reads=[cw_sb, xn_bf], writes=[psc], tag="cmm")
            ts_task(p, "vector", ALU.mult, dd, (slice(None), slice(0, Tc)),
                    psd, (slice(None), slice(0, Tc)),
                    (ramp_sb, (slice(None), slice(which * 8 + (oc % 8), which * 8 + (oc % 8) + 1))),
                    tag="dscale")
            act_task(p, ev, (slice(None), slice(0, Tc)), psc,
                     (slice(None), slice(0, Tc)), AF.Identity,
                     bias=(cbq, (slice(None), slice(oc, oc + 1))), tag="cbias")
            tt_task(p, "vector", ALU.add, ev, (slice(None), slice(0, Tc)),
                    ev, (slice(None), slice(0, Tc)), dd, (slice(None), slice(0, Tc)),
                    tag="qsum")
            dstb = (cq, ck, cv)[which]
            p.dma("gpsimd_dma", dstb.ap[(oc % 8) * 128:(oc % 8) * 128 + 128, t0:t0 + Tc],
                  ev.ap[:, 0:Tc], reads=[ev], writes=[dstb], tag="cst")
        psD = pool.nxt(); psB = pool.nxt()

        def cmmD(te, psD=psD):
            last = None
            for k in range(8):
                last = te.matmul(psD.ap[0:16, 0:Tc],
                                 cw_sb.ap[:, k * 4128 + 3072:k * 4128 + 3072 + 16],
                                 xn_bf.ap[:, k * Tc:(k + 1) * Tc],
                                 start=(k == 0), stop=(k == 7))
            return last
        p.task("tensor", cmmD, reads=[cw_sb, xn_bf], writes=[psD], tag="cmmD")

        def cmmB(te, psB=psB):
            last = None
            for k in range(8):
                last = te.matmul(psB.ap[0:16, 0:Tc],
                                 cw_sb.ap[:, k * 4128 + 3088:k * 4128 + 3088 + 16],
                                 xn_bf.ap[:, k * Tc:(k + 1) * Tc],
                                 start=(k == 0), stop=(k == 7))
            return last
        p.task("tensor", cmmB, reads=[cw_sb, xn_bf], writes=[psB], tag="cmmB")
        psd = pool.nxt()

        def dmm2(te, psd=psd):
            return te.matmul(psd.ap[0:16, 0:Tc], dw_sb.ap[:, 3072:3088],
                             cond_bf.ap[:, 0:Tc], start=True, stop=True)
        p.task("tensor", dmm2, reads=[dw_sb, cond_bf], writes=[psd], tag="dmm2")

        def decf(v, psD=psD):
            return v.tensor_single_scalar(ev.ap[0:16, 0:Tc], psD.ap[0:16, 0:Tc],
                                          cbx.ap[0:16, 0:1], ALU.add)
        p.task("vector", decf, reads=[psD, cbx], writes=[ev], tag="dec")
        p.dma("gpsimd_dma", bg.ap[0:16, t0:t0 + Tc], ev.ap[0:16, 0:Tc],
              reads=[ev], writes=[bg], tag="bgst1")

        def bdf(v, psd=psd):
            return v.tensor_single_scalar(dd.ap[0:16, 0:Tc], psd.ap[0:16, 0:Tc],
                                          br_sb.ap[0:16, 0:1], ALU.mult)
        p.task("vector", bdf, reads=[psd, br_sb], writes=[dd], tag="bd")

        def betf(v, psB=psB):
            return v.tensor_single_scalar(ev.ap[0:16, 0:Tc], psB.ap[0:16, 0:Tc],
                                          cbb.ap[0:16, 0:1], ALU.add)
        p.task("vector", betf, reads=[psB, cbb], writes=[ev], tag="bet1")

        def betf2(v):
            return v.tensor_add(ev.ap[0:16, 0:Tc], ev.ap[0:16, 0:Tc],
                                dd.ap[0:16, 0:Tc])
        p.task("vector", betf2, reads=[ev, dd], writes=[ev], tag="bet2")
        p.dma("gpsimd_dma", bg.ap[16:32, t0:t0 + Tc], ev.ap[0:16, 0:Tc],
              reads=[ev], writes=[bg], tag="bgst2")
        for oc in range(8):
            psc = pool.nxt()

            def cmm3(te, oc=oc, psc=psc):
                last = None
                for k in range(8):
                    last = te.matmul(psc.ap[:, 0:Tc],
                                     cw_sb.ap[:, k * 4128 + 3104 + oc * 128:k * 4128 + 3104 + oc * 128 + 128],
                                     xn_bf.ap[:, k * Tc:(k + 1) * Tc],
                                     start=(k == 0), stop=(k == 7))
                return last
            p.task("tensor", cmm3, reads=[cw_sb, xn_bf], writes=[psc], tag="cmm3")
            act_task(p, ev, (slice(None), slice(0, Tc)), psc,
                     (slice(None), slice(0, Tc)), AF.Identity,
                     bias=(cbg, (slice(None), slice(oc, oc + 1))), tag="gbias")
            p.dma("gpsimd_dma", gate_l.ap[oc * 128:(oc + 1) * 128, t0:t0 + Tc],
                  ev.ap[:, 0:Tc], reads=[ev], writes=[gate_l], tag="gst")

    p.finalize()
    return nc


def build_l2_c3():
    nc = bass.Bass("TRN2", target_bir_lowering=False, debug=False)
    I = lambda n, s: Buf(n, nc.dram_tensor(n, s, F32, kind="ExternalInput").ap(), "DRAM")
    cqh = I("cqh", [1024, 2054]); ckh = I("ckh", [1024, 2054]); cvh = I("cvh", [1024, 2054])
    convw = I("convw", [128, 96]); dec = I("dec", [16, 2048]); betpre = I("betpre", [16, 2048])
    O = lambda n, s: Buf(n, nc.dram_tensor(n, s, F32, kind="ExternalOutput").ap(), "DRAM")
    g1t = O("g1t", [1024, 2048]); g2t = O("g2t", [1024, 2048])
    m1s = O("m1s", [16, 64, 1024]); m2s = O("m2s", [16, 64, 1024])
    gl_out = O("gl_out", [1, 256])
    q_d = Buf("q_d", nc.dram_tensor("q_d", [1024, 2048], F32, kind="Internal").ap(), "DRAM")
    k_d = Buf("k_d", nc.dram_tensor("k_d", [1024, 2048], F32, kind="Internal").ap(), "DRAM")
    v_d = Buf("v_d", nc.dram_tensor("v_d", [1024, 2048], F32, kind="Internal").ap(), "DRAM")
    gam_d = Buf("gam_d", nc.dram_tensor("gam_d", [16, 2048], F32, kind="Internal").ap(), "DRAM")
    glr_d = Buf("glr_d", nc.dram_tensor("glr_d", [1, 256], F32, kind="Internal").ap(), "DRAM")
    beta_d = Buf("beta_d", nc.dram_tensor("beta_d", [16, 2048], F32, kind="Internal").ap(), "DRAM")
    hsq_d = Buf("hsq_d", nc.dram_tensor("hsq_d", [2, 3072], F32, kind="Internal").ap(), "DRAM")

    p = Prog(nc)
    pool = PsumPool(p, n=8)
    identf = make_identity_f32(p)

    cw_sb = p.sb("cw_sb", [128, 96], F32)
    p.dma("sync", cw_sb.ap[:], convw.ap[:, :], reads=[convw], writes=[cw_sb])
    ones2 = p.sb("ones2", [128, 2], BF16)
    p.task("vector", lambda v: v.memset(ones2.ap[:], 0.0), writes=[ones2])
    p.task("vector", lambda v: v.memset(ones2.ap[0:64, 0:1], 1.0), writes=[ones2])
    p.task("vector", lambda v: v.memset(ones2.ap[64:128, 1:2], 1.0), writes=[ones2])

    NCS = 3
    conv_in = [p.sb("conv_in", [128, 1027], F32) for _ in range(NCS)]
    cacc = [p.sb("cacc", [128, 1024], F32) for _ in range(NCS)]
    cbf = [p.sb("cbf", [128, 1024], BF16) for _ in range(NCS)]
    hsq = [p.sb("hsq", [2, 1024], F32) for _ in range(NCS)]
    nrep = [p.sb("nrep", [128, 1024], F32) for _ in range(NCS)]
    hsq_sl = [Buf(f"hsq_d{i}", hsq_d.ap, "DRAM") for i in range(NCS)]
    qd_blk = [Buf(f"q_d{b}", q_d.ap, "DRAM") for b in range(2)]
    kd_blk = [Buf(f"k_d{b}", k_d.ap, "DRAM") for b in range(2)]
    vd_blk = [Buf(f"v_d{b}", v_d.ap, "DRAM") for b in range(2)]
    dst_blk = (qd_blk, kd_blk, vd_blk)

    conv_iters = [(wi, fc, blk) for blk in range(2) for wi in range(3)
                  for fc in range(8)]
    srcs = (cqh, ckh, cvh)

    def cst_load(it, sl):
        wi, fc, blk = it
        p.dma("sync", conv_in[sl].ap[:],
              srcs[wi].ap[fc * 128:(fc + 1) * 128, blk * 1027:(blk + 1) * 1027],
              reads=[srcs[wi]], writes=[conv_in[sl]], tag="cvld")

    def cst_conv(it, sl):
        wi, fc, blk = it
        ts_task(p, "vector", ALU.mult, cacc[sl], (slice(None), slice(0, 1024)),
                conv_in[sl], (slice(None), slice(0, 1024)),
                (cw_sb, (slice(None), slice(wi * 32 + fc * 4, wi * 32 + fc * 4 + 1))),
                tag="cv0")
        for i in range(1, 4):
            stt_task(p, "vector", cacc[sl], (slice(None), slice(0, 1024)),
                     conv_in[sl], (slice(None), slice(i, i + 1024)),
                     (cw_sb, (slice(None), slice(wi * 32 + fc * 4 + i, wi * 32 + fc * 4 + i + 1))),
                     cacc[sl], (slice(None), slice(0, 1024)), ALU.mult, ALU.add,
                     tag="cv")
        act_task(p, cacc[sl], (slice(None), slice(0, 1024)), cacc[sl],
                 (slice(None), slice(0, 1024)), AF.Silu, tag="cvsilu")

    def cst_norm1(it, sl):
        wi, fc, blk = it
        if wi >= 2:
            return
        act_task(p, cbf[sl], (slice(None), slice(0, 1024)), cacc[sl],
                 (slice(None), slice(0, 1024)), AF.Square, tag="nsq")
        for sc_i in range(2):
            psb = pool.nxt()

            def nmm(te, sc_i=sc_i, psb=psb, sl=sl):
                return te.matmul(psb.ap[0:2, 0:512], ones2.ap[:, 0:2],
                                 cbf[sl].ap[:, sc_i * 512:(sc_i + 1) * 512],
                                 start=True, stop=True)
            p.task("tensor", nmm, reads=[ones2, cbf[sl]], writes=[psb], tag="nmm")

            def nst(v, sc_i=sc_i, psb=psb, sl=sl):
                return v.tensor_scalar(
                    hsq[sl].ap[0:2, sc_i * 512:sc_i * 512 + 512],
                    psb.ap[0:2, 0:512], 1.0, 1e-6, ALU.mult, ALU.add)
            p.task("vector", nst, reads=[psb], writes=[hsq[sl]], tag="nst")
        for sc_i in range(2):
            c0 = sc_i * 512

            def nln(s_, c0=c0, sl=sl):
                return s_.activation(hsq[sl].ap[0:2, c0:c0 + 512],
                                     hsq[sl].ap[0:2, c0:c0 + 512], AF.Ln)
            p.task("scalar", nln, reads=[hsq[sl]], writes=[hsq[sl]], tag="nln")

            def nex(s_, c0=c0, sl=sl):
                return s_.activation(hsq[sl].ap[0:2, c0:c0 + 512],
                                     hsq[sl].ap[0:2, c0:c0 + 512], AF.Exp,
                                     scale=-0.5)
            p.task("scalar", nex, reads=[hsq[sl]], writes=[hsq[sl]], tag="nex")
        p.dma("scalar_dma", hsq_d.ap[:, sl * 1024:(sl + 1) * 1024],
              hsq[sl].ap[0:2, 0:1024],
              reads=[hsq[sl]], writes=[hsq_sl[sl]], tag="nsd")

    def cst_norm2(it, sl):
        wi, fc, blk = it
        if wi >= 2:
            return
        for hh in range(2):
            p.dma("gpsimd_dma", nrep[sl].ap[hh * 64:(hh + 1) * 64, 0:1024],
                  hsq_d.ap[hh:hh + 1, sl * 1024:(sl + 1) * 1024].partition_broadcast(64),
                  reads=[hsq_sl[sl]], writes=[nrep[sl]], tag="nrp")
        tt_task(p, "vector", ALU.mult, cacc[sl], (slice(None), slice(0, 1024)),
                cacc[sl], (slice(None), slice(0, 1024)),
                nrep[sl], (slice(None), slice(0, 1024)), tag="napp")

    def cst_store(it, sl):
        wi, fc, blk = it
        dstd = dst_blk[wi][blk]
        p.dma("scalar_dma", dstd.ap[fc * 128:(fc + 1) * 128,
                                    blk * 1024:(blk + 1) * 1024],
              cacc[sl].ap[:, 0:1024], reads=[cacc[sl]], writes=[dstd], tag="qst")

    conv_stages = (cst_load, cst_conv, cst_norm1, cst_norm2, cst_store)
    for g0 in range(0, len(conv_iters), NCS):
        grp = conv_iters[g0:g0 + NCS]
        for stg in conv_stages:
            for si, it in enumerate(grp):
                stg(it, (g0 + si) % NCS)

    T1 = p.sb("T1", [16, 2048], F32)
    T2 = p.sb("T2", [16, 2048], F32)
    T3 = p.sb("T3", [16, 2048], F32)
    bsl = (slice(0, 16), slice(None))
    p.dma("sync", T1.ap[bsl], betpre.ap[:, :], reads=[betpre], writes=[T1])
    act_task(p, T1, bsl, T1, bsl, AF.Exp, scale=-1.0, tag="bsg1")
    ts_task(p, "vector", ALU.add, T1, bsl, T1, bsl, 1.0, tag="bsg2")

    def brc(v):
        return v.reciprocal(T1.ap[bsl], T1.ap[bsl])
    p.task("vector", brc, reads=[T1], writes=[T1], tag="bsg3")
    p.dma("scalar_dma", beta_d.ap[:, :], T1.ap[bsl], reads=[T1], writes=[beta_d])
    p.dma("sync", T2.ap[bsl], dec.ap[:, :], reads=[dec], writes=[T2])
    act_task(p, T2, bsl, T2, bsl, AF.Exp, scale=-1.0, tag="gsg1")
    ts_task(p, "vector", ALU.add, T2, bsl, T2, bsl, 1.0, tag="gsg2")

    def grc(v):
        return v.reciprocal(T2.ap[bsl], T2.ap[bsl])
    p.task("vector", grc, reads=[T2], writes=[T2], tag="gsg3")
    ts_task(p, "vector", ALU.max, T2, bsl, T2, bsl, 1.0e-30, tag="gclamp")
    act_task(p, T2, bsl, T2, bsl, AF.Ln, tag="lgg")
    p.task("vector", lambda v: v.memset(T3.ap[bsl], 1.0), writes=[T3])
    for c0m in range(0, 2048, 128):
        p.task("vector", lambda v, c0m=c0m: v.memset(T3.ap[0:16, c0m:c0m + 1], 0.0),
               writes=[T3])

    def gsc(v):
        return v.tensor_tensor_scan(T1.ap[bsl], T3.ap[bsl], T2.ap[bsl], 0.0,
                                    ALU.mult, ALU.add)
    p.task("vector", gsc, reads=[T3, T2], writes=[T1], tag="gsc")
    p.dma("scalar_dma", gam_d.ap[:, :], T1.ap[bsl], reads=[T1], writes=[gam_d])
    src_gl = gam_d.ap.rearrange("h (c l) -> c h l", l=128)[:, :, 127:128]
    p.dma("sync", glr_d.ap.rearrange("o (c h) -> c h o", h=16), src_gl,
          reads=[gam_d], writes=[glr_d], tag="glg", allow_slow_non_contiguous=True)
    p.dma("sync", gl_out.ap.rearrange("o (c h) -> c h o", h=16), src_gl,
          reads=[gam_d], writes=[gl_out], tag="glo", allow_slow_non_contiguous=True)

    trones = p.sb("trones", [128, 128], F32)
    striU = p.sb("striU", [128, 128], F32)
    triuI = p.sb("triuI", [128, 128], F32)
    bmask = p.sb("bmask", [128, 128], F32)
    p.task("vector", lambda v: v.memset(trones.ap[:], 1.0), writes=[trones])

    def selU(g):
        return g.affine_select(striU.ap[:], trones.ap[:], pattern=[[1, 128]],
                               compare_op=ALU.is_ge, fill=0.0, base=-1,
                               channel_multiplier=-1)
    p.task("gpsimd", selU, reads=[trones], writes=[striU], tag="selU")

    def selUI(g):
        return g.affine_select(triuI.ap[:], trones.ap[:], pattern=[[1, 128]],
                               compare_op=ALU.is_ge, fill=0.0, base=0,
                               channel_multiplier=-1)
    p.task("gpsimd", selUI, reads=[trones], writes=[triuI], tag="selUI")
    p.task("vector", lambda v: v.memset(bmask.ap[:], 0.0), writes=[bmask])
    for bi in range(4):
        b0 = 32 * bi
        p.task("vector", lambda v, b0=b0: v.memset(
            bmask.ap[b0:b0 + 32, b0:b0 + 32], 1.0), writes=[bmask])
    istk = p.sb("istk", [128, 32], F32)
    for bi in range(4):
        b0 = 32 * bi
        copy_task(p, "vector", istk, (slice(b0, b0 + 32), slice(0, 32)),
                  identf, (slice(0, 32), slice(0, 32)), tag="istk")

    NSL = 8

    class Slot:
        pass
    slots = []
    for i in range(NSL):
        s = Slot()
        s.qblk = p.sb("qblk", [64, 128], F32)
        s.kblk = p.sb("kblk", [64, 128], F32)
        s.vblk = p.sb("vblk", [64, 128], F32)
        s.ktm = p.sb("ktm", [128, 64], F32)
        s.vtm = p.sb("vtm", [128, 64], F32)
        s.gtrep = p.sb("gtrep", [128, 128], F32)
        s.egt = p.sb("egt", [64, 128], F32)
        s.qg = p.sb("qg", [64, 128], F32)
        s.NT0 = p.sb("NT0", [128, 128], F32)
        s.NT = p.sb("NTs", [128, 128], F32)
        s.X = p.sb("Xs", [128, 128], F32)
        s.XB = p.sb("XBs", [32, 128], F32)
        s.Mb = p.sb("Mb", [32, 128], F32)
        s.scx = p.sb("scxs", [128, 128], F32)
        s.A_stk = p.sb("A_stk", [128, 32], F32)
        s.P_stk = p.sb("P_stk", [128, 32], F32)
        s.M_stk = p.sb("M_stk", [128, 32], F32)
        s.BD_A = p.sb("BD_A", [128, 128], F32)
        s.BD_P = p.sb("BD_P", [128, 128], F32)
        s.wc = p.sb("wc", [128, 1], F32)
        s.ktw = p.sb("ktw", [128, 64], F32)
        s.m2ib = p.sb("m2ib", [64, 64], F32)
        s.g1b = p.sb("g1b", [64, 128], F32)
        s.g2b = p.sb("g2b", [64, 128], F32)
        p.task("vector", lambda v, s=s: v.memset(s.BD_A.ap[:], 0.0), writes=[s.BD_A])
        p.task("vector", lambda v, s=s: v.memset(s.BD_P.ap[:], 0.0), writes=[s.BD_P])
        slots.append(s)

    gamS = [p.sb("gamS", [16, 128], F32) for _ in range(2)]
    betS = [p.sb("betS", [16, 128], F32) for _ in range(2)]
    gcolA = [p.sb("gcolA", [128, 16], F32) for _ in range(2)]
    bcolA = [p.sb("bcolA", [128, 16], F32) for _ in range(2)]
    ecolA = [p.sb("ecolA", [128, 16], F32) for _ in range(2)]
    glrepC = [p.sb("glrepC", [128, 16], F32) for _ in range(2)]
    eglcC = [p.sb("eglcC", [128, 16], F32) for _ in range(2)]
    m1strip = [p.sb("m1strip", [64, 16 * 64], F32) for _ in range(2)]
    m2strip = [p.sb("m2strip", [64, 16 * 64], F32) for _ in range(2)]
    ALLSL = slice(None)

    def st_load(s, c, h, cp):
        c0 = c * 128
        fc, off = h // 2, (h % 2) * 64
        r0 = fc * 128 + off
        blk = c // 8
        p.dma("sync", s.qblk.ap[0:64, :], q_d.ap[r0:r0 + 64, c0:c0 + 128],
              reads=[qd_blk[blk]], writes=[s.qblk], tag="qbld")
        p.dma("scalar_dma", s.kblk.ap[0:64, :], k_d.ap[r0:r0 + 64, c0:c0 + 128],
              reads=[kd_blk[blk]], writes=[s.kblk], tag="kbld")
        p.dma("gpsimd_dma", s.vblk.ap[0:64, :], v_d.ap[r0:r0 + 64, c0:c0 + 128],
              reads=[vd_blk[blk]], writes=[s.vblk], tag="vbld")
        p.dma("gpsimd_dma", s.gtrep.ap[:],
              gam_d.ap[h:h + 1, c0:c0 + 128].partition_broadcast(128),
              reads=[gam_d], writes=[s.gtrep], tag="gtrep")

    def st_tr(s, c, h, cp):
        for src_, dst in ((s.kblk, s.ktm), (s.vblk, s.vtm)):
            pst = pool.nxt()

            def trf(te, src_=src_, pst=pst):
                return te.transpose(pst.ap[0:128, 0:64], src_.ap[0:64, 0:128],
                                    identf.ap[0:64, 0:64])
            p.task("tensor", trf, reads=[src_, identf], writes=[pst], tag="ktr")
            copy_task(p, "scalar", dst, (ALLSL, slice(0, 64)),
                      pst, (slice(0, 128), slice(0, 64)), tag="ktev")

    def st_pre(s, c, h, cp):
        act_task(p, s.egt, (slice(0, 64), slice(0, 128)), s.gtrep,
                 (slice(0, 64), slice(0, 128)), AF.Exp, tag="egt")
        tt_task(p, "vector", ALU.mult, s.qg, (slice(0, 64), slice(0, 128)),
                s.qblk, (slice(0, 64), slice(0, 128)), s.egt,
                (slice(0, 64), slice(0, 128)), tag="qgam")
        ts_task(p, "vector", ALU.subtract, s.NT0, (ALLSL, slice(0, 128)),
                s.gtrep, (ALLSL, slice(0, 128)),
                (gcolA[cp], (ALLSL, slice(h, h + 1))), tag="lam1")
        tt_task(p, "vector", ALU.mult, s.NT0, (ALLSL, slice(0, 128)),
                s.NT0, (ALLSL, slice(0, 128)), triuI, (ALLSL, slice(0, 128)),
                tag="lam2")
        act_task(p, s.NT0, (ALLSL, slice(0, 128)), s.NT0,
                 (ALLSL, slice(0, 128)), AF.Exp, tag="lam3")
        act_task(p, s.NT0, (ALLSL, slice(0, 128)), s.NT0,
                 (ALLSL, slice(0, 128)), AF.Copy,
                 scale=(bcolA[cp], (ALLSL, slice(h, h + 1))), tag="lamb")

    def st_X(s, c, h, cp):
        copy_task(p, "scalar", s.X, (ALLSL, slice(0, 64)),
                  s.vtm, (ALLSL, slice(0, 64)), tag="xv")
        act_task(p, s.X, (ALLSL, slice(64, 128)), s.ktm,
                 (ALLSL, slice(0, 64)), AF.Copy,
                 scale=(ecolA[cp], (ALLSL, slice(h, h + 1))), tag="xk")
        tt_task(p, "vector", ALU.subtract, s.wc, (ALLSL, slice(0, 1)),
                glrepC[cp], (ALLSL, slice(h, h + 1)),
                gcolA[cp], (ALLSL, slice(h, h + 1)), tag="wc0")
        act_task(p, s.wc, (ALLSL, slice(0, 1)), s.wc,
                 (ALLSL, slice(0, 1)), AF.Exp, tag="wc")
        tt_task(p, "vector", ALU.mult, s.wc, (ALLSL, slice(0, 1)),
                s.wc, (ALLSL, slice(0, 1)),
                bcolA[cp], (ALLSL, slice(h, h + 1)), tag="wcb")
        act_task(p, s.ktw, (ALLSL, slice(0, 64)), s.ktm,
                 (ALLSL, slice(0, 64)), AF.Copy,
                 scale=(s.wc, (ALLSL, slice(0, 1))), tag="ktw")
        act_task(p, s.m2ib, (slice(0, 64), slice(0, 64)), identf,
                 (slice(0, 64), slice(0, 64)), AF.Copy,
                 scale=(eglcC[cp], (slice(0, 64), slice(h, h + 1))), tag="m2i")

    def st_kk(s, c, h, cp):
        pkk = pool.nxt()

        def kkm(te, pkk=pkk, s=s):
            return te.matmul(pkk.ap[:, 0:128], s.kblk.ap[0:64, :],
                             s.kblk.ap[0:64, :], start=True, stop=True)
        p.task("tensor", kkm, reads=[s.kblk], writes=[pkk], tag="kkm")

        def nt2(v, pkk=pkk, s=s):
            return v.tensor_mul(s.NT.ap[:, 0:128], s.NT0.ap[:, 0:128],
                                pkk.ap[:, 0:128])
        p.task("vector", nt2, reads=[s.NT0, pkk], writes=[s.NT], tag="nt2")
        tt_task(p, "vector", ALU.mult, s.NT, (ALLSL, slice(0, 128)),
                s.NT, (ALLSL, slice(0, 128)), striU, (ALLSL, slice(0, 128)),
                tag="nt3")

    def st_kq(s, c, h, cp):
        pkq = pool.nxt()

        def kqm(te, pkq=pkq, s=s):
            return te.matmul(pkq.ap[:, 0:128], s.kblk.ap[0:64, :],
                             s.qblk.ap[0:64, :], start=True, stop=True)
        p.task("tensor", kqm, reads=[s.kblk, s.qblk], writes=[pkq], tag="kqm")

        def pp2(v, pkq=pkq, s=s):
            return v.tensor_mul(s.scx.ap[:, 0:128], s.NT0.ap[:, 0:128],
                                pkq.ap[:, 0:128])
        p.task("vector", pp2, reads=[s.NT0, pkq], writes=[s.scx], tag="pp2")
        tt_task(p, "vector", ALU.mult, s.scx, (ALLSL, slice(0, 128)),
                s.scx, (ALLSL, slice(0, 128)), triuI, (ALLSL, slice(0, 128)),
                tag="pp3")

    def st_diag(s, c, h, cp):
        psc = pool.nxt()

        def trd(te, psc=psc, s=s):
            return te.transpose(psc.ap[0:128, 0:128], s.NT.ap[:, 0:128],
                                identf.ap[:, :])
        p.task("tensor", trd, reads=[s.NT, identf], writes=[psc], tag="trd")

        def bda(v, psc=psc, s=s):
            return v.tensor_mul(s.BD_A.ap[:, 0:128], bmask.ap[:, 0:128],
                                psc.ap[:, 0:128])
        p.task("vector", bda, reads=[bmask, psc], writes=[s.BD_A], tag="bda")
        for bi in range(4):
            b0 = 32 * bi
            copy_task(p, "vector", s.A_stk, (slice(b0, b0 + 32), slice(0, 32)),
                      psc, (slice(b0, b0 + 32), slice(b0, b0 + 32)), tag="astk")
            copy_task(p, "scalar", s.P_stk, (slice(b0, b0 + 32), slice(0, 32)),
                      s.NT, (slice(b0, b0 + 32), slice(b0, b0 + 32)), tag="pstk")
        tt_task(p, "vector", ALU.mult, s.BD_P, (ALLSL, slice(0, 128)),
                s.NT, (ALLSL, slice(0, 128)), bmask, (ALLSL, slice(0, 128)),
                tag="bdp")
        tt_task(p, "vector", ALU.subtract, s.M_stk, (ALLSL, slice(0, 32)),
                istk, (ALLSL, slice(0, 32)), s.P_stk, (ALLSL, slice(0, 32)),
                tag="m0")

    def st_pow(s, c, h, cp, j):
        psd = pool.nxt()
        pse = pool.nxt()

        def sqn(te, psd=psd, s=s):
            return te.matmul(psd.ap[:, 0:32], s.BD_P.ap[:, 0:128],
                             s.A_stk.ap[:, 0:32], start=True, stop=True)
        p.task("tensor", sqn, reads=[s.BD_P, s.A_stk], writes=[psd], tag="sqn")

        def sqt(te, pse=pse, s=s):
            return te.matmul(pse.ap[:, 0:32], s.BD_A.ap[:, 0:128],
                             s.P_stk.ap[:, 0:32], start=True, stop=True)
        p.task("tensor", sqt, reads=[s.BD_A, s.P_stk], writes=[pse], tag="sqt")
        copy_task(p, "scalar", s.A_stk, (ALLSL, slice(0, 32)),
                  psd, (ALLSL, slice(0, 32)), tag="aup")
        copy_task(p, "vector", s.P_stk, (ALLSL, slice(0, 32)),
                  pse, (ALLSL, slice(0, 32)), tag="pup")
        for bi in range(4):
            b0 = 32 * bi
            copy_task(p, "scalar", s.BD_A, (slice(b0, b0 + 32), slice(b0, b0 + 32)),
                      psd, (slice(b0, b0 + 32), slice(0, 32)), tag="bdau")
            copy_task(p, "vector", s.BD_P, (slice(b0, b0 + 32), slice(b0, b0 + 32)),
                      pse, (slice(b0, b0 + 32), slice(0, 32)), tag="bdpu")
        psf = pool.nxt()

        def mnv(te, psf=psf, s=s):
            return te.matmul(psf.ap[:, 0:32], s.BD_A.ap[:, 0:128],
                             s.M_stk.ap[:, 0:32], start=True, stop=True)
        p.task("tensor", mnv, reads=[s.BD_A, s.M_stk], writes=[psf], tag="mnv")

        def mup(v, psf=psf, s=s):
            return v.tensor_add(s.M_stk.ap[:, 0:32], s.M_stk.ap[:, 0:32],
                                psf.ap[:, 0:32])
        p.task("vector", mup, reads=[s.M_stk, psf], writes=[s.M_stk], tag="mup")

    def st_mfin(s, c, h, cp):
        for bi in range(4):
            b0 = 32 * bi
            copy_task(p, "scalar", s.Mb, (slice(0, 32), slice(bi * 32, bi * 32 + 32)),
                      s.M_stk, (slice(b0, b0 + 32), slice(0, 32)), tag="mb")

    def st_solve(s, c, h, cp, bi):
        b0 = 32 * bi
        if bi > 0:
            psh = pool.nxt()

            def smm(te, b0=b0, psh=psh, s=s):
                return te.matmul(psh.ap[0:32, 0:128], s.NT.ap[0:b0, b0:b0 + 32],
                                 s.X.ap[0:b0, 0:128], start=True, stop=True)
            p.task("tensor", smm, reads=[s.NT, s.X], writes=[psh], tag="smm")

            def supd(v, b0=b0, psh=psh, s=s):
                return v.tensor_sub(s.XB.ap[0:32, 0:128],
                                    s.X.ap[b0:b0 + 32, 0:128],
                                    psh.ap[0:32, 0:128])
            p.task("vector", supd, reads=[s.X, psh], writes=[s.XB], tag="supd")
            rhs_buf = s.XB
        else:
            rhs_buf = s.X
        psg = pool.nxt()

        def apm(te, bi=bi, psg=psg, s=s, rhs_buf=rhs_buf):
            return te.matmul(psg.ap[0:32, 0:128],
                             s.Mb.ap[0:32, bi * 32:bi * 32 + 32],
                             rhs_buf.ap[0:32, 0:128], start=True, stop=True)
        p.task("tensor", apm, reads=[s.Mb, rhs_buf], writes=[psg], tag="apm")
        copy_task(p, "scalar", s.X, (slice(b0, b0 + 32), slice(0, 128)),
                  psg, (slice(0, 32), slice(0, 128)), tag="fxb")

    def st_out(s, c, h, cp):
        c0 = c * 128
        fc, off = h // 2, (h % 2) * 64
        r0 = fc * 128 + off
        pg = pool.nxt()

        def g1m(te, pg=pg, s=s):
            return te.matmul(pg.ap[0:64, 0:128], s.X.ap[:, 0:64],
                             s.scx.ap[:, 0:128], start=True, stop=True)
        p.task("tensor", g1m, reads=[s.X, s.scx], writes=[pg], tag="g1m")
        copy_task(p, "scalar", s.g1b, (slice(0, 64), slice(0, 128)),
                  pg, (slice(0, 64), slice(0, 128)), tag="g1ev")
        p.dma("gpsimd_dma", g1t.ap[r0:r0 + 64, c0:c0 + 128], s.g1b.ap[0:64, :],
              reads=[s.g1b], writes=[g1t], tag="g1st")
        pz = pool.nxt()

        def zpm(te, pz=pz, s=s):
            return te.matmul(pz.ap[0:64, 0:128], s.X.ap[:, 64:128],
                             s.scx.ap[:, 0:128], start=True, stop=True)
        p.task("tensor", zpm, reads=[s.X, s.scx], writes=[pz], tag="zpm")

        def g2c(v, pz=pz, s=s):
            return v.tensor_sub(s.g2b.ap[0:64, 0:128], s.qg.ap[0:64, 0:128],
                                pz.ap[0:64, 0:128])
        p.task("vector", g2c, reads=[s.qg, pz], writes=[s.g2b], tag="g2c")
        p.dma("scalar_dma", g2t.ap[r0:r0 + 64, c0:c0 + 128], s.g2b.ap[0:64, :],
              reads=[s.g2b], writes=[g2t], tag="g2st")
        pm1 = pool.nxt()

        def m1m(te, pm1=pm1, s=s):
            return te.matmul(pm1.ap[0:64, 0:64], s.ktw.ap[:, 0:64],
                             s.X.ap[:, 0:64], start=True, stop=True)
        p.task("tensor", m1m, reads=[s.ktw, s.X], writes=[pm1], tag="m1m")
        copy_task(p, "scalar", m1strip[cp], (slice(0, 64), slice(h * 64, h * 64 + 64)),
                  pm1, (slice(0, 64), slice(0, 64)), tag="m1ev")
        pm2 = pool.nxt()

        def m2m(te, pm2=pm2, s=s):
            return te.matmul(pm2.ap[0:64, 0:64], s.X.ap[:, 64:128],
                             s.ktw.ap[:, 0:64], start=True, stop=True)
        p.task("tensor", m2m, reads=[s.X, s.ktw], writes=[pm2], tag="m2m")

        def m2c2(v, pm2=pm2, s=s, h=h, cp=cp):
            return v.tensor_sub(m2strip[cp].ap[0:64, h * 64:h * 64 + 64],
                                s.m2ib.ap[0:64, 0:64], pm2.ap[0:64, 0:64])
        p.task("vector", m2c2, reads=[s.m2ib, pm2], writes=[m2strip[cp]], tag="m2c")

    for c in range(16):
        c0 = c * 128
        cp = c % 2
        p.dma("sync", gamS[cp].ap[0:16, :], gam_d.ap[:, c0:c0 + 128],
              reads=[gam_d], writes=[gamS[cp]], tag="gamS")
        p.dma("sync", betS[cp].ap[0:16, :], beta_d.ap[:, c0:c0 + 128],
              reads=[beta_d], writes=[betS[cp]], tag="betS")
        for src_, dst in ((gamS[cp], gcolA[cp]), (betS[cp], bcolA[cp])):
            pst = pool.nxt()

            def trc(te, src_=src_, pst=pst):
                return te.transpose(pst.ap[0:128, 0:16], src_.ap[0:16, 0:128],
                                    identf.ap[0:16, 0:16])
            p.task("tensor", trc, reads=[src_, identf], writes=[pst], tag="trc")
            copy_task(p, "vector", dst, (ALLSL, slice(0, 16)),
                      pst, (slice(0, 128), slice(0, 16)), tag="trcev")
        act_task(p, ecolA[cp], (ALLSL, slice(0, 16)), gcolA[cp],
                 (ALLSL, slice(0, 16)), AF.Exp, tag="ecol")
        p.dma("scalar_dma", glrepC[cp].ap[:],
              glr_d.ap[0:1, c * 16:(c + 1) * 16].partition_broadcast(128),
              reads=[glr_d], writes=[glrepC[cp]], tag="glrld")
        act_task(p, eglcC[cp], (ALLSL, slice(0, 16)), glrepC[cp],
                 (ALLSL, slice(0, 16)), AF.Exp, tag="egl")
        for wave in range(2):
            hs = [(slots[i], c, wave * 8 + i, cp) for i in range(NSL)]
            for a in hs:
                st_load(*a)
            for a in hs:
                st_tr(*a)
            for a in hs:
                st_pre(*a)
            for a in hs:
                st_X(*a)
            for a in hs:
                st_kk(*a)
            for a in hs:
                st_kq(*a)
            for a in hs:
                st_diag(*a)
            for j in range(1, 5):
                for a in hs:
                    st_pow(*a, j)
            for a in hs:
                st_mfin(*a)
            for bi in range(4):
                for a in hs:
                    st_solve(*a, bi)
            for a in hs:
                st_out(*a)
        p.dma("gpsimd_dma", m1s.ap[c, :, :], m1strip[cp].ap[0:64, :],
              reads=[m1strip[cp]], writes=[m1s], tag="m1st")
        p.dma("gpsimd_dma", m2s.ap[c, :, :], m2strip[cp].ap[0:64, :],
              reads=[m2strip[cp]], writes=[m2s], tag="m2st")

    p.finalize()
    return nc


def build_l2_c4a():
    nc = bass.Bass("TRN2", target_bir_lowering=False, debug=False)
    I = lambda n, s: Buf(n, nc.dram_tensor(n, s, F32, kind="ExternalInput").ap(), "DRAM")
    m1s_all = I("m1s_all", [64, 64, 1024]); m2s_all = I("m2s_all", [64, 64, 1024])
    s_ins = Buf("s_ins", nc.dram_tensor("s_ins", [64, 64, 1024], F32, kind="ExternalOutput").ap(), "DRAM")

    p = Prog(nc)
    pool = PsumPool(p, n=8)
    S = p.sb("S", [64, 16 * 64], F32)
    m1c = p.sb("m1c", [64, 16 * 64], F32)
    m2c = p.sb("m2c", [64, 16 * 64], F32)
    Snew = p.sb("Snew", [64, 16 * 64], F32)
    p.task("vector", lambda v: v.memset(S.ap[0:64, :], 0.0), writes=[S])
    for g in range(64):
        p.dma("gpsimd_dma", s_ins.ap[g, :, :], S.ap[0:64, :], reads=[S],
              writes=[s_ins], tag="sst")
        p.dma("sync", m1c.ap[0:64, :], m1s_all.ap[g, :, :], reads=[m1s_all],
              writes=[m1c], tag="m1ld")
        p.dma("sync", m2c.ap[0:64, :], m2s_all.ap[g, :, :], reads=[m2s_all],
              writes=[m2c], tag="m2ld")
        for h in range(16):
            hs = slice(h * 64, h * 64 + 64)
            psb = pool.nxt()

            def mm(te, hs=hs, psb=psb):
                return te.matmul(psb.ap[0:64, 0:64], m2c.ap[0:64, hs],
                                 S.ap[0:64, hs], start=True, stop=True)
            p.task("tensor", mm, reads=[m2c, S], writes=[psb], tag="smm")

            def upd(v, hs=hs, psb=psb):
                return v.tensor_add(Snew.ap[0:64, hs], psb.ap[0:64, 0:64],
                                    m1c.ap[0:64, hs])
            p.task("vector", upd, reads=[psb, m1c], writes=[Snew], tag="supd")
        copy_task(p, "vector", S, (slice(0, 64), slice(None)),
                  Snew, (slice(0, 64), slice(None)), tag="scp")
    p.finalize()
    return nc


def build_l2_c4b():
    nc = bass.Bass("TRN2", target_bir_lowering=False, debug=False)
    I = lambda n, s: Buf(n, nc.dram_tensor(n, s, F32, kind="ExternalInput").ap(), "DRAM")
    g1t = I("g1t", [1024, 2048]); g2t = I("g2t", [1024, 2048])
    s_ins_own = I("s_ins_own", [16, 64, 1024]); gate_l = I("gate_l", [1024, 2048])
    onw = I("onw", [128, 1]); opT = I("opT", [1024, 1024]); opb = I("opb", [128, 8])
    s2_o = Buf("s2", nc.dram_tensor("s2", [1024, 2048], F32, kind="ExternalOutput").ap(), "DRAM")
    on_d = Buf("on_d", nc.dram_tensor("on_d", [2, 2048], F32, kind="Internal").ap(), "DRAM")

    p = Prog(nc)
    pool = PsumPool(p, n=8)
    ones2 = p.sb("ones2", [128, 2], BF16)
    p.task("vector", lambda v: v.memset(ones2.ap[:], 0.0), writes=[ones2])
    p.task("vector", lambda v: v.memset(ones2.ap[0:64, 0:1], 1.0), writes=[ones2])
    p.task("vector", lambda v: v.memset(ones2.ap[64:128, 1:2], 1.0), writes=[ones2])
    op_sb, _ = load_w_nat(p, opT, 1024, 1024, tag="op")
    opb_sb = p.sb("opb_sb", [128, 8], F32)
    p.dma("sync", opb_sb.ap[:], opb.ap[:, :], reads=[opb], writes=[opb_sb])
    onw_sb = p.sb("onw_sb", [128, 1], F32)
    p.dma("sync", onw_sb.ap[:], onw.ap[:, :], reads=[onw], writes=[onw_sb])
    sin_sb = p.sb("sin_sb", [64, 1024], F32)
    g2blk = p.sb("g2blk", [64, 128], F32)
    ob = p.sb("ob", [128, 8 * 2048], BF16)
    og = p.sb("og", [64, 128], F32)
    o_fm = p.sb("o_fm", [128, 2048], F32)
    hsq = p.sb("hsq", [2, 2048], F32)
    nrep = p.sb("nrep", [128, 2048], F32)
    osq = p.sb("osq", [128, 2048], BF16)
    gch = p.sb("gch", [128, 2048], F32)

    for fc in range(8):
        for pos in range(2):
            off = pos * 64
            r0 = fc * 128 + off
            for c in range(16):
                h = fc * 2 + pos
                c0 = c * 128
                p.dma("sync", sin_sb.ap[0:64, 0:64],
                      s_ins_own.ap[c, :, h * 64:(h + 1) * 64],
                      reads=[s_ins_own], writes=[sin_sb], tag="sinld")
                p.dma("scalar_dma", g2blk.ap[0:64, :], g2t.ap[r0:r0 + 64, c0:c0 + 128],
                      reads=[g2t], writes=[g2blk], tag="g2ld")
                psb = pool.nxt()

                def im(te, psb=psb):
                    return te.matmul(psb.ap[0:64, 0:128], sin_sb.ap[0:64, 0:64],
                                     g2blk.ap[0:64, 0:128], start=True, stop=True)
                p.task("tensor", im, reads=[sin_sb, g2blk], writes=[psb], tag="oint")
                p.dma("sync", og.ap[0:64, :], g1t.ap[r0:r0 + 64, c0:c0 + 128],
                      reads=[g1t], writes=[og], tag="g1ld")

                def cmb(v, psb=psb, off=off, c0=c0):
                    return v.tensor_add(o_fm.ap[off:off + 64, c0:c0 + 128],
                                        psb.ap[0:64, 0:128], og.ap[0:64, 0:128])
                p.task("vector", cmb, reads=[psb, og], writes=[o_fm], tag="ocmb")
        fsl = (slice(None), slice(None))
        act_task(p, osq, fsl, o_fm, fsl, AF.Square, tag="osq")
        for sc_i in range(4):
            w0 = sc_i * 512
            psb = pool.nxt()

            def nmm(te, w0=w0, psb=psb):
                return te.matmul(psb.ap[0:2, 0:512], ones2.ap[:, 0:2],
                                 osq.ap[:, w0:w0 + 512], start=True, stop=True)
            p.task("tensor", nmm, reads=[ones2, osq], writes=[psb], tag="nmm")

            def nst(v, w0=w0, psb=psb):
                return v.tensor_scalar(hsq.ap[0:2, w0:w0 + 512], psb.ap[0:2, 0:512],
                                       1.0 / 64.0, 1e-6, ALU.mult, ALU.add)
            p.task("vector", nst, reads=[psb], writes=[hsq], tag="nst")

            def nln(s_, w0=w0):
                return s_.activation(hsq.ap[0:2, w0:w0 + 512], hsq.ap[0:2, w0:w0 + 512],
                                     AF.Ln)
            p.task("scalar", nln, reads=[hsq], writes=[hsq], tag="nln")

            def nex(s_, w0=w0):
                return s_.activation(hsq.ap[0:2, w0:w0 + 512], hsq.ap[0:2, w0:w0 + 512],
                                     AF.Exp, scale=-0.5)
            p.task("scalar", nex, reads=[hsq], writes=[hsq], tag="nex")
        p.dma("scalar_dma", on_d.ap[:, :], hsq.ap[0:2, :], reads=[hsq], writes=[on_d])
        for hh in range(2):
            p.dma("sync", nrep.ap[hh * 64:(hh + 1) * 64, :],
                  on_d.ap[hh:hh + 1, :].partition_broadcast(64),
                  reads=[on_d], writes=[nrep], tag="nrp")
        tt_task(p, "vector", ALU.mult, o_fm, fsl, o_fm, fsl, nrep, fsl, tag="napp")
        ts_task(p, "vector", ALU.mult, o_fm, fsl, o_fm, fsl,
                (onw_sb, (slice(None), slice(0, 1))), tag="nw")
        p.dma("sync", gch.ap[:], gate_l.ap[fc * 128:(fc + 1) * 128, :],
              reads=[gate_l], writes=[gch], tag="gld")
        act_task(p, gch, fsl, gch, fsl, AF.Silu, tag="gsilu")

        def gmul(v, fc=fc):
            return v.tensor_mul(ob.ap[:, fc * 2048:(fc + 1) * 2048], o_fm.ap[fsl],
                                gch.ap[fsl])
        p.task("vector", gmul, reads=[o_fm, gch], writes=[ob], tag="gmul")

    ev = p.sb("ev", [128, 512], F32)
    for tq in range(4):
        t0 = tq * 512
        for oc in range(8):
            psb = pool.nxt()

            def wm(te, oc=oc, psb=psb, t0=t0):
                last = None
                for k in range(8):
                    last = te.matmul(psb.ap[:, 0:512],
                                     op_sb.ap[:, k * 1024 + oc * 128:k * 1024 + oc * 128 + 128],
                                     ob.ap[:, k * 2048 + t0:k * 2048 + t0 + 512],
                                     start=(k == 0), stop=(k == 7))
                return last
            p.task("tensor", wm, reads=[op_sb, ob], writes=[psb], tag="opm")
            act_task(p, ev, (slice(None), slice(0, 512)), psb,
                     (slice(None), slice(0, 512)), AF.Identity,
                     bias=(opb_sb, (slice(None), slice(oc, oc + 1))), tag="opev")
            p.dma("gpsimd_dma", s2_o.ap[oc * 128:(oc + 1) * 128, t0:t0 + 512],
                  ev.ap[:, 0:512], reads=[ev], writes=[s2_o], tag="s2st")

    p.finalize()
    return nc


def build_l2_c5():
    nc = bass.Bass("TRN2", target_bir_lowering=False, debug=False)
    I = lambda n, s: Buf(n, nc.dram_tensor(n, s, F32, kind="ExternalInput").ap(), "DRAM")
    x2h = I("x2h", [1024, 1024]); s2h = I("s2h", [1024, 1024])
    WrT = I("WrT", [1024, 8]); br = I("br", [1, 8])
    W1 = I("W1", [8, 1024, 4096]); b1pc = I("b1pc", [128, 256])
    W2 = I("W2", [8, 4096, 1024]); b2pc = I("b2pc", [128, 64])
    x3h = Buf("x3h", nc.dram_tensor("x3h", [1024, 1024], F32, kind="ExternalOutput").ap(), "DRAM")
    cw_dram = Buf("cw_dram", nc.dram_tensor("cw_dram", [8, 1024], F32, kind="Internal").ap(), "DRAM")

    p = Prog(nc)
    pool = PsumPool(p, n=8)
    ones_col = make_ones_col(p)
    ones_lhs = make_ones_lhs(p)
    identf = make_identity_f32(p)

    wr_sb, _ = load_w_nat(p, WrT, 1024, 8, tag="wr")
    br_rep = p.sb("br_rep", [128, 8], F32)
    p.dma("sync", br_rep.ap[:], br.ap[0:1, :].partition_broadcast(128),
          reads=[br], writes=[br_rep])
    b1_sb = p.sb("b1_sb", [128, 256], F32)
    p.dma("sync", b1_sb.ap[:], b1pc.ap[:, :], reads=[b1pc], writes=[b1_sb])
    b2_sb = p.sb("b2_sb", [128, 64], F32)
    p.dma("sync", b2_sb.ap[:], b2pc.ap[:, :], reads=[b2pc], writes=[b2_sb])

    T = 1024
    Tc = 512
    scr = ln_scratch(p, Tc)
    x_ch = p.sb("x_ch", [128, 8 * Tc], F32)
    xn_bf = p.sb("xn_bf", [128, 8 * T], BF16)
    lg = p.sb("lg", [8, T], F32)
    lg_tm = p.sb("lg_tm", [128, 8], F32)
    mx = p.sb("mx", [128, 4], F32)
    sel = p.sb("sel", [128, 8], F32)
    w_tm = p.sb("w_tm", [128, 8], F32)
    w_fm = p.sb("w_fm", [8, T], F32)
    y_sb = p.sb("y_sb", [128, 8 * T], F32)
    w1_sb = p.sb("w1_sb", [128, 8 * 2048], BF16)
    w2_sb = p.sb("w2_sb", [128, 16 * 1024], BF16)
    h_bf = p.sb("h_bf", [128, 16 * Tc], BF16)
    hg = p.sb("hg", [128, Tc], F32)
    yt = p.sb("yt", [128, Tc], F32)
    cwrep = p.sb("cwrep", [128, T], F32)

    for tq in range(2):
        t0 = tq * Tc
        p.dma("sync", x_ch.ap[:],
              x2h.ap.rearrange("(c p) t -> p c t", p=128)[:, :, t0:t0 + Tc],
              reads=[x2h], writes=[x_ch], tag="xld")
        emit_ln_to(p, pool, ones_lhs, ones_col, x_ch, Tc, xn_bf, t0, T, scr)

    for tq in range(2):
        t0 = tq * Tc
        psb = pool.nxt()

        def rmm(te, psb=psb, t0=t0):
            last = None
            for k in range(8):
                last = te.matmul(psb.ap[0:8, 0:Tc], wr_sb.ap[:, k * 8:(k + 1) * 8],
                                 xn_bf.ap[:, k * T + t0:k * T + t0 + Tc],
                                 start=(k == 0), stop=(k == 7))
            return last
        p.task("tensor", rmm, reads=[wr_sb, xn_bf], writes=[psb], tag="rmm")
        copy_task(p, "vector", lg, (slice(0, 8), slice(t0, t0 + Tc)),
                  psb, (slice(0, 8), slice(0, Tc)), tag="rgev")
    for sub in range(8):
        s0 = sub * 128
        pst = pool.nxt()

        def trf(te, s0=s0, pst=pst):
            return te.transpose(pst.ap[0:128, 0:8], lg.ap[0:8, s0:s0 + 128],
                                identf.ap[0:8, 0:8])
        p.task("tensor", trf, reads=[lg, identf], writes=[pst], tag="rtr")
        copy_task(p, "vector", lg_tm, (slice(None), slice(0, 8)),
                  pst, (slice(None), slice(0, 8)), tag="rtev")
        tt_task(p, "vector", ALU.add, lg_tm, (slice(None), slice(0, 8)),
                lg_tm, (slice(None), slice(0, 8)), br_rep, (slice(None), slice(0, 8)),
                tag="rbadd")

        def m1f(v):
            return v.reduce_max(mx.ap[:, 0:1], lg_tm.ap[:, 0:8], axis=AX.X)
        p.task("vector", m1f, reads=[lg_tm], writes=[mx], tag="rm1")
        ts_task(p, "vector", ALU.is_equal, sel, (slice(None), slice(0, 8)),
                lg_tm, (slice(None), slice(0, 8)), (mx, (slice(None), slice(0, 1))),
                tag="req")
        stt_task(p, "vector", sel, (slice(None), slice(0, 8)),
                 sel, (slice(None), slice(0, 8)), -1.0e30,
                 lg_tm, (slice(None), slice(0, 8)), ALU.mult, ALU.add, tag="rg2")

        def m2f(v):
            return v.reduce_max(mx.ap[:, 1:2], sel.ap[:, 0:8], axis=AX.X)
        p.task("vector", m2f, reads=[sel], writes=[mx], tag="rm2")
        ts_task(p, "vector", ALU.is_ge, sel, (slice(None), slice(0, 8)),
                lg_tm, (slice(None), slice(0, 8)), (mx, (slice(None), slice(1, 2))),
                tag="rsel")
        ts_task(p, "vector", ALU.subtract, w_tm, (slice(None), slice(0, 8)),
                lg_tm, (slice(None), slice(0, 8)), (mx, (slice(None), slice(0, 1))),
                tag="rsub")
        act_task(p, w_tm, (slice(None), slice(0, 8)), w_tm, (slice(None), slice(0, 8)),
                 AF.Exp, tag="rexp")
        tt_task(p, "vector", ALU.mult, w_tm, (slice(None), slice(0, 8)),
                w_tm, (slice(None), slice(0, 8)), sel, (slice(None), slice(0, 8)),
                tag="rmask")

        def dsm(v):
            return v.reduce_sum(mx.ap[:, 2:3], w_tm.ap[:, 0:8], axis=AX.X)
        p.task("vector", dsm, reads=[w_tm], writes=[mx], tag="rds")

        def rrc(v):
            return v.reciprocal(mx.ap[:, 3:4], mx.ap[:, 2:3])
        p.task("vector", rrc, reads=[mx], writes=[mx], tag="rrec")
        ts_task(p, "vector", ALU.mult, w_tm, (slice(None), slice(0, 8)),
                w_tm, (slice(None), slice(0, 8)), (mx, (slice(None), slice(3, 4))),
                tag="rnrm")
        pst2 = pool.nxt()

        def trb(te, pst2=pst2):
            return te.transpose(pst2.ap[0:8, 0:128], w_tm.ap[:, 0:8], identf.ap[:, :])
        p.task("tensor", trb, reads=[w_tm, identf], writes=[pst2], tag="rtrb")
        copy_task(p, "vector", w_fm, (slice(0, 8), slice(s0, s0 + 128)),
                  pst2, (slice(0, 8), slice(0, 128)), tag="rwev")
    p.dma("scalar_dma", cw_dram.ap[:, :], w_fm.ap[0:8, :], reads=[w_fm],
          writes=[cw_dram], tag="rwst")

    p.task("vector", lambda v: v.memset(y_sb.ap[:], 0.0), writes=[y_sb], tag="y0")
    for e in range(8):
        p.dma("sync", cwrep.ap[:], cw_dram.ap[e:e + 1, :].partition_broadcast(128),
              reads=[cw_dram], writes=[cwrep], tag="cwrep")
        for hh in range(2):
            p.dma("gpsimd_dma", w1_sb.ap[:],
                  W1.ap[e].rearrange("(c p) o -> p c o", p=128)[:, :, hh * 2048:(hh + 1) * 2048],
                  reads=[W1], writes=[w1_sb], tag="w1ld")
            p.dma("gpsimd_dma", w2_sb.ap[:],
                  W2.ap[e, hh * 2048:(hh + 1) * 2048, :].rearrange("(c p) o -> p c o", p=128),
                  reads=[W2], writes=[w2_sb], tag="w2ld")
            for tq in range(2):
                t0 = tq * Tc
                for oc in range(16):
                    psb = pool.nxt()

                    def hmm(te, oc=oc, psb=psb, t0=t0):
                        last = None
                        for k in range(8):
                            last = te.matmul(psb.ap[:, 0:Tc],
                                             w1_sb.ap[:, k * 2048 + oc * 128:k * 2048 + oc * 128 + 128],
                                             xn_bf.ap[:, k * T + t0:k * T + t0 + Tc],
                                             start=(k == 0), stop=(k == 7))
                        return last
                    p.task("tensor", hmm, reads=[w1_sb, xn_bf], writes=[psb], tag="hmm")
                    act_task(p, h_bf, (slice(None), slice(oc * Tc, oc * Tc + Tc)),
                             psb, (slice(None), slice(0, Tc)), AF.Gelu,
                             bias=(b1_sb, (slice(None), slice(e * 32 + hh * 16 + oc, e * 32 + hh * 16 + oc + 1))),
                             tag="gelu")
                for oc2 in range(8):
                    psd = pool.nxt()

                    def dmm(te, oc2=oc2, psd=psd):
                        last = None
                        for k in range(16):
                            last = te.matmul(psd.ap[:, 0:Tc],
                                             w2_sb.ap[:, k * 1024 + oc2 * 128:k * 1024 + oc2 * 128 + 128],
                                             h_bf.ap[:, k * Tc:(k + 1) * Tc],
                                             start=(k == 0), stop=(k == 15))
                        return last
                    p.task("tensor", dmm, reads=[w2_sb, h_bf], writes=[psd], tag="dmm")
                    if hh == 0:
                        act_task(p, yt, (slice(None), slice(0, Tc)), psd,
                                 (slice(None), slice(0, Tc)), AF.Identity,
                                 bias=(b2_sb, (slice(None), slice(e * 8 + oc2, e * 8 + oc2 + 1))),
                                 tag="dbias")
                    else:
                        copy_task(p, "vector", yt, (slice(None), slice(0, Tc)),
                                  psd, (slice(None), slice(0, Tc)), tag="dev")
                    def ya(v, oc2=oc2, t0=t0):
                        return v.tensor_mul(yt.ap[:, 0:Tc], yt.ap[:, 0:Tc],
                                            cwrep.ap[:, t0:t0 + Tc])
                    p.task("vector", ya, reads=[yt, cwrep], writes=[yt], tag="ycw")
                    tt_task(p, "vector", ALU.add, y_sb,
                            (slice(None), slice(oc2 * T + t0, oc2 * T + t0 + Tc)),
                            y_sb, (slice(None), slice(oc2 * T + t0, oc2 * T + t0 + Tc)),
                            yt, (slice(None), slice(0, Tc)), tag="yacc")

    s_ch = p.sb("s_ch", [128, T], F32)
    x_fl = p.sb("x_fl", [128, T], F32)
    for oc2 in range(8):
        ysl = (slice(None), slice(oc2 * T, oc2 * T + T))
        p.dma("sync", s_ch.ap[:], s2h.ap[oc2 * 128:(oc2 + 1) * 128, :],
              reads=[s2h], writes=[s_ch], tag="sld")
        p.dma("scalar_dma", x_fl.ap[:], x2h.ap[oc2 * 128:(oc2 + 1) * 128, :],
              reads=[x2h], writes=[x_fl], tag="xld2")
        act_task(p, y_sb, ysl, y_sb, ysl, AF.Silu, tag="g2silu")
        tt_task(p, "vector", ALU.mult, y_sb, ysl, y_sb, ysl, s_ch,
                (slice(None), slice(0, T)), tag="sg2")
        tt_task(p, "vector", ALU.add, y_sb, ysl, y_sb, ysl, x_fl,
                (slice(None), slice(0, T)), tag="x3")
        p.dma("gpsimd_dma", x3h.ap[oc2 * 128:(oc2 + 1) * 128, :],
              y_sb.ap[ysl], reads=[y_sb], writes=[x3h], tag="x3st")

    p.finalize()
    return nc



_NC_CACHE = {}


def _get(name, builder):
    if name not in _NC_CACHE:
        _NC_CACHE[name] = builder()
    return _NC_CACHE[name]


def _run8(nc, in_maps):
    return run_bass_kernel_spmd(nc, in_maps, core_ids=list(range(8))).results


def _core_cols(c):
    j = c % 4
    return list(range(j * 1024, (j + 1) * 1024)) + \
           list(range((7 - j) * 1024, (8 - j) * 1024))


def _pc(v, n=8):
    return np.ascontiguousarray(np.asarray(v).reshape(n, 128).T)


def _kdbg(tag, arrs):
    if os.environ.get("KDBG"):
        bad = {}
        for i, a in enumerate(arrs):
            nf = int((~np.isfinite(np.asarray(a, dtype=np.float32))).sum())
            if nf:
                bad[i] = nf
        print(f"[KDBG] {tag}: nonfinite {bad}" if bad else f"[KDBG] {tag}: clean",
              flush=True)


def kernel(**inputs):
    inp = {k: np.asarray(v) for k, v in inputs.items()}
    x = inp["L0_attn_x"]
    xT = [np.ascontiguousarray(x[b].T) for b in range(2)]
    cols = [_core_cols(c) for c in range(8)]

    def owner_of_block(blk):
        j = min(blk, 7 - blk)
        return j, (0 if blk == j else 1)

    W = inp["L0_attn_in_proj_w"]; cw = inp["L0_attn_conv_w"]
    cb = inp["L0_attn_conv_b"]; xp = inp["L0_attn_x_proj_w"]
    nc = _get("l0a1", build_l0_a1)
    im = []
    for c in range(8):
        b, g = c // 4, c % 4
        e0 = g * 512
        wslice = np.concatenate([W[:, e0:e0 + 512], W[:, 2048 + e0:2048 + e0 + 512]], axis=1)
        im.append({"xb": xT[b], "w_inproj": np.ascontiguousarray(wslice),
                   "conv_w": np.ascontiguousarray(cw[e0:e0 + 512, 0, :]),
                   "conv_b": np.ascontiguousarray(cb[e0:e0 + 512]),
                   "xproj": np.ascontiguousarray(xp[e0:e0 + 512, :])})
    rA1 = _run8(nc, im)

    nc = _get("l0a2", build_l0_a2)
    dtw = inp["L0_attn_dt_proj_w"]; dtb = inp["L0_attn_dt_proj_b"]
    Alog = inp["L0_attn_A_log"]; Dv = inp["L0_attn_D"]
    im = []
    for c in range(8):
        b, g = c // 4, c % 4
        e0 = g * 512
        d4 = np.stack([rA1[b * 4 + gg]["o_dbc"] for gg in range(4)])
        im.append({"dbc4": np.ascontiguousarray(d4),
                   "xcc": rA1[c]["o_xc"], "z_pre": rA1[c]["o_z"],
                   "dtp": np.ascontiguousarray(dtw[:, e0:e0 + 512]),
                   "dtb": np.ascontiguousarray(dtb[e0:e0 + 512]),
                   "alog": np.ascontiguousarray(Alog[e0:e0 + 512]),
                   "dvec": np.ascontiguousarray(Dv[e0:e0 + 512])})
    rA2 = _run8(nc, im)

    nc = _get("l0a3", build_l0_a3)
    im = []
    for c in range(8):
        b = c // 4
        sg_own = np.concatenate(
            [rA2[b * 4 + gg]["o_sg"][:, :] for gg in range(4)], axis=0)[:, cols[c]]
        im.append({"x_own": np.ascontiguousarray(xT[b][:, cols[c]]),
                   "sg_own": np.ascontiguousarray(sg_own),
                   "outproj": inp["L0_attn_out_proj_w"],
                   "w_ffn": inp["L0_ffn_W"], "b_ffn": inp["L0_ffn_b"],
                   "lora_down": inp["L0_ffn_lora_down"],
                   "lora_up": inp["L0_ffn_lora_up"],
                   "down_aux": inp["L0_ffn_down_aux"],
                   "up_aux": inp["L0_ffn_up_aux"]})
    rA3 = _run8(nc, im)
    _kdbg("x1", [rA3[c]["o_x1"] for c in range(8)])
    x1 = [None] * 8
    for c in range(8):
        x1[c] = rA3[c]["o_x1"]

    wq = inp["L1_attn_w_q"]; wk = inp["L1_attn_w_k"]; wv = inp["L1_attn_w_v"]
    nc = _get("l1b1", build_l1_b1)
    im = []
    for c in range(8):
        im.append({"x1": x1[c],
                   "wq_e": np.ascontiguousarray(wq.T[:, 0::2]),
                   "wq_o": np.ascontiguousarray(wq.T[:, 1::2]),
                   "wk_e": np.ascontiguousarray(wk.T[:, 0::2]),
                   "wk_o": np.ascontiguousarray(wk.T[:, 1::2]),
                   "wvT": np.ascontiguousarray(wv.T),
                   "posv": np.asarray(cols[c], np.float32).reshape(1, 2048)})
    rB1 = _run8(nc, im)
    k_full = {}; v_full = {}
    for b in range(2):
        kb = np.zeros((1024, 8192), dtype=rB1[0]["o_k"].dtype)
        vb = np.zeros((1024, 8192), dtype=rB1[0]["o_v"].dtype)
        for g in range(4):
            c = b * 4 + g
            kb[:, cols[c]] = rB1[c]["o_k"]
            vb[:, cols[c]] = rB1[c]["o_v"]
        k_full[b] = kb
        v_full[b] = np.ascontiguousarray(vb.T)
    nc = _get("l1b2", build_l1_b2)
    im = []
    for c in range(8):
        b = c // 4
        qpos = np.asarray(cols[c], np.float32).reshape(16, 128).T
        im.append({"q_rot": rB1[c]["o_q"], "k_rot": k_full[b], "v_tm": v_full[b],
                   "qpos": np.ascontiguousarray(qpos),
                   "kpos": np.arange(8192, dtype=np.float32).reshape(1, 8192)})
    rB2 = _run8(nc, im)
    nc = _get("l1b3", build_l1_b3)
    im = []
    for c in range(8):
        im.append({"x1": x1[c],
                   "s_att": np.ascontiguousarray(rB2[c]["o_att"].T),
                   "gwT": np.ascontiguousarray(inp["L1_ffn_gate_weight"].T),
                   "ebias": inp["L1_ffn_e_score_correction_bias"].reshape(1, 8),
                   "gT": np.ascontiguousarray(inp["L1_ffn_gate_proj"].transpose(0, 2, 1)),
                   "uT": np.ascontiguousarray(inp["L1_ffn_up_proj"].transpose(0, 2, 1)),
                   "dT": np.ascontiguousarray(inp["L1_ffn_down_proj"].transpose(0, 2, 1))})
    rB3 = _run8(nc, im)
    _kdbg("x2", [rB3[c]["o_x2"] for c in range(8)])
    x2 = [rB3[c]["o_x2"] for c in range(8)]
    x2f = {}
    for b in range(2):
        fb = np.zeros((1024, 8192), np.float32)
        for g in range(4):
            c = b * 4 + g
            fb[:, cols[c]] = x2[c]
        x2f[b] = fb

    perm = list(range(0, 128, 2)) + list(range(1, 128, 2))
    Wc = inp["L2_attn_cond_in_W"]
    wUV = np.concatenate([Wc[:, 0::2], Wc[:, 0::2], Wc[:, 1::2], Wc[:, 1::2]], axis=1)
    theta2 = np.tile(inp["L2_attn_rot_theta"], 2).reshape(128, 1)
    cnw = inp["L2_attn_cond_norm_w"][perm].reshape(128, 1)
    nc = _get("l2c1", build_l2_c1)
    im = [{"x2f": np.ascontiguousarray(x2f[c // 4]),
           "wUV": np.ascontiguousarray(wUV),
           "theta2": np.ascontiguousarray(theta2.astype(np.float32)),
           "cnw": np.ascontiguousarray(cnw)} for c in range(8)]
    rC1 = _run8(nc, im)

    ctrl_b = inp["L2_attn_ctrl_b"]
    nc = _get("l2c2", build_l2_c2)
    im = []
    for c in range(8):
        im.append({
            "x2": x2[c],
            "cond": np.ascontiguousarray(rC1[c]["cond"][:, cols[c]]),
            "dWp": np.ascontiguousarray(inp["L2_attn_delta_W"][perm]),
            "ctrlW": inp["L2_attn_ctrl_W"],
            "cbq": _pc(ctrl_b[:3072], 24),
            "cbx": np.ascontiguousarray(ctrl_b[3072:3104].reshape(32, 1)),
            "cbg": _pc(ctrl_b[3104:], 8),
            "ramp": np.ascontiguousarray(np.concatenate(
                [_pc(inp["L2_attn_qkv_ramp_logit"][i]) for i in range(3)], axis=1)),
            "bramp": np.ascontiguousarray(
                inp["L2_attn_beta_ramp_logit"].reshape(16, 1))})
    rC2 = _run8(nc, im)

    nc = _get("l2c3", build_l2_c3)
    convw = np.concatenate(
        [inp["L2_attn_" + nm + "_conv_w"].reshape(8, 128, 4).transpose(1, 0, 2).reshape(128, 32)
         for nm in ("q", "k", "v")], axis=1)
    fulls = {}
    for b in range(2):
        fulls[b] = {}
        for nm in ("cq", "ck", "cv"):
            fb = np.zeros((1024, 8192), np.float32)
            for g in range(4):
                cc = b * 4 + g
                fb[:, cols[cc]] = rC2[cc][nm]
            fulls[b][nm] = fb
    im = []
    for c in range(8):
        b, j = c // 4, c % 4

        def with_halo(fb, j=j):
            out = np.zeros((1024, 2054), np.float32)
            for bi, blk in enumerate((j, 7 - j)):
                s0 = blk * 1024
                halo = fb[:, s0 - 3:s0] if s0 >= 3 else np.zeros((1024, 3), np.float32)
                out[:, bi * 1027:bi * 1027 + 3] = halo
                out[:, bi * 1027 + 3:(bi + 1) * 1027] = fb[:, s0:s0 + 1024]
            return out
        im.append({"cqh": with_halo(fulls[b]["cq"]), "ckh": with_halo(fulls[b]["ck"]),
                   "cvh": with_halo(fulls[b]["cv"]),
                   "convw": np.ascontiguousarray(convw),
                   "dec": np.ascontiguousarray(rC2[c]["bg"][0:16]),
                   "betpre": np.ascontiguousarray(rC2[c]["bg"][16:32])})
    rC3 = _run8(nc, im)

    nc = _get("l2c4a", build_l2_c4a)
    im = []
    for c in range(8):
        b = c // 4
        m1a = np.zeros((64, 64, 1024), np.float32)
        m2a = np.zeros((64, 64, 1024), np.float32)
        for g in range(64):
            blk, l = g // 8, g % 8
            jj, loc = owner_of_block(blk)
            oc_ = b * 4 + jj
            cl = loc * 8 + l
            m1a[g] = rC3[oc_]["m1s"][cl]
            m2a[g] = rC3[oc_]["m2s"][cl]
        im.append({"m1s_all": m1a, "m2s_all": m2a})
    rC4a = _run8(nc, im)

    nc = _get("l2c4b", build_l2_c4b)
    im = []
    for c in range(8):
        b, j = c // 4, c % 4
        sins = np.zeros((16, 64, 1024), np.float32)
        for cl in range(16):
            blk = j if cl < 8 else 7 - j
            g = blk * 8 + (cl % 8)
            sins[cl] = rC4a[b * 4]["s_ins"][g]
        im.append({"g1t": rC3[c]["g1t"], "g2t": rC3[c]["g2t"], "s_ins_own": sins,
                   "gate_l": rC2[c]["gate_l"],
                   "onw": np.ascontiguousarray(
                       np.tile(inp["L2_attn_out_norm_w"], 2).reshape(128, 1)),
                   "opT": np.ascontiguousarray(inp["L2_attn_out_proj_W"].T),
                   "opb": _pc(inp["L2_attn_out_proj_b"])})
    rC4b = _run8(nc, im)

    nc = _get("l2c5", build_l2_c5)
    b1pc = np.ascontiguousarray(
        inp["L2_ffn_b1"].reshape(8, 32, 128).transpose(2, 0, 1).reshape(128, 256))
    b2pc = np.ascontiguousarray(
        inp["L2_ffn_b2"].reshape(8, 8, 128).transpose(2, 0, 1).reshape(128, 64))
    x3h = {}
    for half in range(2):
        im = []
        for c in range(8):
            b = c // 4
            hcols = cols[c][half * 1024:(half + 1) * 1024]
            im.append({"x2h": np.ascontiguousarray(x2[c][:, half * 1024:(half + 1) * 1024]),
                       "s2h": np.ascontiguousarray(
                           rC4b[c]["s2"][:, half * 1024:(half + 1) * 1024]),
                       "WrT": inp["L2_ffn_Wr"],
                       "br": inp["L2_ffn_br"].reshape(1, 8),
                       "W1": inp["L2_ffn_W1"], "b1pc": b1pc,
                       "W2": inp["L2_ffn_W2"], "b2pc": b2pc})
        x3h[half] = _run8(nc, im)
    x3 = [np.concatenate([x3h[0][c]["x3h"], x3h[1][c]["x3h"]], axis=1)
          for c in range(8)]
    _kdbg("x3", x3)
    _kdbg("s2", [rC4b[c]["s2"] for c in range(8)])

    nc = _get("l3d1a", build_l3_d1a)
    im = []
    for c in range(8):
        im.append({"x3": np.ascontiguousarray(x3[c]),
                   "rin_w": _pc(inp["L3_attn_rin_w"]), "rk_w": _pc(inp["L3_attn_rk_w"]),
                   "rv_w": _pc(inp["L3_attn_rv_w"]),
                   "wqT": np.ascontiguousarray(inp["L3_attn_Wq"].T),
                   "wkT": np.ascontiguousarray(inp["L3_attn_Wk"].T),
                   "wvT": np.ascontiguousarray(inp["L3_attn_Wv"].T),
                   "betaT": np.ascontiguousarray(inp["L3_attn_beta_w"].T),
                   "beta_b": inp["L3_attn_beta_b"].reshape(1, 16),
                   "base_beta": inp["L3_attn_base_beta_logit"].reshape(1, 16)})
    rD1a = _run8(nc, im)
    _kdbg("d1a", [rD1a[c][k] for c in range(8) for k in ("q_o", "k_o", "v_o", "betap")])
    nc = _get("l3d1b", build_l3_d1b)
    im = [{"q_i": rD1a[c]["q_o"], "k_i": rD1a[c]["k_o"], "v_i": rD1a[c]["v_o"],
           "betap": rD1a[c]["betap"]} for c in range(8)]
    rD1 = _run8(nc, im)
    _kdbg("d1b", [rD1[c][k] for c in range(8) for k in ("olocT", "qg", "m1s", "denc", "cumb")])

    nc = _get("l3d2a", build_l3_d2a)
    im = []
    for c in range(8):
        b = c // 4
        m1a = np.zeros((64, 64, 1024), np.float32)
        gla = np.zeros((1, 1024), np.float32)
        den_ends = np.zeros((16, 8), np.float32)
        cumb_ends = np.zeros((16, 8), np.float32)
        for g in range(64):
            blk, l = g // 8, g % 8
            jj, loc = owner_of_block(blk)
            oc_ = b * 4 + jj
            cl = loc * 8 + l
            m1a[g] = rD1[oc_]["m1s"][cl]
            gla[0, g * 16:(g + 1) * 16] = rD1[oc_]["gl_out"][0, cl * 16:(cl + 1) * 16]
        for blk in range(8):
            jj, loc = owner_of_block(blk)
            oc_ = b * 4 + jj
            den_ends[:, blk] = rD1[oc_]["dsum"][:, 0 + loc]
            cumb_ends[:, blk] = rD1[oc_]["dsum"][:, 2 + loc]
        im.append({"m1s_all": m1a, "gl_all": gla,
                   "den_ends": den_ends, "cumb_ends": cumb_ends})
    rD2a = _run8(nc, im)

    nc = _get("l3d2b", build_l3_d2b)
    im = []
    for c in range(8):
        b, j = c // 4, c % 4
        sins = np.zeros((16, 64, 1024), np.float32)
        for cl in range(16):
            blk = j if cl < 8 else 7 - j
            g = blk * 8 + (cl % 8)
            sins[cl] = rD2a[b * 4]["s_ins"][g]
        den_in_own = np.stack([rD2a[b * 4]["den_ins"][:, j],
                               rD2a[b * 4]["den_ins"][:, 7 - j]], axis=1)
        im.append({"olocT": rD1[c]["olocT"],
                   "qg": rD1[c]["qg"], "s_ins": sins,
                   "denc": rD1[c]["denc"], "cumb": rD1[c]["cumb"],
                   "den_in_own": np.ascontiguousarray(den_in_own)})
    rD2b = _run8(nc, im)
    _kdbg("d2b", [rD2b[c]["o_o"] for c in range(8)])
    nc = _get("l3d2c", build_l3_d2c)
    im = []
    for c in range(8):
        im.append({"x3": np.ascontiguousarray(x3[c]), "o_i": rD2b[c]["o_o"],
                   "WoT": np.ascontiguousarray(inp["L3_attn_Wo"].T),
                   "bo128": _pc(inp["L3_attn_bo"]), "lnw": _pc(inp["L3_attn_ln_w"]),
                   "lnb": _pc(inp["L3_attn_ln_b"]),
                   "basewT": np.ascontiguousarray(inp["L3_ffn_base_w"].T),
                   "baseb": _pc(inp["L3_ffn_base_b"]),
                   "A_all": np.ascontiguousarray(
                       inp["L3_ffn_lora_A"].transpose(1, 0, 2).reshape(1024, 128)),
                   "B_all": np.ascontiguousarray(
                       inp["L3_ffn_lora_B"].reshape(128, 1024)),
                   "routerT": np.ascontiguousarray(inp["L3_ffn_router_w"].T),
                   "router_b": inp["L3_ffn_router_b"].reshape(1, 8)})
    rD2c = _run8(nc, im)
    _kdbg("d2c", [rD2c[c]["o_x4"] for c in range(8)])

    out = np.zeros((2, 8192, 1024), np.float32)
    for c in range(8):
        b = c // 4
        out[b][cols[c], :] = rD2c[c]["o_x4"].T
    return out



def load_w_hilo(p, dram_buf, K, C, tag="whl"):
    n_rc = K // 128
    hi = p.sb(tag + "_hi", [128, n_rc * C], BF16)
    lo = p.sb(tag + "_lo", [128, n_rc * C], BF16)
    f32 = p.sb(tag + "_f32", [128, C], F32)
    for rc in range(n_rc):
        p.dma("sync", f32.ap[:], dram_buf.ap[rc * 128:(rc + 1) * 128, :],
              reads=[dram_buf], writes=[f32], tag=tag + "ld")
        sl = (slice(None), slice(rc * C, (rc + 1) * C))
        copy_task(p, "vector", hi, sl, f32, (slice(None), slice(0, C)), tag=tag + "h")

        def sub(v, rc=rc):
            return v.tensor_sub(f32.ap[:, 0:C], f32.ap[:, 0:C],
                                hi.ap[:, rc * C:(rc + 1) * C])
        p.task("vector", sub, reads=[f32, hi], writes=[f32], tag=tag + "s")
        copy_task(p, "vector", lo, sl, f32, (slice(None), slice(0, C)), tag=tag + "l")
    return hi, lo


def emit_hilo_split(p, src_f32, ssl, hi, lo, hsl, tag="xhl"):
    copy_task(p, "vector", hi, hsl, src_f32, ssl, tag=tag + "h")

    def sub(v):
        return v.tensor_sub(lo.ap[hsl].bitcast(BF16) if False else lo.ap[hsl],
                            src_f32.ap[ssl], hi.ap[hsl])
    p.task("vector", sub, reads=[src_f32, hi], writes=[lo], tag=tag + "l")


def emit_mm_hilo(p, pool, whi, wlo, xhi, xlo, wcol, xcol, oc, Tc, n_k, wstride,
                 xstride, psb):
    def f(te):
        last = None
        first = True
        for k in range(n_k):
            wsl = slice(k * wstride + wcol + oc * 128, k * wstride + wcol + oc * 128 + 128)
            xsl = slice(k * xstride + xcol, k * xstride + xcol + Tc)
            for wa, xa in ((whi, xhi), (whi, xlo), (wlo, xhi)):
                last = te.matmul(psb.ap[:, 0:Tc], wa.ap[:, wsl], xa.ap[:, xsl],
                                 start=first, stop=False)
                first = False
        last = te.matmul(psb.ap[:, 0:Tc], whi.ap[:, wcol + oc * 128:wcol + oc * 128 + 1],
                         xhi.ap[:, xcol:xcol + Tc], start=False, stop=True) if False else last
        return last
    def f2(te):
        ops = []
        for k in range(n_k):
            wsl = slice(k * wstride + wcol + oc * 128, k * wstride + wcol + oc * 128 + 128)
            xsl = slice(k * xstride + xcol, k * xstride + xcol + Tc)
            for wa, xa in ((whi, xhi), (whi, xlo), (wlo, xhi)):
                ops.append((wa, wsl, xa, xsl))
        last = None
        for i, (wa, wsl, xa, xsl) in enumerate(ops):
            last = te.matmul(psb.ap[:, 0:Tc], wa.ap[:, wsl], xa.ap[:, xsl],
                             start=(i == 0), stop=(i == len(ops) - 1))
        return last
    return p.task("tensor", f2, reads=[whi, wlo, xhi, xlo], writes=[psb], tag="mmhl")


def build_l3_d1a():
    nc = bass.Bass("TRN2", target_bir_lowering=False, debug=False)
    I = lambda n, s: Buf(n, nc.dram_tensor(n, s, F32, kind="ExternalInput").ap(), "DRAM")
    x3 = I("x3", [1024, 2048])
    rin_w = I("rin_w", [128, 8]); rk_w = I("rk_w", [128, 8]); rv_w = I("rv_w", [128, 8])
    wqT = I("wqT", [1024, 1024]); wkT = I("wkT", [1024, 1024]); wvT = I("wvT", [1024, 1024])
    betaT = I("betaT", [1024, 16]); beta_b = I("beta_b", [1, 16])
    base_beta = I("base_beta", [1, 16])
    O = lambda n, s: Buf(n, nc.dram_tensor(n, s, F32, kind="ExternalOutput").ap(), "DRAM")
    q_o = O("q_o", [1024, 2048]); k_o = O("k_o", [1024, 2048]); v_o = O("v_o", [1024, 2048])
    betap = O("betap", [16, 2048])

    p = Prog(nc)
    pool = PsumPool(p, n=8)
    ones_col = make_ones_col(p)
    ones_lhs = make_ones_lhs(p)

    wqh, wql = load_w_hilo(p, wqT, 1024, 1024, tag="wq")
    wkh, wkl = load_w_hilo(p, wkT, 1024, 1024, tag="wk")
    wvh, wvl = load_w_hilo(p, wvT, 1024, 1024, tag="wv")
    wb_sb, _ = load_w_nat(p, betaT, 1024, 16, tag="wb")
    rin_sb = p.sb("rin_sb", [128, 8], F32)
    p.dma("sync", rin_sb.ap[:], rin_w.ap[:, :], reads=[rin_w], writes=[rin_sb])
    rk_sb = p.sb("rk_sb", [128, 8], F32)
    p.dma("sync", rk_sb.ap[:], rk_w.ap[:, :], reads=[rk_w], writes=[rk_sb])
    rv_sb = p.sb("rv_sb", [128, 8], F32)
    p.dma("sync", rv_sb.ap[:], rv_w.ap[:, :], reads=[rv_w], writes=[rv_sb])

    Tc = 512
    scrl = ln_scratch(p, Tc)
    x_ch = p.sb("x_ch", [128, 8 * Tc], F32)
    xnb = p.sb("xnb", [128, 8 * Tc], F32)
    xsq = scrl["x2"]
    xhi = p.sb("xhi", [128, 8 * Tc], BF16)
    xlo = p.sb("xlo", [128, 8 * Tc], BF16)
    rst = scrl["stat"]
    rrep = p.sb("rrep", [128, Tc], F32)
    tmp = p.sb("tmp", [128, Tc], F32)
    ev = p.sb("ev", [128, Tc], F32)
    beta = p.sb("beta", [16, 2048], F32)

    for tq in range(4):
        t0 = tq * Tc
        p.dma("sync", x_ch.ap[:],
              x3.ap.rearrange("(c p) t -> p c t", p=128)[:, :, t0:t0 + Tc],
              reads=[x3], writes=[x_ch], tag="xld")
        emit_ln_full(p, pool, ones_lhs, ones_col, x_ch, Tc, None, xnb, scrl)
        for fc in range(8):
            sl = (slice(None), slice(fc * Tc, (fc + 1) * Tc))
            act_task(p, xsq, sl, xnb, sl, AF.Square, tag="xsq")
        emit_rms_stats(p, pool, ones_lhs, ones_col, xsq, Tc, rst, 0,
                       None, 1e-6, use_mean=True, tag="rms1")
        rbf = p.sb("rbf", [1, Tc], BF16) if tq == 0 else rbf
        copy_task(p, "vector", rbf, (slice(0, 1), slice(0, Tc)),
                  rst, (slice(0, 1), slice(0, Tc)), tag="rbf")
        emit_bcast_row(p, pool, ones_col, rbf.ap[0:1, 0:Tc], rbf, rrep, 0, Tc)
        for fc in range(8):
            sl = (slice(None), slice(fc * Tc, (fc + 1) * Tc))
            tt_task(p, "vector", ALU.mult, x_ch, sl, xnb, sl, rrep,
                    (slice(None), slice(0, Tc)), tag="xia")
            ts_task(p, "vector", ALU.mult, x_ch, sl, x_ch, sl,
                    (rin_sb, (slice(None), slice(fc, fc + 1))), tag="xib")
            emit_hilo_split(p, x_ch, sl, xhi, xlo, sl, tag="xhl")
            act_task(p, xsq, sl, x_ch, sl, AF.Square, tag="xisq")
        for oc in range(8):
            psb = pool.nxt()
            emit_mm_hilo(p, pool, wqh, wql, xhi, xlo, 0, 0, oc, Tc, 8, 1024,
                         Tc, psb)
            copy_task(p, "vector", ev, (slice(None), slice(0, Tc)),
                      psb, (slice(None), slice(0, Tc)), tag="qev")
            p.dma("gpsimd_dma", q_o.ap[oc * 128:(oc + 1) * 128, t0:t0 + Tc],
                  ev.ap[:, 0:Tc], reads=[ev], writes=[q_o], tag="qst")
        psb = pool.nxt()

        def bmm(te, psb=psb):
            last = None
            for k in range(8):
                last = te.matmul(psb.ap[0:16, 0:Tc],
                                 wb_sb.ap[:, k * 16:(k + 1) * 16],
                                 xhi.ap[:, k * Tc:(k + 1) * Tc],
                                 start=(k == 0), stop=(k == 7))
            return last
        p.task("tensor", bmm, reads=[wb_sb, xhi], writes=[psb], tag="bmm")
        copy_task(p, "vector", beta, (slice(0, 16), slice(t0, t0 + Tc)),
                  psb, (slice(0, 16), slice(0, Tc)), tag="bev")
        emit_rms_stats(p, pool, ones_lhs, ones_col, xsq, Tc, rst, Tc,
                       None, 1e-6, use_mean=True, tag="rms2")
        copy_task(p, "vector", rbf, (slice(0, 1), slice(0, Tc)),
                  rst, (slice(0, 1), slice(Tc, 2 * Tc)), tag="rbf2")
        emit_bcast_row(p, pool, ones_col, rbf.ap[0:1, 0:Tc], rbf, rrep, 0, Tc)
        for which, rw, whib, wlob, dstd in (("k", rk_sb, wkh, wkl, k_o),
                                            ("v", rv_sb, wvh, wvl, v_o)):
            for fc in range(8):
                sl = (slice(None), slice(fc * Tc, (fc + 1) * Tc))
                tt_task(p, "vector", ALU.mult, tmp, (slice(None), slice(0, Tc)),
                        x_ch, sl, rrep, (slice(None), slice(0, Tc)), tag="kv1")
                ts_task(p, "vector", ALU.mult, tmp, (slice(None), slice(0, Tc)),
                        tmp, (slice(None), slice(0, Tc)),
                        (rw, (slice(None), slice(fc, fc + 1))), tag="kv2")
                emit_hilo_split(p, tmp, (slice(None), slice(0, Tc)), xhi, xlo, sl,
                                tag="khl")
            for oc in range(8):
                psb = pool.nxt()
                emit_mm_hilo(p, pool, whib, wlob, xhi, xlo, 0, 0, oc, Tc, 8, 1024,
                             Tc, psb)
                copy_task(p, "vector", ev, (slice(None), slice(0, Tc)),
                          psb, (slice(None), slice(0, Tc)), tag="kev")
                p.dma("gpsimd_dma", dstd.ap[oc * 128:(oc + 1) * 128, t0:t0 + Tc],
                      ev.ap[:, 0:Tc], reads=[ev], writes=[dstd], tag="kst")

    bbcol = p.sb("bbcol", [16, 1], F32)
    p.dma("sync", bbcol.ap[:], beta_b.ap.rearrange("o (e x) -> e (o x)", x=1),
          reads=[beta_b], writes=[bbcol], tag="bbcol", allow_slow_non_contiguous=True)
    bscol = p.sb("bscol", [16, 1], F32)
    p.dma("sync", bscol.ap[:], base_beta.ap.rearrange("o (e x) -> e (o x)", x=1),
          reads=[base_beta], writes=[bscol], tag="bscol", allow_slow_non_contiguous=True)
    act_task(p, bscol, (slice(0, 16), slice(0, 1)), bscol, (slice(0, 16), slice(0, 1)),
             AF.Exp, scale=-1.0, tag="bsc1")
    ts_task(p, "vector", ALU.add, bscol, (slice(0, 16), slice(0, 1)),
            bscol, (slice(0, 16), slice(0, 1)), 1.0, tag="bsc2")

    def bscr(v):
        return v.reciprocal(bscol.ap[0:16, 0:1], bscol.ap[0:16, 0:1])
    p.task("vector", bscr, reads=[bscol], writes=[bscol], tag="bsc3")
    bsl = (slice(0, 16), slice(None))
    ts_task(p, "vector", ALU.add, beta, bsl, beta, bsl,
            (bbcol, (slice(0, 16), slice(0, 1))), tag="badd")
    act_task(p, beta, bsl, beta, bsl, AF.Exp, scale=-1.0, tag="bsig1")
    ts_task(p, "vector", ALU.add, beta, bsl, beta, bsl, 1.0, tag="bsig2")

    def brec(v):
        return v.reciprocal(beta.ap[bsl], beta.ap[bsl])
    p.task("vector", brec, reads=[beta], writes=[beta], tag="bsig3")
    ts_task(p, "vector", ALU.mult, beta, bsl, beta, bsl,
            (bscol, (slice(0, 16), slice(0, 1))), tag="bbase")
    ts_task(p, "vector", ALU.max, beta, bsl, beta, bsl, 0.9, tag="bclip1")
    ts_task(p, "vector", ALU.min, beta, bsl, beta, bsl, 0.9995, tag="bclip2")
    p.dma("gpsimd_dma", betap.ap[:, :], beta.ap[bsl], reads=[beta], writes=[betap])

    p.finalize()
    return nc


def build_l3_d1b():
    nc = bass.Bass("TRN2", target_bir_lowering=False, debug=False)
    I = lambda n, s: Buf(n, nc.dram_tensor(n, s, F32, kind="ExternalInput").ap(), "DRAM")
    q_d = I("q_i", [1024, 2048]); k_d = I("k_i", [1024, 2048]); v_d = I("v_i", [1024, 2048])
    betap = I("betap", [16, 2048])
    O = lambda n, s, dt=F32: Buf(n, nc.dram_tensor(n, s, dt, kind="ExternalOutput").ap(), "DRAM")
    olocT = O("olocT", [1024, 2048]); qg = O("qg", [1024, 2048])
    m1s = O("m1s", [16, 64, 1024])
    gl_out = O("gl_out", [1, 256])
    gam_o = O("gam_o", [16, 2048])
    denc_o = O("denc", [16, 2048]); cumb_o = O("cumb", [16, 2048])
    dsum = O("dsum", [16, 8])
    gam_d = Buf("gam_d", nc.dram_tensor("gam_d", [16, 2048], F32, kind="Internal").ap(), "DRAM")
    glr_d = Buf("glr_d", nc.dram_tensor("glr_d", [1, 256], F32, kind="Internal").ap(), "DRAM")

    p = Prog(nc)
    pool = PsumPool(p, n=8)
    ones_col = make_ones_col(p)
    ones_lhs = make_ones_lhs(p)
    identf = make_identity_f32(p)
    beta = p.sb("beta", [16, 2048], F32)
    T1 = p.sb("T1", [16, 2048], F32)
    T2 = p.sb("T2", [16, 2048], F32)
    T3 = p.sb("T3", [16, 2048], F32)
    bsl = (slice(0, 16), slice(None))
    p.dma("sync", beta.ap[bsl], betap.ap[:, :], reads=[betap], writes=[beta])

    act_task(p, T1, bsl, beta, bsl, AF.Ln, tag="lgb")
    p.task("vector", lambda v: v.memset(T2.ap[bsl], 1.0), writes=[T2])
    for c0m in range(0, 2048, 128):
        p.task("vector", lambda v, c0m=c0m: v.memset(T2.ap[0:16, c0m:c0m + 1], 0.0),
               writes=[T2])

    def gsc(v):
        return v.tensor_tensor_scan(T3.ap[bsl], T2.ap[bsl], T1.ap[bsl], 0.0,
                                    ALU.mult, ALU.add)
    p.task("vector", gsc, reads=[T2, T1], writes=[T3], tag="gsc")
    p.dma("scalar_dma", gam_d.ap[:, :], T3.ap[bsl], reads=[T3], writes=[gam_d])
    p.dma("gpsimd_dma", gam_o.ap[:, :], T3.ap[bsl], reads=[T3], writes=[gam_o])
    src_gl = gam_d.ap.rearrange("h (c l) -> c h l", l=128)[:, :, 127:128]
    p.dma("sync", glr_d.ap.rearrange("o (c h) -> c h o", h=16), src_gl,
          reads=[gam_d], writes=[glr_d], tag="glgather",
          allow_slow_non_contiguous=True)
    p.dma("sync", gl_out.ap.rearrange("o (c h) -> c h o", h=16), src_gl,
          reads=[gam_d], writes=[gl_out], tag="glout",
          allow_slow_non_contiguous=True)
    p.task("vector", lambda v: v.memset(T2.ap[bsl], 1.0), writes=[T2])
    for c0m in (0, 1024):
        p.task("vector", lambda v, c0m=c0m: v.memset(T2.ap[0:16, c0m:c0m + 1], 0.0),
               writes=[T2])
    tt_task(p, "vector", ALU.mult, T1, bsl, beta, bsl, T2, bsl, tag="bmask")
    p.task("vector", lambda v: v.memset(T2.ap[bsl], 1.0), writes=[T2])

    def dsc(v):
        return v.tensor_tensor_scan(T3.ap[bsl], T1.ap[bsl], T2.ap[bsl], 0.0,
                                    ALU.mult, ALU.add)
    p.task("vector", dsc, reads=[T1, T2], writes=[T3], tag="dsc")
    p.dma("gpsimd_dma", denc_o.ap[:, :], T3.ap[bsl], reads=[T3], writes=[denc_o])
    dsb = p.sb("dsb", [16, 8], F32)
    copy_task(p, "vector", dsb, (slice(0, 16), slice(0, 1)),
              T3, (slice(0, 16), slice(1023, 1024)), tag="dsb0")
    copy_task(p, "vector", dsb, (slice(0, 16), slice(1, 2)),
              T3, (slice(0, 16), slice(2047, 2048)), tag="dsb1")
    def bs2(v):
        return v.tensor_sub(T2.ap[bsl], beta.ap[bsl], T1.ap[bsl])
    p.task("vector", bs2, reads=[beta, T1], writes=[T2], tag="bstart")

    def csc(v):
        return v.tensor_tensor_scan(T3.ap[bsl], T1.ap[bsl], T2.ap[bsl], 0.0,
                                    ALU.mult, ALU.add)
    p.task("vector", csc, reads=[T1, T2], writes=[T3], tag="csc")
    p.dma("gpsimd_dma", cumb_o.ap[:, :], T3.ap[bsl], reads=[T3], writes=[cumb_o])
    copy_task(p, "vector", dsb, (slice(0, 16), slice(2, 3)),
              T3, (slice(0, 16), slice(1023, 1024)), tag="dsb2")
    copy_task(p, "vector", dsb, (slice(0, 16), slice(3, 4)),
              T3, (slice(0, 16), slice(2047, 2048)), tag="dsb3")
    p.task("vector", lambda v: v.memset(dsb.ap[0:16, 4:8], 0.0), writes=[dsb])
    p.dma("gpsimd_dma", dsum.ap[:, :], dsb.ap[0:16, 0:8], reads=[dsb], writes=[dsum])

    qblk = p.sb("qblk", [64, 128], F32)
    kblk = p.sb("kblk", [64, 128], F32)
    vblk = p.sb("vblk", [64, 128], F32)
    ktm = p.sb("ktm", [128, 64], F32)
    vtm = p.sb("vtm", [128, 64], F32)
    gcol = p.sb("gcol", [128, 1], F32)
    glrep_c = p.sb("glrep_c", [128, 16], F32)
    gtrep = p.sb("gtrep", [128, 128], F32)
    lam = p.sb("lam", [128, 128], F32)
    pp_bf = p.sb("pp_bf", [128, 128], F32)
    wcoef = p.sb("wcoef", [128, 1], F32)
    ktw = p.sb("ktw", [128, 64], F32)
    m1strip = p.sb("m1strip", [64, 16 * 64], F32)
    egt = p.sb("egt", [64, 128], F32)
    qgb = p.sb("qgb", [64, 128], F32)
    olb = p.sb("olb", [64, 128], F32)
    triu = p.sb("triu", [128, 128], F32)
    trones = p.sb("trones", [128, 128], F32)
    p.task("vector", lambda v: v.memset(trones.ap[:], 1.0), writes=[trones])

    def trsel(g):
        return g.affine_select(triu.ap[:], trones.ap[:], pattern=[[1, 128]],
                               compare_op=ALU.is_ge, fill=0.0, base=0,
                               channel_multiplier=-1)
    p.task("gpsimd", trsel, reads=[trones], writes=[triu], tag="triu")

    for c in range(16):
        c0 = c * 128
        p.dma("sync", glrep_c.ap[:],
              glr_d.ap[0:1, c * 16:(c + 1) * 16].partition_broadcast(128),
              reads=[glr_d], writes=[glrep_c], tag="glrld")
        for h in range(16):
            fc, off = h // 2, (h % 2) * 64
            r0 = fc * 128 + off
            p.dma("sync", gcol.ap[:],
                  gam_d.ap[h:h + 1, c0:c0 + 128].rearrange("o (t x) -> t (o x)", x=1),
                  reads=[gam_d], writes=[gcol], tag="gcol",
                  allow_slow_non_contiguous=True)
            p.dma("scalar_dma", gtrep.ap[:],
                  gam_d.ap[h:h + 1, c0:c0 + 128].partition_broadcast(128),
                  reads=[gam_d], writes=[gtrep], tag="gtrep")
            p.dma("sync", qblk.ap[0:64, :], q_d.ap[r0:r0 + 64, c0:c0 + 128],
                  reads=[q_d], writes=[qblk], tag="qbld")
            p.dma("sync", kblk.ap[0:64, :], k_d.ap[r0:r0 + 64, c0:c0 + 128],
                  reads=[k_d], writes=[kblk], tag="kbld")
            p.dma("sync", vblk.ap[0:64, :], v_d.ap[r0:r0 + 64, c0:c0 + 128],
                  reads=[v_d], writes=[vblk], tag="vbld")
            for src_, dst in ((kblk, ktm), (vblk, vtm)):
                pst = pool.nxt()

                def trf(te, src_=src_, pst=pst):
                    return te.transpose(pst.ap[0:128, 0:64],
                                        src_.ap[0:64, 0:128],
                                        identf.ap[0:64, 0:64])
                p.task("tensor", trf, reads=[src_, identf], writes=[pst], tag="ktr")

                def tev(v, dst=dst, pst=pst):
                    return v.tensor_copy(dst.ap[:, 0:64], pst.ap[0:128, 0:64])
                p.task("vector", tev, reads=[pst], writes=[dst], tag="ktev")
            psb = pool.nxt()

            def kqm(te, psb=psb):
                return te.matmul(psb.ap[:, 0:128], kblk.ap[0:64, :], qblk.ap[0:64, :],
                                 start=True, stop=True)
            p.task("tensor", kqm, reads=[kblk, qblk], writes=[psb], tag="kqm")
            ts_task(p, "vector", ALU.subtract, lam, (slice(None), slice(0, 128)),
                    gtrep, (slice(None), slice(0, 128)),
                    (gcol, (slice(None), slice(0, 1))), tag="lam1")
            act_task(p, lam, (slice(None), slice(0, 128)), lam,
                     (slice(None), slice(0, 128)), AF.Exp, tag="lam2")
            tt_task(p, "vector", ALU.mult, lam, (slice(None), slice(0, 128)),
                    lam, (slice(None), slice(0, 128)), triu, (slice(None), slice(0, 128)),
                    tag="lam3")

            def ppm(v, psb=psb):
                return v.tensor_mul(pp_bf.ap[:, 0:128], lam.ap[:, 0:128],
                                    psb.ap[:, 0:128])
            p.task("vector", ppm, reads=[lam, psb], writes=[pp_bf], tag="ppm")
            pso = pool.nxt()

            def olm(te, pso=pso):
                return te.matmul(pso.ap[0:64, 0:128], vtm.ap[:, 0:64],
                                 pp_bf.ap[:, 0:128], start=True, stop=True)
            p.task("tensor", olm, reads=[vtm, pp_bf], writes=[pso], tag="olm")
            copy_task(p, "vector", olb, (slice(0, 64), slice(0, 128)),
                      pso, (slice(0, 64), slice(0, 128)), tag="olev")
            p.dma("gpsimd_dma", olocT.ap[r0:r0 + 64, c0:c0 + 128],
                  olb.ap[0:64, 0:128], reads=[olb], writes=[olocT], tag="olst")
            ts_task(p, "vector", ALU.subtract, wcoef, (slice(None), slice(0, 1)),
                    gcol, (slice(None), slice(0, 1)),
                    (glrep_c, (slice(None), slice(h, h + 1))), tag="wc1")
            act_task(p, wcoef, (slice(None), slice(0, 1)), wcoef,
                     (slice(None), slice(0, 1)), AF.Exp, scale=-1.0, tag="wc2")
            ts_task(p, "vector", ALU.mult, ktw, (slice(None), slice(0, 64)),
                    ktm, (slice(None), slice(0, 64)), (wcoef, (slice(None), slice(0, 1))),
                    tag="ktw")
            psm = pool.nxt()

            def m1m(te, psm=psm):
                return te.matmul(psm.ap[0:64, 0:64], ktw.ap[:, 0:64], vtm.ap[:, 0:64],
                                 start=True, stop=True)
            p.task("tensor", m1m, reads=[ktw, vtm], writes=[psm], tag="m1m")
            copy_task(p, "vector", m1strip, (slice(0, 64), slice(h * 64, h * 64 + 64)),
                      psm, (slice(0, 64), slice(0, 64)), tag="m1ev")
            act_task(p, egt, (slice(0, 64), slice(0, 128)), gtrep,
                     (slice(0, 64), slice(0, 128)), AF.Exp, tag="egt")
            tt_task(p, "vector", ALU.mult, qgb, (slice(0, 64), slice(0, 128)),
                    qblk, (slice(0, 64), slice(0, 128)), egt, (slice(0, 64), slice(0, 128)),
                    tag="qgb")
            p.dma("scalar_dma", qg.ap[r0:r0 + 64, c0:c0 + 128],
                  qgb.ap[0:64, 0:128], reads=[qgb], writes=[qg], tag="qgst")
        p.dma("gpsimd_dma", m1s.ap[c, :, :], m1strip.ap[0:64, :],
              reads=[m1strip], writes=[m1s], tag="m1st")

    p.finalize()
    return nc


